# version 32
# baseline (speedup 1.0000x reference)
import numpy as np
import contextlib
import ml_dtypes
import concourse.bass as bass
import concourse.mybir as mybir
from concourse.bass_utils import run_bass_kernel_spmd

F32 = mybir.dt.float32
BF16 = mybir.dt.bfloat16
AF = mybir.ActivationFunctionType
ALU = mybir.AluOpType
AX = mybir.AxisListType

D = 1024
ED = 2048
NST = 16
KC = 15
RDT = 64
HF = 4096
EPS = 1e-5

ENGS = ("sync", "vector", "scalar", "tensor", "gpsimd")
ATTN_STAGE = 4


class Sem:
    __slots__ = ("h", "n")

    def __init__(self, h):
        self.h = h
        self.n = 0


class Tok:
    __slots__ = ("q", "box")

    def __init__(self, q, box):
        self.q = q
        self.box = box

    def inc(self, sem, k=1):
        sem.n += k
        box = self.box
        self.q.rec.append(lambda e: box["i"].then_inc(sem.h, k))
        return self


class Q:
    def __init__(self, name):
        self.name = name
        self.rec = []

    def op(self, fname, *a, **kw):
        box = {}

        def fn(e, a=a, kw=kw, fname=fname, box=box):
            box["i"] = getattr(e, fname)(*a, **kw)

        self.rec.append(fn)
        return Tok(self, box)

    def __getattr__(self, fname):
        if fname.startswith("_"):
            raise AttributeError(fname)
        return lambda *a, **kw: self.op(fname, *a, **kw)

    def wait(self, sem, n=None):
        v = sem.n if n is None else n
        if v > 0:
            self.rec.append(lambda e, v=v: e.wait_ge(sem.h, v))

    def dr(self):
        self.rec.append(lambda e: e.drain())


class B:
    def __init__(self, nc, es):
        self.nc = nc
        self.es = es
        self.q = {n: Q(n) for n in ENGS}
        self.sy = self.q["sync"]
        self.ve = self.q["vector"]
        self.ac = self.q["scalar"]
        self.te = self.q["tensor"]
        self.gp = self.q["gpsimd"]
        self._semi = 0

    def sem(self, name=None):
        self._semi += 1
        return Sem(self.es.enter_context(self.nc.semaphore(name or "s%d" % self._semi)))

    def sb(self, name, shape, dt):
        return self.es.enter_context(self.nc.sbuf_tensor(name, list(shape), dt))

    def finalize(self):
        with self.nc.Block() as block:
            for name in ENGS:
                rec = self.q[name].rec

                def body(e, rec=rec):
                    for fn in rec:
                        fn(e)

                getattr(block, name)(body)


def build_program(T=2048, nsub=4):
    NT = T // 128
    TH = T // 2
    NTH = NT // 2
    nc = bass.Bass("TRN2", target_bir_lowering=False)

    def dram(name, shape, dt=F32, kind="ExternalInput"):
        return nc.dram_tensor(name, list(shape), dt, kind=kind).ap()

    hx = dram("x", [T, D])
    h_gamma = dram("gamma", [D]); h_beta = dram("beta", [D])
    h_Win = dram("Win", [2 * ED, D]); h_bin = dram("bin", [2 * ED])
    h_convw = dram("convw", [ED, KC]); h_convb = dram("convb", [ED])
    h_Wx = dram("Wxm", [RDT + 2 * NST, ED])
    h_Wdt = dram("Wdt", [ED, RDT]); h_bdt = dram("bdt", [ED])
    h_Alog = dram("Alog", [ED, NST]); h_Dp = dram("Dp", [ED])
    h_Wout = dram("Wout", [D, ED]); h_bout = dram("bout", [D])
    h_Wq = dram("Wq", [D, D]); h_bq = dram("bq", [D])
    h_Wk = dram("Wk", [D, D]); h_bk = dram("bk", [D])
    h_Wv = dram("Wv", [D, D]); h_bv = dram("bv", [D])
    h_W1 = dram("W1", [HF, D]); h_b1 = dram("b1", [HF])
    h_W2 = dram("W2", [D, HF]); h_b2 = dram("b2", [D])
    h_scale = dram("fscale", [1, 1])
    h_eye = dram("eyebf", [128, 128], BF16)
    h_out = dram("out", [T, D], kind="ExternalOutput")
    s_x = dram("s_x", [T, D], kind="Internal")
    s_WinT = dram("s_WinT", [32, 128, 1024], BF16, kind="Internal")
    s_W1T = dram("s_W1T", [32, 128, 1024], BF16, kind="Internal")
    s_W2T = dram("s_W2T", [32, 128, 1024], BF16, kind="Internal")
    s_WoutT = dram("s_WoutT", [16, 128, 1024], BF16, kind="Internal")
    s_rows = dram("s_rows", [32, T], BF16, kind="Internal")
    s_yfc = dram("s_yfc", [16, 128, T // 4], BF16, kind="Internal")
    s_brow = dram("s_brow", [1, 2 * ED], kind="Internal")

    es = contextlib.ExitStack()
    b = B(nc, es)
    sy, ve, ac, te, gp = b.sy, b.ve, b.ac, b.te, b.gp

    lx = b.sem("lx"); lg = b.sem("lg"); st = b.sem("st")
    pv = b.sem("pv")
    s_dmm = b.sem("sdmm"); s_sp = b.sem("ssp"); s_dl = b.sem("sdl")
    s_dx = b.sem("sdx"); s_dA = b.sem("sdA"); s_dB = b.sem("sdB")
    s_sc = b.sem("ssc"); s_g = b.sem("sg"); s_ym = b.sem("sym")
    s_zm = b.sem("szm"); s_sz = b.sem("ssz"); s_yf = b.sem("syf")
    s_wz0 = b.sem("swz0"); s_wz1 = b.sem("swz1"); s_ev = b.sem("sev")
    s_wzp = (s_wz0, s_wz1)
    s_wzz0 = b.sem("swzz0"); s_wzz1 = b.sem("swzz1"); s_wzz = (s_wzz0, s_wzz1)
    lxr0 = b.sem("lxr0"); lxr1 = b.sem("lxr1"); lxrp = (lxr0, lxr1)
    s_q1 = b.sem("sq1"); s_q2 = b.sem("sq2"); s_q3 = b.sem("sq3")
    s_q4 = b.sem("sq4"); s_q5 = b.sem("sq5")
    l_tp = b.sem("ltp_"); l_ap = b.sem("lap_"); l_ev = b.sem("lev_")
    l_wi = b.sem("lwi"); l_cv = b.sem("lcv"); l_dg = b.sem("ldg")
    l_xin = b.sem("lxin"); l_tl = b.sem("ltl"); l_cm = b.sem("lcm")
    l_om = b.sem("lom"); l_ox = b.sem("lox"); l_wo = b.sem("lwo")
    l_w2 = b.sem("lw2"); l_lt = b.sem("llt"); l_tr = b.sem("ltr")
    l_u = b.sem("lu"); l_h = b.sem("lh"); l_v = b.sem("lv"); l_o = b.sem("lo")
    l_qk = b.sem("lqk"); l_qe = b.sem("lqe"); l_vm = b.sem("lvm")
    l_vv = b.sem("lvv"); l_sc = b.sem("lsc"); l_sm = b.sem("lsm")
    l_ex = b.sem("lex"); l_pn = b.sem("lpn"); l_pt = b.sem("lpt")
    l_tt = b.sem("ltt"); l_av = b.sem("lav")

    eye = b.sb("eye", [128, 128], BF16)
    A_fm = b.sb("A_fm", [128, 16, NST], F32)
    convw_fm = b.sb("convw_fm", [128, 16, KC], F32)
    cb_fm = b.sb("cb_fm", [128, 16], F32)
    Dp_fm = b.sb("Dp_fm", [128, 16], F32)
    bin_fm = b.sb("bin_fm", [128, 32], F32)
    bo_bc = b.sb("bo_bc", [128, D], F32)
    b2_bc = b.sb("b2_bc", [128, D], F32)
    bv_bc = b.sb("bv_bc", [128, D], F32)
    b1_fm = b.sb("b1_fm", [128, 32], F32)
    bq_fm = b.sb("bq_fm", [128, 8], F32)
    bk_fm = b.sb("bk_fm", [128, 8], F32)
    gam_fm = b.sb("gam_fm", [128, 8], F32)
    scale_sb = b.sb("scale_sb", [1, 1], F32)
    bdtrow_fm = b.sb("bdt_fm", [128, 16], F32)
    carry = b.sb("carry", [128, 256], F32)
    stat6 = b.sb("stat6", [128, 12], F32)
    mv2 = b.sb("mv2", [128, 2], F32)
    mu1 = b.sb("mu1", [128, 16], F32)
    ss1 = b.sb("ss1", [128, 16], F32)
    r1a = b.sb("r1a", [128, 16], F32)
    tmp1 = b.sb("tmp1", [128, 1], F32)
    xres = b.sb("xres", [128, 2, D], F32)
    xntok = b.sb("xntok", [128, 2, D], BF16)
    xr_last = {0: None, 1: None}

    def xr_load(sl, src_slice):
        if xr_last[sl] is not None:
            sem, v = xr_last[sl]
            sy.wait(sem, v)
        sy.wait(st, st.n)
        sy.dma_start(out=xres[:, sl, :], in_=src_slice).inc(lxrp[sl], 16)

    def xr_used(sl, sem):
        xr_last[sl] = (sem, sem.n)
    WxT = b.sb("WxT", [128, 16 * 96], BF16)
    WdtT = b.sb("WdtT", [64, ED], BF16)

    sy.dma_start(out=eye[:, :], in_=h_eye).inc(lx, 16)
    for dst, src in [(cb_fm, h_convb), (bdtrow_fm, h_bdt), (Dp_fm, h_Dp),
                     (b1_fm, h_b1), (bq_fm, h_bq), (bk_fm, h_bk),
                     (gam_fm, h_gamma)]:
        sy.dma_start(out=dst[:, :], in_=src.rearrange("(j p) -> p j", p=128), allow_slow_non_contiguous=True).inc(lx, 16)
    sy.dma_start(out=A_fm[:, :, :], in_=h_Alog.rearrange("(c p) n -> p c n", p=128)).inc(lx, 16)
    sy.dma_start(out=convw_fm[:, :, :], in_=h_convw.rearrange("(c p) k -> p c k", p=128)).inc(lx, 16)
    sy.dma_start(out=bo_bc[:, :], in_=h_bout.rearrange("(o d) -> o d", o=1).broadcast_to((128, D))).inc(lx, 16)
    sy.dma_start(out=b2_bc[:, :], in_=h_b2.rearrange("(o d) -> o d", o=1).broadcast_to((128, D))).inc(lx, 16)
    sy.dma_start(out=bv_bc[:, :], in_=h_bv.rearrange("(o d) -> o d", o=1).broadcast_to((128, D))).inc(lx, 16)
    sy.dma_start(out=scale_sb[:, :], in_=h_scale).inc(lx, 16)

    ac.wait(lx)
    ac.activation(out=A_fm[:, :, :], in_=A_fm[:, :, :], func=AF.Exp)
    ac.dr()
    ac.op("mul", A_fm[:, :, :], A_fm[:, :, :], -1.0)
    ve.wait(lx)
    ve.memset(carry[:, :], 0.0)
    ve.dr()

    xr_load(0, hx[0:128, :])
    xr_load(1, hx[128:256, :])
    PREP1 = {}
    with contextlib.ExitStack() as prep:
        pb = lambda n, s, d: prep.enter_context(nc.sbuf_tensor(n, list(s), d))
        pps = lambda n, s, d: prep.enter_context(nc.psum_tensor(n, list(s), d))
        wnat = pb("wnat", [128, 2, 1024], BF16)
        wnatB = pb("wnatB", [128, 8, 4096], BF16)
        wstage = pb("wstage", [128, 2, 1024], BF16)
        beta_bf = pb("beta_bf", [128, 8], BF16)
        binrow = pb("binrow", [1, 2 * ED], F32)
        p_tp = pps("p_tp", [128, 2, 1024], BF16)
        p_br = pps("p_br", [1, 2048], F32)
        ltp0 = b.sem("ltp0"); ltp1 = b.sem("ltp1"); ltpp = (ltp0, ltp1)
        lnb = b.sem("lnb"); lnc = b.sem("lnc")
        lbr = b.sem("lbr")
        ttp = b.sem("ttp"); etp = b.sem("etp")
        wtp0 = b.sem("wtp0"); wtp1 = b.sem("wtp1"); wtpp = (wtp0, wtp1)
        btp = b.sem("btp")

        sy.dma_start(out=binrow[:, :], in_=h_bin.rearrange("(o f) -> o f", o=1)).inc(lbr, 16)
        gp.wait(lx)
        gp.dma_start(out=beta_bf[:, :], in_=h_beta.rearrange("(j p) -> p j", p=128), allow_slow_non_contiguous=True).inc(lnb, 16)

        P1TT = []; P1ET = []; P1WB = []; P1BT = []

        def prep_core(i, src_slice, cache_row, beta=False):
            sl = i % 2
            if i >= 2:
                gp.wait(ttp, P1TT[i - 2])
            gp.dma_start(out=wnat[:, sl, :], in_=src_slice).inc(ltpp[sl], 16)
            te.wait(ltpp[sl])
            if i >= 2:
                te.wait(etp, P1ET[i - 2])
            tok = None
            for k in range(8):
                tok = te.transpose(out=p_tp[:, sl, k * 128:(k + 1) * 128],
                                   in_=wnat[:, sl, k * 128:(k + 1) * 128],
                                   identity=eye[:, :])
            tok.inc(ttp, 8)
            P1TT.append(ttp.n)
            ac.wait(ttp, P1TT[i])
            if i >= 2:
                ac.wait(wtpp[sl], P1WB[i - 2][1])
                if beta:
                    ac.wait(btp, P1BT[i - 2])
            ac.activation(out=wstage[:, sl, :], in_=p_tp[:, sl, :], func=AF.Copy).inc(etp, 1)
            P1ET.append(etp.n)
            if beta:
                te.wait(etp, P1ET[i])
                if i == 0:
                    te.wait(lnb, 16)
                tok = None
                for k in range(8):
                    tok = te.matmul(out=p_br[0:1, (i % 16) * 128:(i % 16) * 128 + 128],
                                    lhsT=beta_bf[:, k:k + 1],
                                    rhs=wstage[:, sl, k * 128:(k + 1) * 128],
                                    start=((i % 4 == 0) and k == 0),
                                    stop=((i % 4 == 3) and k == 7))
                tok.inc(btp, 1)
                if i % 16 == 15:
                    ve.wait(btp, btp.n)
                    ve.wait(lbr)
                    half = i // 16
                    ve.tensor_add(out=binrow[0:1, half * 2048:(half + 1) * 2048],
                                  in0=binrow[0:1, half * 2048:(half + 1) * 2048],
                                  in1=p_br[0:1, :]).inc(pv, 1)
                    te.wait(pv, pv.n)
            P1BT.append(btp.n)
            sy.wait(etp, P1ET[i])
            sy.dma_start(out=cache_row, in_=wstage[:, sl, :]).inc(wtpp[sl], 16)
            P1WB.append((wtpp[sl], wtpp[sl].n))

        for i in range(32):
            prep_core(i, h_Win[i * 128:(i + 1) * 128, :], s_WinT[i, :, :], beta=True)

        def prepB(src_h, nj, ncols, cache):
            for k in range(8):
                gp.dma_start(out=wnatB[:, k, 0:ncols],
                             in_=src_h[k * 128:(k + 1) * 128, :]).inc(lnb, 16)
            te.wait(lnb, lnb.n)
            for j in range(nj):
                i = len(P1TT)
                sl = i % 2
                te.wait(etp, P1ET[i - 2])
                tok = None
                for k in range(8):
                    tok = te.transpose(out=p_tp[:, sl, k * 128:(k + 1) * 128],
                                       in_=wnatB[:, k, j * 128:(j + 1) * 128],
                                       identity=eye[:, :])
                tok.inc(ttp, 8)
                P1TT.append(ttp.n)
                ac.wait(ttp, P1TT[i])
                ac.wait(wtpp[sl], P1WB[i - 2][1])
                if i - 2 < len(P1BT):
                    ac.wait(btp, P1BT[min(i - 2, len(P1BT) - 1)])
                ac.activation(out=wstage[:, sl, :], in_=p_tp[:, sl, :], func=AF.Copy).inc(etp, 1)
                P1ET.append(etp.n)
                sy.wait(etp, P1ET[i])
                sy.dma_start(out=cache[j, :, :], in_=wstage[:, sl, :]).inc(wtpp[sl], 16)
                P1WB.append((wtpp[sl], wtpp[sl].n))

        prepB(h_Wout, 16, 2048, s_WoutT)

        sy.wait(pv)
        sy.dma_start(out=s_brow, in_=binrow[0:1, :]).inc(st, 16)
        sy.wait(st)
        sy.dma_start(out=bin_fm[:, :], in_=s_brow.rearrange("o (j p) -> p (o j)", p=128), allow_slow_non_contiguous=True).inc(lx, 16)

        wxnat = pb("wxnat", [96, ED], BF16)
        wdtnat = pb("wdtnat", [128, 16, RDT], BF16)
        gp.dma_start(out=wxnat[:, :], in_=h_Wx).inc(lnc, 16)
        gp.dma_start(out=wdtnat[:, :, :],
                     in_=h_Wdt.rearrange("(c p) r -> p c r", p=128)).inc(lnc, 16)
        te.wait(lnc)
        for grp in range(2):
            te.wait(etp)
            tok = None
            for cc in range(8):
                c = grp * 8 + cc
                tok = te.transpose(out=p_tp[:, 0, cc * 128:cc * 128 + 96],
                                   in_=wxnat[0:96, c * 128:(c + 1) * 128],
                                   identity=eye[0:96, 0:96])
            tok.inc(ttp, 8)
            ac.wait(ttp)
            tok = None
            for cc in range(8):
                c = grp * 8 + cc
                tok = ac.activation(out=WxT[:, c * 96:(c + 1) * 96],
                                    in_=p_tp[:, 0, cc * 128:cc * 128 + 96],
                                    func=AF.Copy)
            tok.inc(etp, 1)
        for grp in range(2):
            te.wait(etp)
            tok = None
            for cc in range(8):
                c = grp * 8 + cc
                tok = te.transpose(out=p_tp[0:64, 0, cc * 128:(cc + 1) * 128],
                                   in_=wdtnat[:, c, :],
                                   identity=eye[:, :])
            tok.inc(ttp, 8)
            ac.wait(ttp)
            tok = ac.activation(out=WdtT[:, grp * 1024:(grp + 1) * 1024],
                                in_=p_tp[0:64, 0, :], func=AF.Copy)
            tok.inc(etp, 1)
        PREP1.update({"etp": (etp, etp.n), "ttp": (ttp, ttp.n),
                      "btp": (btp, btp.n), "pv": (pv, pv.n),
                      "win": P1WB[0:32], "wout": P1WB[32:48]})


    def emit_prep2():
        with contextlib.ExitStack() as prep:
            pb = lambda n, s, d: prep.enter_context(nc.sbuf_tensor(n, list(s), d))
            pps = lambda n, s, d: prep.enter_context(nc.psum_tensor(n, list(s), d))
            wnat = pb("p2_wnat", [128, 2, 1024], BF16)
            wnatB = pb("p2_wnatB", [128, 8, 4096], BF16)
            wstage = pb("p2_wstage", [128, 2, 1024], BF16)
            p_tp = pps("p2_ptp", [128, 2, 1024], BF16)
            ltp0 = b.sem("p2ltp0"); ltp1 = b.sem("p2ltp1"); ltpp = (ltp0, ltp1)
            lnb = b.sem("p2lnb")
            ttp = b.sem("p2ttp"); etp = b.sem("p2etp")
            wtp0 = b.sem("p2wtp0"); wtp1 = b.sem("p2wtp1"); wtpp = (wtp0, wtp1)
            tt_toks = []; et_toks = []; wb_toks = []

            def prep_row(ci, in_tile, cache_row):
                sl = ci % 2
                if ci >= 2:
                    te.wait(etp, et_toks[ci - 2])
                tok = None
                for k in range(8):
                    tok = te.transpose(out=p_tp[:, sl, k * 128:(k + 1) * 128],
                                       in_=in_tile[k],
                                       identity=eye[:, :])
                tok.inc(ttp, 8)
                tt_toks.append(ttp.n)
                ac.wait(ttp, tt_toks[ci])
                if ci >= 2:
                    ac.wait(wtpp[sl], wb_toks[ci - 2][1])
                ac.activation(out=wstage[:, sl, :], in_=p_tp[:, sl, :],
                              func=AF.Copy).inc(etp, 1)
                et_toks.append(etp.n)
                sy.wait(etp, et_toks[ci])
                sy.dma_start(out=cache_row, in_=wstage[:, sl, :]).inc(wtpp[sl], 16)
                wb_toks.append((wtpp[sl], wtpp[sl].n))

            ci = 0
            for i in range(32):
                sl = i % 2
                if i >= 2:
                    gp.wait(ttp, tt_toks[i - 2])
                gp.dma_start(out=wnat[:, sl, :],
                             in_=h_W1[i * 128:(i + 1) * 128, :]).inc(ltpp[sl], 16)
                te.wait(ltpp[sl])
                prep_row(ci, [wnat[:, sl, k * 128:(k + 1) * 128] for k in range(8)],
                         s_W1T[i, :, :])
                ci += 1
            for k in range(8):
                gp.dma_start(out=wnatB[:, k, :],
                             in_=h_W2[k * 128:(k + 1) * 128, :]).inc(lnb, 16)
            te.wait(lnb, lnb.n)
            for j in range(32):
                prep_row(ci, [wnatB[:, k, j * 128:(j + 1) * 128] for k in range(8)],
                         s_W2T[j, :, :])
                ci += 1
        return (wb_toks[0:32], wb_toks[32:64], (ttp, ttp.n))

    def emit_ln(xsrc, tiles, xn_fm, kind, lnps_bf, preloaded=0):
        for j, i in enumerate(tiles):
            sl = j % 2
            if j >= preloaded:
                xr_load(sl, xsrc[i * 128:(i + 1) * 128, :])
            ve.wait(lxrp[sl])
            ve.bn_stats(out=stat6[:, 0:6], in_=xres[:, sl, 0:512])
            ve.bn_stats(out=stat6[:, 6:12], in_=xres[:, sl, 512:1024])
            ve.dr()
            ve.bn_aggr(out=mv2[:, :], in_=stat6[:, :])
            ve.dr()
            ve.tensor_copy(out=mu1[:, i:i + 1], in_=mv2[:, 0:1])
            if kind == "mamba":
                ve.tensor_scalar(out=tmp1[:, 0:1], in0=mv2[:, 1:2], scalar1=1.0 + EPS,
                                 scalar2=EPS * EPS, op0=ALU.mult, op1=ALU.add)
                ve.dr()
                ve.reciprocal(out=tmp1[:, 0:1], in_=tmp1[:, 0:1])
                ve.dr()
                ve.tensor_scalar(out=r1a[:, i:i + 1], in0=mv2[:, 1:2], scalar1=EPS,
                                 scalar2=None, op0=ALU.add)
                ve.dr()
                ve.reciprocal(out=r1a[:, i:i + 1], in_=r1a[:, i:i + 1]).inc(s_q1, 1)
                ac.wait(s_q1)
                ac.op("sqrt", ss1[:, i:i + 1], tmp1[:, 0:1])
                ac.dr()
                ac.op("sqrt", r1a[:, i:i + 1], r1a[:, i:i + 1]).inc(s_q2, 1)
            else:
                ve.tensor_scalar(out=tmp1[:, 0:1], in0=mv2[:, 1:2], scalar1=EPS,
                                 scalar2=None, op0=ALU.add)
                ve.dr()
                ve.reciprocal(out=tmp1[:, 0:1], in_=tmp1[:, 0:1]).inc(s_q1, 1)
                ac.wait(s_q1)
                ac.op("sqrt", ss1[:, i:i + 1], tmp1[:, 0:1]).inc(s_q2, 1)
            ve.wait(s_q2)
            if j >= 2:
                ve.wait(l_tp, l_tp.n - 8)
            ve.tensor_scalar(out=xntok[:, sl, :], in0=xres[:, sl, :],
                             scalar1=mu1[:, i:i + 1],
                             scalar2=ss1[:, i:i + 1],
                             op0=ALU.subtract, op1=ALU.mult).inc(l_ap, 1)
            xr_used(sl, l_ap)
            te.wait(l_ap)
            te.wait(l_ev, l_ev.n - (0 if j >= 1 else 0))
            if j == 0:
                te.wait(l_cv)
                te.wait(l_ox, l_ox.n)
                te.wait(s_q1, s_q1.n)
                te.wait(PREP1["etp"][0], PREP1["etp"][1])
                te.wait(PREP1["pv"][0], PREP1["pv"][1])
                te.wait(lx, lx.n); ve.wait(lx, lx.n); ac.wait(lx, lx.n)
            tok = None
            for k in range(8):
                tok = te.transpose(out=lnps_bf[:, k * 128:(k + 1) * 128],
                                   in_=xntok[:, sl, k * 128:(k + 1) * 128],
                                   identity=eye[:, :])
            tok.inc(l_tp, 8)
            ac.wait(l_tp)
            if kind == "mamba":
                tok = None
                for k in range(8):
                    tok = ac.activation(out=xn_fm[:, k, j * 128:(j + 1) * 128],
                                        in_=lnps_bf[:, k * 128:(k + 1) * 128],
                                        func=AF.Copy, scale=gam_fm[:, k:k + 1])
                tok.inc(l_ev, 1)
            else:
                ac.activation(out=xn_fm[:, :, j * 128:(j + 1) * 128],
                              in_=lnps_bf.rearrange("p (k c) -> p k c", k=8),
                              func=AF.Copy).inc(l_ev, 1)
        return l_ev

    def barrier(sem):
        for qq in (ve, te, ac, gp, sy):
            qq.wait(sem)

    def emit_mamba():
        TQ = T // 4
        NTQ = NT // 4
        tch = [(s0, min(512, TQ - s0)) for s0 in range(0, TQ, 512)]
        with contextlib.ExitStack() as ms:
            mb = lambda n, s, d: ms.enter_context(nc.sbuf_tensor(n, list(s), d))
            xn_fm = mb("m_xn", [128, 2, 8, TQ], BF16)
            xc = mb("m_xc", [128, 2, 16, TQ], BF16)
            tail = mb("m_tail", [128, 16, 14], BF16)
            dtl = mb("m_dtl", [64, 2, TQ], BF16)
            bcB = mb("m_bcB", [128, 16, TQ], BF16)
            bcC = mb("m_bcC", [128, 16, TQ], BF16)
            delta = mb("m_delta", [128, 2, TQ], F32)
            dxb = mb("m_dxb", [128, TQ], BF16)
            dAb = mb("m_dA", [128, 2, TQ], F32)
            dBxall = mb("m_dBall", [128, 16, TQ], BF16)
            hall = mb("m_hall", [128, 16, TQ], BF16)
            gall = mb("m_gall", [128, 16, TQ], BF16)
            szb = mb("m_sz", [128, 3, TQ], BF16)
            txd = mb("m_txd", [128, D], F32)
            yfull = mb("m_yf", [128, 2, TQ], BF16)
            yfr = mb("m_yfr", [128, 2, TQ], BF16)
            xinp = mb("m_xinp", [128, 2, 14 + TQ], BF16)
            diag = mb("m_diag", [128, KC, 128], BF16)
            bcrows = mb("m_bcr", [32, TQ], BF16)
            wz = mb("m_wz", [128, 2, 1024], BF16)
            wzz = mb("m_wzz", [128, 2, 1024], BF16)
            woutt = mb("m_wouta", [128, 2, 1024], BF16)
            pA = ms.enter_context(nc.psum_tensor("m_pA", [128, 4, 512], F32))
            pB = ms.enter_context(nc.psum_tensor("m_pB", [128, 4, 512], F32))
            lnps_bf = pB[:, 2:4, :].rearrange("p a b -> p (a b)").bitcast(BF16)[:, 0:1024]
            l_fx = b.sem("lfx"); l_yfd0 = b.sem("lyfd0"); l_yfd1 = b.sem("lyfd1")
            yfdp = (l_yfd0, l_yfd1)
            wz_last = {0: None, 1: None}
            wzz_last = {0: None, 1: None}
            EPI_YM = {}; EPI_SZ = {}

            gwz0 = b.sem("gwz0"); gwz1 = b.sem("gwz1"); gwzp = (gwz0, gwz1)

            def wz_load(sl, src_row, pre=None):
                if wz_last[sl] is not None:
                    sem, v = wz_last[sl]
                    gp.wait(sem, v)
                if pre is not None:
                    for t in pre:
                        gp.wait(t[0], t[1])
                gp.dma_start(out=wz[:, sl, :], in_=src_row).inc(gwzp[sl], 16)

            ve.wait(PREP1["ttp"][0], PREP1["ttp"][1])
            ve.memset(tail[:, :, :], 0.0)
            ve.dr()

            FT = {}

            def front_A(h, c):
                hh = h % 2
                sl = c % 2
                pre = None
                if h == 0:
                    pre = [PREP1["win"][c]]
                    if c < 2:
                        pre.append(PREP1["ttp"])
                wz_load(sl, s_WinT[c, :, :], pre=pre)
                te.wait(gwzp[sl])
                if c >= 2:
                    te.wait(l_xin, FT[("xin", h, c - 2)])
                elif ("xin", h - 1, c + 14) in FT:
                    te.wait(l_xin, FT[("xin", h - 1, c + 14)])
                if c == 0:
                    te.wait(l_ev, l_ev.n)
                    te.wait(l_ox, l_ox.n)
                tok = None
                for k in range(8):
                    tok = te.matmul(out=pA[:, 2 + sl, 0:TQ],
                                    lhsT=wz[:, sl, k * 128:(k + 1) * 128],
                                    rhs=xn_fm[:, hh, k, 0:TQ],
                                    start=(k == 0), stop=(k == 7))
                tok.inc(l_wi, 1)
                FT[("wi", h, c)] = l_wi.n
                wz_last[sl] = (l_wi, l_wi.n)

            def front_B(h, c):
                sl = c % 2
                ac.wait(l_wi, FT[("wi", h, c)])
                if c >= 2:
                    ac.wait(l_cm, FT[("cm", h, c - 2)])
                elif ("cm", h - 1, c + 14) in FT:
                    ac.wait(l_cm, FT[("cm", h - 1, c + 14)])
                ac.activation(out=xinp[:, sl, 14:14 + TQ], in_=pA[:, 2 + sl, 0:TQ],
                              func=AF.Identity, bias=bin_fm[:, c:c + 1]).inc(l_xin, 1)
                FT[("xin", h, c)] = l_xin.n
                ac.activation(out=xinp[:, sl, 0:14], in_=tail[:, c, :], func=AF.Copy)
                ac.dr()
                ac.activation(out=tail[:, c, :], in_=xinp[:, sl, TQ:TQ + 14],
                              func=AF.Copy).inc(l_tl, 1)
                FT[("tl", h, c)] = l_tl.n
                if c >= 1:
                    gp.wait(l_cm, FT[("cm", h, c - 1)])
                elif ("cm", h - 1, 15) in FT:
                    gp.wait(l_cm, FT[("cm", h - 1, 15)])
                if h == 0 and c == 0:
                    gp.wait(PREP1["ttp"][0], PREP1["ttp"][1])
                    gp.wait(lx, lx.n)
                gp.tensor_mul(out=diag[:, :, :],
                              in0=convw_fm[:, c, :].rearrange("p (k o) -> p k o", o=1).to_broadcast((128, KC, 128)),
                              in1=eye[:, :].rearrange("p (o j) -> p o j", o=1).to_broadcast((128, KC, 128))).inc(l_dg, 1)
                FT[("dg", h, c)] = l_dg.n
                te.wait(l_dg, FT[("dg", h, c)])
                te.wait(l_tl, FT[("tl", h, c)])
                if c >= 2:
                    te.wait(l_cv, FT[("cv", h, c - 2)])
                elif ("cv", h - 1, c + 14) in FT:
                    te.wait(l_cv, FT[("cv", h - 1, c + 14)])
                tok = None
                for k in range(KC):
                    tok = te.matmul(out=pB[:, 2 + sl, 0:TQ],
                                    lhsT=diag[:, k, :],
                                    rhs=xinp[:, sl, k:k + TQ],
                                    start=(k == 0), stop=(k == KC - 1))
                tok.inc(l_cm, 1)
                FT[("cm", h, c)] = l_cm.n

            def front_C(h, c):
                hh = h % 2
                sl = c % 2
                ac.wait(l_cm, FT[("cm", h, c)])
                if h >= 2:
                    ac.wait(s_dx, 16 * (h - 1))
                ac.activation(out=xc[:, hh, c, :], in_=pB[:, 2 + sl, 0:TQ], func=AF.Silu,
                              bias=cb_fm[:, c:c + 1]).inc(l_cv, 1)
                FT[("cv", h, c)] = l_cv.n

            def front_staged(h, c):
                if c < 16:
                    front_A(h, c)
                if 1 <= c <= 16:
                    front_B(h, c - 1)
                if c >= 2:
                    front_C(h, c - 2)

            def front_tail(h):
                hh = h % 2
                if h == 0:
                    te.wait(PREP1["etp"][0], PREP1["etp"][1])
                te.wait(l_cv, l_cv.n)
                te.wait(l_fx, l_fx.n)
                tok = None
                for k in range(16):
                    tok = te.matmul(out=pA[0:96, 2, 0:TQ],
                                    lhsT=WxT[:, k * 96:(k + 1) * 96],
                                    rhs=xc[:, hh, k, 0:TQ],
                                    start=(k == 0), stop=(k == 15))
                tok.inc(s_q4, 1)
                ac.wait(s_q4)
                ac.activation(out=dtl[:, hh, :], in_=pA[0:64, 2, 0:TQ], func=AF.Copy)
                ac.dr()
                ac.activation(out=bcrows[:, :], in_=pA[64:96, 2, 0:TQ], func=AF.Copy).inc(l_fx, 1)
                sy.wait(l_fx)
                sy.dma_start(out=s_rows[:, h * TQ:(h + 1) * TQ], in_=bcrows[:, :]).inc(s_ev, 16)
                sy.wait(s_ev)
                sy.wait(s_dB, s_dB.n)
                sy.wait(s_g, s_g.n)
                for n in range(16):
                    sy.dma_start(out=bcB[:, n, :],
                                 in_=s_rows[n:n + 1, h * TQ:(h + 1) * TQ].broadcast_to((128, TQ))).inc(s_ev, 16)
                    sy.dma_start(out=bcC[:, n, :],
                                 in_=s_rows[16 + n:17 + n, h * TQ:(h + 1) * TQ].broadcast_to((128, TQ))).inc(s_ev, 16)

            def front(h):
                for c in range(18):
                    front_staged(h, c)
                front_tail(h)

            DTOK = {}

            def emit_delta(h, c):
                hh = h % 2
                sl = c % 2
                te.wait(s_sz, s_sz.n)
                if c == 0:
                    te.wait(l_fx, l_fx.n)
                    if h == 0:
                        te.wait(PREP1["etp"][0], PREP1["etp"][1])
                    if h >= 1:
                        te.wait(l_ox, l_ox.n)
                tok = te.matmul(out=pA[:, sl, 0:TQ],
                                lhsT=WdtT[:, c * 128:(c + 1) * 128],
                                rhs=dtl[:, hh, 0:TQ],
                                start=True, stop=True)
                tok.inc(s_dmm, 1)
                DTOK[c] = s_dmm.n

            def scan(h):
                hh = h % 2
                emit_delta(h, 0)
                for c in range(16):
                    sl = c % 2
                    ac.wait(s_dmm, DTOK[c])
                    ac.wait(s_dl, s_dl.n)
                    ac.activation(out=txd[:, 512:512 + TQ], in_=pA[:, sl, 0:TQ], func=AF.Abs)
                    ac.dr()
                    ac.activation(out=txd[:, 512:512 + TQ], in_=txd[:, 512:512 + TQ], func=AF.Exp, scale=-1.0)
                    ac.dr()
                    ac.activation(out=txd[:, 512:512 + TQ], in_=txd[:, 512:512 + TQ], func=AF.Ln, bias=1.0)
                    ac.dr()
                    ac.wait(s_dx, s_dx.n)
                    ac.activation(out=delta[:, sl, 0:TQ], in_=pA[:, sl, 0:TQ],
                                  func=AF.Relu).inc(s_sp, 1)
                    ve.wait(s_sp)
                    if c == 0:
                        ve.wait(s_ev)
                    ve.tensor_add(out=delta[:, sl, :], in0=delta[:, sl, :],
                                  in1=txd[:, 512:512 + TQ]).inc(s_dl, 1)
                    if wzz_last[sl] is not None:
                        sy.wait(wzz_last[sl][0], wzz_last[sl][1])
                    if h == 0:
                        sy.wait(PREP1["win"][16 + c][0], PREP1["win"][16 + c][1])
                        if c == 0:
                            sy.wait(PREP1["btp"][0], PREP1["btp"][1])
                    sy.dma_start(out=wzz[:, sl, :], in_=s_WinT[16 + c, :, :]).inc(s_wzz[sl], 16)
                    te.wait(s_wzz[sl])
                    te.wait(s_sp)
                    tok = None
                    for k in range(8):
                        tok = te.matmul(out=pA[:, sl, 0:TQ],
                                        lhsT=wzz[:, sl, k * 128:(k + 1) * 128],
                                        rhs=xn_fm[:, hh, k, 0:TQ],
                                        start=(k == 0), stop=(k == 7))
                    tok.inc(s_zm, 1)
                    wzz_last[sl] = (s_zm, s_zm.n)
                    if c < 15:
                        emit_delta(h, c + 1)
                    ve.dr()
                    ve.tensor_mul(out=dxb[:, :], in0=delta[:, sl, :],
                                  in1=xc[:, hh, c, :]).inc(s_dx, 1)
                    ve.dr()
                    ve.tensor_mul(out=dBxall[:, :, :],
                                  in0=dxb[:, :].rearrange("p (o t) -> p o t", o=1).to_broadcast((128, 16, TQ)),
                                  in1=bcB[:, :, :]).inc(s_dB, 1)
                    ve.dr()
                    for n in range(16):
                        nn = n % 2
                        gidx = c * 16 + n
                        ac.wait(s_dl)
                        if gidx >= 2:
                            ac.wait(s_sc, s_sc.n - 1)
                        ac.activation(out=dAb[:, nn, :], in_=delta[:, sl, :], func=AF.Exp,
                                      scale=A_fm[:, c, n:n + 1]).inc(s_dA, 1)
                        ve.wait(s_dA)
                        ve.tensor_tensor_scan(out=hall[:, n, :], data0=dAb[:, nn, :],
                                              data1=dBxall[:, n, :],
                                              initial=carry[:, gidx % 256:gidx % 256 + 1],
                                              op0=ALU.mult, op1=ALU.add).inc(s_sc, 1)
                    ac.wait(s_zm, s_zm.n)
                    ac.wait(s_yf, s_yf.n)
                    ac.activation(out=szb[:, c % 3, 0:TQ], in_=pA[:, sl, 0:TQ], func=AF.Silu,
                                  bias=bin_fm[:, 16 + c:17 + c]).inc(s_sz, 1)
                    EPI_SZ[c] = s_sz.n
                    ve.dr()
                    ve.tensor_copy(out=carry[:, c * 16:c * 16 + 16],
                                   in_=hall[:, :, TQ - 1:TQ])
                    ve.wait(s_ym, s_ym.n)
                    ve.tensor_mul(out=gall[:, :, :], in0=hall[:, :, :],
                                  in1=bcC[:, :, :]).inc(s_g, 1)
                    te.wait(s_g)
                    te.wait(s_yf, s_yf.n - 1)
                    if c == 0 and h >= 1:
                        te.wait(l_ox, l_ox.n)
                    tok = None
                    for n in range(16):
                        tok = te.matmul(out=pB[:, sl, 0:TQ],
                                        lhsT=eye[:, :],
                                        rhs=gall[:, n, :],
                                        start=(n == 0), stop=(n == 15))
                    tok.inc(s_ym, 1)
                    EPI_YM[c] = s_ym.n
                    if c >= 1:
                        epilogue_ve(h, c - 1)
                    if c == 15:
                        epilogue_ve(h, 15)
                    if h < 3:
                        front_staged(h + 1, c)

            def epilogue_ve(h, c):
                hh = h % 2
                sl = c % 2
                ve.wait(s_ym, EPI_YM[c])
                ve.wait(s_sz, EPI_SZ[c])
                ve.dr()
                ve.scalar_tensor_tensor(out=txd[:, 0:TQ], in0=xc[:, hh, c, :],
                                        scalar=Dp_fm[:, c:c + 1], in1=pB[:, sl, 0:TQ],
                                        op0=ALU.mult, op1=ALU.add)
                ve.dr()
                ve.wait(yfdp[sl], yfdp[sl].n)
                ve.tensor_mul(out=yfull[:, sl, :], in0=txd[:, 0:TQ],
                              in1=szb[:, c % 3, 0:TQ]).inc(s_yf, 1)
                sy.wait(s_yf)
                sy.dma_start(out=s_yfc[c, :, :], in_=yfull[:, sl, :]).inc(yfdp[sl], 16)

            def wout(h):
                ptile = [pA[:, 0:2, :], pA[:, 2:4, :], pB[:, 0:2, :], pB[:, 2:4, :]]
                ptile = [p.rearrange("p a b -> p (a b)") for p in ptile]
                te.wait(s_yf)
                te.wait(s_sz); te.wait(s_sp)
                te.wait(l_xin); te.wait(l_cv); te.wait(l_ev); te.wait(l_fx)
                sy.wait(l_yfd0); sy.wait(l_yfd1)
                for k in range(16):
                    ksl = k % 2
                    if wz_last[ksl] is not None:
                        sem, v = wz_last[ksl]
                        sy.wait(sem, v)
                    if h == 0:
                        sy.wait(PREP1["wout"][k][0], PREP1["wout"][k][1])
                    sy.dma_start(out=woutt[:, ksl, :], in_=s_WoutT[k, :, :]).inc(s_wzp[ksl], 16)
                    sy.dma_start(out=yfr[:, ksl, :], in_=s_yfc[k, :, :]).inc(s_wzp[ksl], 16)
                    te.wait(s_wzp[ksl])
                    tok = None
                    for tt in range(NTQ):
                        for t0 in range(2):
                            tok = te.matmul(out=ptile[tt][:, t0 * 512:(t0 + 1) * 512],
                                            lhsT=yfr[:, ksl, tt * 128:(tt + 1) * 128],
                                            rhs=woutt[:, ksl, t0 * 512:(t0 + 1) * 512],
                                            start=(k == 0), stop=(k == 15))
                    tok.inc(l_om, 1)
                    wz_last[ksl] = (l_om, l_om.n)
                for tt in range(NTQ):
                    sl = tt % 2
                    i = h * NTQ + tt
                    xr_load(sl, hx[i * 128:(i + 1) * 128, :])
                    ve.wait(l_om); ve.wait(lxrp[sl])
                    ve.tensor_scalar(out=txd[:, :], in0=xres[:, sl, :],
                                     scalar1=mu1[:, i:i + 1],
                                     scalar2=r1a[:, i:i + 1],
                                     op0=ALU.subtract, op1=ALU.mult)
                    ve.dr()
                    ve.tensor_add(out=txd[:, :], in0=txd[:, :], in1=xres[:, sl, :])
                    ve.dr()
                    ve.tensor_add(out=txd[:, :], in0=txd[:, :], in1=bo_bc[:, :])
                    ve.dr()
                    ve.tensor_add(out=xres[:, sl, :], in0=txd[:, :],
                                  in1=ptile[tt][:, 0:1024]).inc(l_ox, 1)
                    xr_used(sl, l_ox)
                    sy.wait(l_ox)
                    sy.dma_start(out=s_x[i * 128:(i + 1) * 128, :],
                                 in_=xres[:, sl, :]).inc(st, 16)

            def ln_q(h):
                tiles = list(range(h * NTQ, (h + 1) * NTQ))
                return emit_ln(hx, tiles, xn_fm[:, h % 2, :, :], "mamba", lnps_bf,
                               preloaded=(2 if h == 0 else 0))

            l_ev0 = ln_q(0)
            barrier(l_ev0)
            front(0)
            for h in range(4):
                if h < 3:
                    ln_q(h + 1)
                scan(h)
                if h < 3:
                    front_staged(h + 1, 16)
                    front_staged(h + 1, 17)
                    front_tail(h + 1)
                wout(h)
            barrier(st)

    def emit_ffn(tag, is_last, prep_toks=None):
        THF = T // 4
        NTHF = NT // 4
        fch = [(s0, min(512, THF - s0)) for s0 in range(0, THF, 512)]
        with contextlib.ExitStack() as fs:
            fb = lambda n, s, d: fs.enter_context(nc.sbuf_tensor(n, list(s), d))
            xn_fm = fb("f_xn" + tag, [128, 2, 8, THF], BF16)
            hT = fb("f_h" + tag, [128, 32, THF], BF16)
            w2a = fb("f_w2" + tag, [128, 32, 1024], BF16)
            w1s = fb("f_w1" + tag, [128, 3, 1024], BF16)
            ltc = fb("f_ltc" + tag, [128, 8, 16], BF16)
            ev1 = fb("f_ev1" + tag, [1, 4096], BF16)
            ev4 = fb("f_ev4" + tag, [1, 4096], BF16)
            ufl = fb("f_ufl" + tag, [1, 4 * THF], BF16)
            vfl = fb("f_vfl" + tag, [1, 16 * THF], BF16)
            vsb = fb("f_vsb" + tag, [4, THF], BF16)
            usb = fb("f_usb" + tag, [4, THF], BF16)
            pA = fs.enter_context(nc.psum_tensor("f_pA" + tag, [128, 2, 1024], F32))
            pR = fs.enter_context(nc.psum_tensor("f_pR" + tag, [4, 1024], F32))
            pLN = fs.enter_context(nc.psum_tensor("f_pLN" + tag, [128, 1024], BF16))
            pRow = pLN[0:1, :]

            W1TOK = {}
            w1sem = (s_wzp[0], s_wzp[1], s_wzz[0])
            l_ev = emit_ln(s_x, list(range(NTHF)), xn_fm[:, 0, :, :], "plain", pLN[:, :])
            lntok = {0: l_ev.n}
            ptoks = prep_toks

            def load_w2a():
                if ptoks is not None:
                    gp.wait(ptoks[2][0], ptoks[2][1])
                    gp.wait(ptoks[1][30][0], ptoks[1][30][1])
                    gp.wait(ptoks[1][31][0], ptoks[1][31][1])
                for k in range(32):
                    gp.dma_start(out=w2a[:, k, :], in_=s_W2T[k, :, :]).inc(l_w2, 16)
            for h in range(4):
                xh = xn_fm[:, h % 2, :, :]
                te.wait(l_ev, lntok[h])
                if h == 0:
                    ve.wait(l_ev, lntok[0])
                    ve.tensor_copy(out=ltc[:, :, :], in_=xh[:, :, 1:17]).inc(l_lt, 1)
                    te.wait(l_lt)
                    for dstrow, toks in ((ev1, (0, 2, 4, 6)), (ev4, (8, 10, 12, 14))):
                        for ii, tk in enumerate(toks):
                            te.wait(s_q1)
                            tok = None
                            for k in range(8):
                                tok = te.transpose(out=pRow[0:1, k * 128:(k + 1) * 128],
                                                   in_=ltc[:, k, tk:tk + 1],
                                                   identity=eye[:, :])
                            tok.inc(l_tr, 1)
                            ac.wait(l_tr)
                            ac.activation(out=dstrow[0:1, ii * 1024:(ii + 1) * 1024],
                                          in_=pRow[0:1, :], func=AF.Copy,
                                          scale=scale_sb[0:1, 0:1]).inc(s_q1, 1)
                if h == 0:
                    te.wait(s_q1)
                te.wait(s_q2, s_q2.n)
                tok = None
                for k in range(8):
                    for (t0, tw) in fch:
                        tok = te.matmul(out=pR[:, t0:t0 + tw],
                                        lhsT=ltc[:, k, 1:8:2],
                                        rhs=xh[:, k, t0:t0 + tw],
                                        start=(k == 0), stop=(k == 7))
                tok.inc(l_u, 1)
                ac.wait(l_u)
                if h >= 1:
                    ac.wait(lg, lg.n)
                ac.activation(out=usb[:, :], in_=pR[0:4, 0:THF], func=AF.Copy).inc(s_q1, 1)
                gp.wait(s_q1)
                for ii in range(4):
                    gp.dma_start(out=ufl[0:1, ii * THF:(ii + 1) * THF],
                                 in_=usb[ii:ii + 1, :]).inc(lg, 16)
                if h == 0:
                    load_w2a()
                for c in range(32):
                    sl = c % 2
                    sl3 = c % 3
                    if c >= 3:
                        sy.wait(s_q3, W1TOK[c - 3])
                    if ptoks is not None and h == 0:
                        sy.wait(ptoks[0][c][0], ptoks[0][c][1])
                    sy.dma_start(out=w1s[:, sl3, :], in_=s_W1T[c, :, :]).inc(w1sem[sl3], 16)
                    te.wait(w1sem[sl3])
                    if c >= 2:
                        te.wait(l_h, l_h.n - 1)
                    tok = None
                    for k in range(8):
                        for (t0, tw) in fch:
                            tok = te.matmul(out=pA[:, sl, t0:t0 + tw],
                                            lhsT=w1s[:, sl3, k * 128:(k + 1) * 128],
                                            rhs=xh[:, k, t0:t0 + tw],
                                            start=(k == 0), stop=False)
                    if c == 0:
                        te.wait(lg, lg.n)
                    ii = c // 8
                    for ti, (t0, tw) in enumerate(fch):
                        tok = te.matmul(out=pA[:, sl, t0:t0 + tw],
                                        lhsT=ev1[0:1, ii * 1024 + (c % 8) * 128:ii * 1024 + (c % 8) * 128 + 128],
                                        rhs=ufl[0:1, ii * THF + t0:ii * THF + t0 + tw],
                                        start=False, stop=True)
                    tok.inc(s_q3, 1)
                    W1TOK[c] = s_q3.n
                    ac.wait(s_q3)
                    ac.activation(out=hT[:, c, :], in_=pA[:, sl, 0:THF], func=AF.Gelu,
                                  bias=b1_fm[:, c:c + 1]).inc(l_h, 1)
                    if c % 8 == 7:
                        qi = c // 8
                        if qi == 0:
                            te.wait(s_q1, s_q1.n)
                        else:
                            te.wait(lg, lg.n)
                        te.wait(l_h, l_h.n)
                        tok = None
                        for k in range(8):
                            for (t0, tw) in fch:
                                tok = te.matmul(out=pR[:, t0:t0 + tw],
                                                lhsT=ltc[:, k, 9:16:2],
                                                rhs=hT[:, qi * 8 + k, t0:t0 + tw],
                                                start=(k == 0), stop=(k == 7))
                        tok.inc(l_v, 1)
                        ac.wait(l_v)
                        if qi >= 1:
                            ac.wait(lg, lg.n)
                        ac.activation(out=vsb[:, 0:THF], in_=pR[:, 0:THF],
                                      func=AF.Copy).inc(s_q4, 1)
                        gp.wait(s_q4)
                        for ii2 in range(4):
                            gp.dma_start(out=vfl[0:1, (qi * 4 + ii2) * THF:(qi * 4 + ii2 + 1) * THF],
                                         in_=vsb[ii2:ii2 + 1, 0:THF]).inc(lg, 16)
                if h < 3:
                    l_ev = emit_ln(s_x, list(range((h + 1) * NTHF, (h + 2) * NTHF)),
                                   xn_fm[:, (h + 1) % 2, :, :], "plain", pLN[:, :])
                    lntok[h + 1] = l_ev.n
                te.wait(l_h, l_h.n)
                te.wait(lg, lg.n)
                for tt in range(NTHF):
                    sl = tt % 2
                    i = h * NTHF + tt
                    xr_load(sl, s_x[i * 128:(i + 1) * 128, :])
                    te.wait(l_w2)
                    if tt >= 2:
                        te.wait(l_ox, l_ox.n - 1)
                    tok = None
                    for k in range(32):
                        for t0 in range(2):
                            tok = te.matmul(out=pA[:, sl, t0 * 512:(t0 + 1) * 512],
                                            lhsT=hT[:, k, tt * 128:(tt + 1) * 128],
                                            rhs=w2a[:, k, t0 * 512:(t0 + 1) * 512],
                                            start=(k == 0), stop=False)
                    for ii in range(4):
                        for qi in range(4):
                            tok = te.matmul(out=pA[:, sl, ii * 256:(ii + 1) * 256],
                                            lhsT=vfl[0:1, (qi * 4 + ii) * THF + tt * 128:(qi * 4 + ii) * THF + tt * 128 + 128],
                                            rhs=ev4[0:1, ii * 1024 + qi:ii * 1024 + qi + 1021:4],
                                            start=False, stop=(qi == 3 and ii % 2 == 1))
                    tok.inc(l_o, 1)
                    ve.wait(l_o); ve.wait(lxrp[sl])
                    ve.tensor_add(out=xres[:, sl, :], in0=xres[:, sl, :], in1=b2_bc[:, :])
                    ve.dr()
                    ve.tensor_add(out=xres[:, sl, :], in0=xres[:, sl, :],
                                  in1=pA[:, sl, 0:1024]).inc(l_ox, 1)
                    xr_used(sl, l_ox)
                    sy.wait(l_ox)
                    dst = h_out if is_last else s_x
                    sy.dma_start(out=dst[i * 128:(i + 1) * 128, :],
                                 in_=xres[:, sl, :]).inc(st, 16)
                if h == 3:
                    barrier(st)

    def emit_attn():
        THA = T // 4
        NTHA = NT // 4
        fch = [(s0, min(512, THA - s0)) for s0 in range(0, THA, 512)]
        with contextlib.ExitStack() as ats:
            ab = lambda n, s, d: ats.enter_context(nc.sbuf_tensor(n, list(s), d))
            xn_fm = ab("a_xn", [128, 2, 8, THA], BF16)
            wqn = ab("a_wq", [128, 8, 1024], BF16)
            wkn = ab("a_wk", [128, 8, 1024], BF16)
            wvn = ab("a_wv", [128, 8, 1024], BF16)
            qf = ab("a_q", [128, 8, THA], BF16)
            kf = ab("a_k", [128, 8, THA], BF16)
            vt = ab("a_v", [128, NTHA, 1024], BF16)
            qh = ab("a_qh", [64, 8, THA], BF16)
            kh = ab("a_kh", [64, 8, THA], BF16)
            vh = ab("a_vh", [64, NTHA, 1024], BF16)
            pex = ab("a_pex", [128, 2, 1024], BF16)
            sub32 = ab("a_sub", [128, 1024], F32)
            ptb = ab("a_ptb", [64, 2, 2048], BF16)
            nm = ab("a_nm", [128, 2, 16], F32)
            lsum = ab("a_l", [128, 2, 16], F32)
            linv = ab("a_linv", [128, 2, 16], F32)
            pA = ats.enter_context(nc.psum_tensor("a_pA", [128, 2, 1024], F32))
            pT = ats.enter_context(nc.psum_tensor("a_pT", [64, 2048], BF16))
            pO = ats.enter_context(nc.psum_tensor("a_pO", [128, 1024], F32))
            lnps_bf = pO[:, :].bitcast(BF16)[:, 0:1024]

            gp.dma_start(out=wqn[:, :, :], in_=h_Wq.rearrange("(k p) j -> p k j", p=128)).inc(lg, 16)
            gp.dma_start(out=wkn[:, :, :], in_=h_Wk.rearrange("(k p) j -> p k j", p=128)).inc(lg, 16)
            gp.dma_start(out=wvn[:, :, :], in_=h_Wv.rearrange("(k p) j -> p k j", p=128)).inc(lg, 16)

            l_ev = emit_ln(s_x, list(range(NTHA)), xn_fm[:, 0, :, :], "plain", lnps_bf)
            lntok = {0: l_ev.n}
            for h in range(4):
                xh = xn_fm[:, h % 2, :, :]
                te.wait(l_ev, lntok[h])
                sy.wait(l_av, l_av.n)
                te.wait(lg)
                it = 0
                for jc in range(8):
                    for wsrc, dst, bias in ((wqn, qf, bq_fm), (wkn, kf, bk_fm)):
                        sl = it % 2
                        if it >= 2:
                            te.wait(l_qe, l_qe.n - 1)
                        tok = None
                        if it < 2:
                            te.wait(s_q5, s_q5.n)
                        for k in range(8):
                            for (t0, tw) in fch:
                                tok = te.matmul(out=pA[:, sl, t0:t0 + tw],
                                                lhsT=wsrc[:, k, jc * 128:(jc + 1) * 128],
                                                rhs=xh[:, k, t0:t0 + tw],
                                                start=(k == 0), stop=(k == 7))
                        tok.inc(l_qk, 1)
                        ac.wait(l_qk)
                        ac.activation(out=dst[:, jc, :], in_=pA[:, sl, 0:THA], func=AF.Identity,
                                      bias=bias[:, jc:jc + 1]).inc(l_qe, 1)
                        sy.wait(l_qe)
                        sy.dma_start(out=(qh if dst is qf else kh)[0:64, jc, :],
                                     in_=dst[64:128, jc, :]).inc(l_sm, 16)
                        it += 1
                for tt in range(NTHA):
                    sl = tt % 2
                    if tt < 2:
                        te.wait(l_qe)
                    else:
                        te.wait(l_vv, l_vv.n - 1)
                    tok = None
                    for k in range(8):
                        for t0 in range(2):
                            tok = te.matmul(out=pA[:, sl, t0 * 512:(t0 + 1) * 512],
                                            lhsT=xh[:, k, tt * 128:(tt + 1) * 128],
                                            rhs=wvn[:, k, t0 * 512:(t0 + 1) * 512],
                                            start=(k == 0), stop=(k == 7))
                    tok.inc(l_vm, 1)
                    ve.wait(l_vm)
                    if tt == 0:
                        ve.wait(l_av, l_av.n)
                    ve.tensor_add(out=vt[:, tt, :], in0=pA[:, sl, 0:1024],
                                  in1=bv_bc[:, :]).inc(l_vv, 1)
                    sy.wait(l_vv)
                    sy.dma_start(out=vh[0:64, tt, :], in_=vt[64:128, tt, :]).inc(l_sm, 16)
                if ATTN_STAGE < 2:
                    barrier(l_vv)
                    barrier(l_sm)
                    continue
                vv_q = l_vv.n
                sm_q = l_sm.n
                ex_tok = {}; pn_tok = {}; tt_tok = {}; pt_tok = {}
                av_tok = {}; ox_tok = {}; sc_tok = {}; sub_tok = {}

                def scores(tt):
                    sl = tt % 2
                    if tt < 2:
                        te.wait(l_vv, vv_q)
                        te.wait(l_sm, sm_q)
                        te.wait(l_qe, l_qe.n)
                    else:
                        te.wait(s_q5, sub_tok[tt - 2])
                    tok = None
                    for hd in range(16):
                        ch = hd // 2
                        qsrc = qf if hd % 2 == 0 else qh
                        ksrc = kf if hd % 2 == 0 else kh
                        for ib in range(2):
                            tok = te.matmul(out=pA[ib * 64:(ib + 1) * 64, sl, hd * 64:(hd + 1) * 64],
                                            lhsT=qsrc[0:64, ch, tt * 128 + ib * 64:tt * 128 + ib * 64 + 64],
                                            rhs=ksrc[0:64, ch, tt * 128 + ib * 64:tt * 128 + ib * 64 + 64],
                                            start=True, stop=True)
                    tok.inc(l_sc, 1)
                    sc_tok[tt] = l_sc.n

                def chain(tt):
                    sl = tt % 2
                    i = h * NTHA + tt
                    ve.wait(l_sc, sc_tok[tt])
                    ve.reduce_max(out=nm[:, sl, :],
                                  in_=pA[:, sl, :].rearrange("p (a j) -> p a j", a=16),
                                  axis=AX.X)
                    ve.dr()
                    ve.tensor_sub(out=sub32[:, :].rearrange("p (a j) -> p a j", a=16),
                                  in0=pA[:, sl, :].rearrange("p (a j) -> p a j", a=16),
                                  in1=nm[:, sl, :].rearrange("p (a o) -> p a o", o=1).to_broadcast((128, 16, 64))).inc(s_q5, 1)
                    sub_tok[tt] = s_q5.n
                    ac.wait(s_q5, sub_tok[tt])
                    if tt >= 2:
                        ac.wait(l_tt, tt_tok[tt - 2])
                    else:
                        ac.wait(l_tt, l_tt.n)
                    ac.activation(out=pex[:, sl, :], in_=sub32[:, :],
                                  func=AF.Exp, scale=0.125).inc(l_ex, 1)
                    ex_tok[tt] = l_ex.n
                    ve.wait(l_ex, ex_tok[tt])
                    ve.reduce_sum(out=lsum[:, sl, :],
                                  in_=pex[:, sl, :].rearrange("p (a j) -> p a j", a=16),
                                  axis=AX.X)
                    ve.dr()
                    ve.reciprocal(out=linv[:, sl, :], in_=lsum[:, sl, :])
                    ve.dr()
                    ve.tensor_mul(out=pex[:, sl, :].rearrange("p (a j) -> p a j", a=16),
                                  in0=linv[:, sl, :].rearrange("p (a o) -> p a o", o=1).to_broadcast((128, 16, 64)),
                                  in1=pex[:, sl, :].rearrange("p (a j) -> p a j", a=16)).inc(l_pn, 1)
                    pn_tok[tt] = l_pn.n
                    if ATTN_STAGE < 3:
                        return
                    te.wait(l_pn, pn_tok[tt])
                    if tt >= 1:
                        te.wait(l_pt, pt_tok[tt - 1])
                    else:
                        te.wait(l_pt, l_pt.n)
                    tok = None
                    for ib in range(2):
                        for hd in range(16):
                            tok = te.transpose(out=pT[0:64, (ib * 16 + hd) * 64:(ib * 16 + hd) * 64 + 64],
                                               in_=pex[ib * 64:(ib + 1) * 64, sl, hd * 64:(hd + 1) * 64],
                                               identity=eye[ib * 64:(ib + 1) * 64, ib * 64:(ib + 1) * 64])
                    tok.inc(l_tt, 1)
                    tt_tok[tt] = l_tt.n
                    ac.wait(l_tt, tt_tok[tt])
                    if tt >= 2:
                        ac.wait(l_av, av_tok[tt - 2])
                    else:
                        ac.wait(l_av, l_av.n)
                    ac.activation(out=ptb[0:64, sl, :], in_=pT[0:64, :], func=AF.Copy).inc(l_pt, 1)
                    pt_tok[tt] = l_pt.n
                    if ATTN_STAGE < 4:
                        return
                    te.wait(l_pt, pt_tok[tt])
                    if tt >= 1:
                        te.wait(l_ox, ox_tok[tt - 1])
                    else:
                        te.wait(l_ox, l_ox.n)
                        te.wait(l_ev, lntok[h])
                    tok = None
                    for ib in range(2):
                        vsrc = vt if ib == 0 else vh
                        for hd in range(16):
                            tok = te.matmul(out=pO[ib * 64:(ib + 1) * 64, hd * 64:(hd + 1) * 64],
                                            lhsT=ptb[0:64, sl, (ib * 16 + hd) * 64:(ib * 16 + hd) * 64 + 64],
                                            rhs=vsrc[0:64, tt, hd * 64:(hd + 1) * 64],
                                            start=True, stop=True)
                    tok.inc(l_av, 1)
                    av_tok[tt] = l_av.n
                    xr_load(sl, s_x[i * 128:(i + 1) * 128, :])
                    ve.wait(l_av, av_tok[tt]); ve.wait(lxrp[sl])
                    ve.tensor_add(out=xres[:, sl, :], in0=xres[:, sl, :],
                                  in1=pO[:, 0:1024]).inc(l_ox, 1)
                    ox_tok[tt] = l_ox.n
                    xr_used(sl, l_ox)
                    sy.wait(l_ox, ox_tok[tt])
                    sy.dma_start(out=s_x[i * 128:(i + 1) * 128, :],
                                 in_=xres[:, sl, :]).inc(st, 16)

                for tt in range(NTHA):
                    scores(tt)
                    if tt >= 1:
                        chain(tt - 1)
                chain(NTHA - 1)
                if h < 3:
                    l_ev = emit_ln(s_x, list(range((h + 1) * NTHA, (h + 2) * NTHA)),
                                   xn_fm[:, (h + 1) % 2, :, :], "plain", lnps_bf)
                    lntok[h + 1] = l_ev.n
                else:
                    barrier(st)

    emit_mamba()
    p2toks = emit_prep2()
    if nsub >= 2:
        emit_ffn("a", is_last=(nsub == 2), prep_toks=p2toks)
    if nsub >= 3:
        emit_attn()
    if nsub >= 4:
        emit_ffn("b", is_last=True)
    if nsub < 2 or nsub == 3:
        cpb = b.sb("cpb", [128, 2, D], F32)
        for i in range(NT):
            sl = i % 2
            if i >= 2:
                sy.wait(st, st.n)
            sy.dma_start(out=cpb[:, sl, :], in_=s_x[i * 128:(i + 1) * 128, :]).inc(lxrp[sl], 16)
            sy.wait(lxrp[sl])
            sy.dma_start(out=h_out[i * 128:(i + 1) * 128, :], in_=cpb[:, sl, :]).inc(st, 16)
    sy.wait(st)

    b.finalize()
    es.close()
    return nc


def make_inmaps(inputs, T=2048):
    x = np.ascontiguousarray(np.asarray(inputs["L0_attn_a_x"], dtype=np.float32))
    nB = x.shape[0]
    eyec = np.eye(128).astype(ml_dtypes.bfloat16)
    shared = {
        "gamma": np.asarray(inputs["L0_attn_a_gamma"], np.float32),
        "beta": np.asarray(inputs["L0_attn_a_beta"], np.float32),
        "Win": np.asarray(inputs["L0_attn_a_W_in"], np.float32),
        "bin": np.asarray(inputs["L0_attn_a_b_in"], np.float32),
        "convw": np.asarray(inputs["L0_attn_a_conv_w"], np.float32),
        "convb": np.asarray(inputs["L0_attn_a_conv_b"], np.float32),
        "Wxm": np.asarray(inputs["L0_attn_a_W_x"], np.float32),
        "Wdt": np.asarray(inputs["L0_attn_a_W_dt"], np.float32),
        "bdt": np.asarray(inputs["L0_attn_a_b_dt"], np.float32),
        "Alog": np.asarray(inputs["L0_attn_a_A_log"], np.float32),
        "Dp": np.asarray(inputs["L0_attn_a_D_param"], np.float32),
        "Wout": np.asarray(inputs["L0_attn_a_W_out"], np.float32),
        "bout": np.asarray(inputs["L0_attn_a_b_out"], np.float32),
        "Wq": np.asarray(inputs["L0_attn_b_Wq"], np.float32),
        "bq": np.asarray(inputs["L0_attn_b_bq"], np.float32),
        "Wk": np.asarray(inputs["L0_attn_b_Wk"], np.float32),
        "bk": np.asarray(inputs["L0_attn_b_bk"], np.float32),
        "Wv": np.asarray(inputs["L0_attn_b_Wv"], np.float32),
        "bv": np.asarray(inputs["L0_attn_b_bv"], np.float32),
        "W1": np.asarray(inputs["L0_ffn_W1"], np.float32),
        "b1": np.asarray(inputs["L0_ffn_b1"], np.float32),
        "W2": np.asarray(inputs["L0_ffn_W2"], np.float32),
        "b2": np.asarray(inputs["L0_ffn_b2"], np.float32),
        "fscale": np.asarray(inputs["L0_ffn_scale"], np.float32).reshape(1, 1),
        "eyebf": eyec,
    }
    return [dict(shared, x=np.ascontiguousarray(x[bb, :T, :])) for bb in range(nB)]


def kernel(**inputs):
    T = 2048
    nc = build_program(T=T, nsub=4)
    in_maps = make_inmaps(inputs, T=T)
    res = run_bass_kernel_spmd(nc, in_maps, core_ids=list(range(len(in_maps))))
    return np.stack([np.asarray(r["out"], dtype=np.float32) for r in res.results], axis=0)


# version 33
# speedup vs baseline: 1.0107x; 1.0107x over previous
import numpy as np
import contextlib
import ml_dtypes
import concourse.bass as bass
import concourse.mybir as mybir
from concourse.bass_utils import run_bass_kernel_spmd

F32 = mybir.dt.float32
BF16 = mybir.dt.bfloat16
AF = mybir.ActivationFunctionType
ALU = mybir.AluOpType
AX = mybir.AxisListType

D = 1024
ED = 2048
NST = 16
KC = 15
RDT = 64
HF = 4096
EPS = 1e-5

ENGS = ("sync", "vector", "scalar", "tensor", "gpsimd")
ATTN_STAGE = 4


class Sem:
    __slots__ = ("h", "n")

    def __init__(self, h):
        self.h = h
        self.n = 0


class Tok:
    __slots__ = ("q", "box")

    def __init__(self, q, box):
        self.q = q
        self.box = box

    def inc(self, sem, k=1):
        sem.n += k
        box = self.box
        self.q.rec.append(lambda e: box["i"].then_inc(sem.h, k))
        return self


class Q:
    def __init__(self, name):
        self.name = name
        self.rec = []

    def op(self, fname, *a, **kw):
        box = {}

        def fn(e, a=a, kw=kw, fname=fname, box=box):
            box["i"] = getattr(e, fname)(*a, **kw)

        self.rec.append(fn)
        return Tok(self, box)

    def __getattr__(self, fname):
        if fname.startswith("_"):
            raise AttributeError(fname)
        return lambda *a, **kw: self.op(fname, *a, **kw)

    def wait(self, sem, n=None):
        v = sem.n if n is None else n
        if v > 0:
            self.rec.append(lambda e, v=v: e.wait_ge(sem.h, v))

    def dr(self):
        self.rec.append(lambda e: e.drain())


class B:
    def __init__(self, nc, es):
        self.nc = nc
        self.es = es
        self.q = {n: Q(n) for n in ENGS}
        self.sy = self.q["sync"]
        self.ve = self.q["vector"]
        self.ac = self.q["scalar"]
        self.te = self.q["tensor"]
        self.gp = self.q["gpsimd"]
        self._semi = 0

    def sem(self, name=None):
        self._semi += 1
        return Sem(self.es.enter_context(self.nc.semaphore(name or "s%d" % self._semi)))

    def sb(self, name, shape, dt):
        return self.es.enter_context(self.nc.sbuf_tensor(name, list(shape), dt))

    def finalize(self):
        with self.nc.Block() as block:
            for name in ENGS:
                rec = self.q[name].rec

                def body(e, rec=rec):
                    for fn in rec:
                        fn(e)

                getattr(block, name)(body)


def build_program(T=2048, nsub=4):
    NT = T // 128
    TH = T // 2
    NTH = NT // 2
    nc = bass.Bass("TRN2", target_bir_lowering=False)

    def dram(name, shape, dt=F32, kind="ExternalInput"):
        return nc.dram_tensor(name, list(shape), dt, kind=kind).ap()

    hx = dram("x", [T, D])
    h_gamma = dram("gamma", [D]); h_beta = dram("beta", [D])
    h_Win = dram("Win", [2 * ED, D]); h_bin = dram("bin", [2 * ED])
    h_convw = dram("convw", [ED, KC]); h_convb = dram("convb", [ED])
    h_Wx = dram("Wxm", [RDT + 2 * NST, ED])
    h_Wdt = dram("Wdt", [ED, RDT]); h_bdt = dram("bdt", [ED])
    h_Alog = dram("Alog", [ED, NST]); h_Dp = dram("Dp", [ED])
    h_Wout = dram("Wout", [D, ED]); h_bout = dram("bout", [D])
    h_Wq = dram("Wq", [D, D]); h_bq = dram("bq", [D])
    h_Wk = dram("Wk", [D, D]); h_bk = dram("bk", [D])
    h_Wv = dram("Wv", [D, D]); h_bv = dram("bv", [D])
    h_W1 = dram("W1", [HF, D]); h_b1 = dram("b1", [HF])
    h_W2 = dram("W2", [D, HF]); h_b2 = dram("b2", [D])
    h_scale = dram("fscale", [1, 1])
    h_eye = dram("eyebf", [128, 128], BF16)
    h_out = dram("out", [T, D], kind="ExternalOutput")
    s_x = dram("s_x", [T, D], kind="Internal")
    s_WinT = dram("s_WinT", [32, 128, 1024], BF16, kind="Internal")
    s_W1T = dram("s_W1T", [32, 128, 1024], BF16, kind="Internal")
    s_W2T = dram("s_W2T", [32, 128, 1024], BF16, kind="Internal")
    s_WoutT = dram("s_WoutT", [16, 128, 1024], BF16, kind="Internal")
    s_rows = dram("s_rows", [32, T], BF16, kind="Internal")
    s_yfc = dram("s_yfc", [16, 128, T // 4], BF16, kind="Internal")
    s_brow = dram("s_brow", [1, 2 * ED], kind="Internal")

    es = contextlib.ExitStack()
    b = B(nc, es)
    sy, ve, ac, te, gp = b.sy, b.ve, b.ac, b.te, b.gp

    lx = b.sem("lx"); lg = b.sem("lg"); st = b.sem("st")
    pv = b.sem("pv")
    s_dmm = b.sem("sdmm"); s_sp = b.sem("ssp"); s_dl = b.sem("sdl")
    s_dx = b.sem("sdx"); s_dA = b.sem("sdA"); s_dB = b.sem("sdB")
    s_sc = b.sem("ssc"); s_g = b.sem("sg"); s_ym = b.sem("sym")
    s_zm = b.sem("szm"); s_sz = b.sem("ssz"); s_yf = b.sem("syf")
    s_wz0 = b.sem("swz0"); s_wz1 = b.sem("swz1"); s_ev = b.sem("sev")
    s_wzp = (s_wz0, s_wz1)
    s_wzz0 = b.sem("swzz0"); s_wzz1 = b.sem("swzz1"); s_wzz = (s_wzz0, s_wzz1)
    lxr0 = b.sem("lxr0"); lxr1 = b.sem("lxr1"); lxrp = (lxr0, lxr1)
    s_q1 = b.sem("sq1"); s_q2 = b.sem("sq2"); s_q3 = b.sem("sq3")
    s_q4 = b.sem("sq4"); s_q5 = b.sem("sq5")
    l_tp = b.sem("ltp_"); l_ap = b.sem("lap_"); l_ev = b.sem("lev_")
    l_wi = b.sem("lwi"); l_cv = b.sem("lcv"); l_dg = b.sem("ldg")
    l_xin = b.sem("lxin"); l_tl = b.sem("ltl"); l_cm = b.sem("lcm")
    l_om = b.sem("lom"); l_ox = b.sem("lox"); l_wo = b.sem("lwo")
    l_w2 = b.sem("lw2"); l_lt = b.sem("llt"); l_tr = b.sem("ltr")
    l_u = b.sem("lu"); l_h = b.sem("lh"); l_v = b.sem("lv"); l_o = b.sem("lo")
    l_qk = b.sem("lqk"); l_qe = b.sem("lqe"); l_vm = b.sem("lvm")
    l_vv = b.sem("lvv"); l_sc = b.sem("lsc"); l_sm = b.sem("lsm")
    l_ex = b.sem("lex"); l_pn = b.sem("lpn"); l_pt = b.sem("lpt")
    l_tt = b.sem("ltt"); l_av = b.sem("lav")

    eye = b.sb("eye", [128, 128], BF16)
    A_fm = b.sb("A_fm", [128, 16, NST], F32)
    convw_fm = b.sb("convw_fm", [128, 16, KC], F32)
    cb_fm = b.sb("cb_fm", [128, 16], F32)
    Dp_fm = b.sb("Dp_fm", [128, 16], F32)
    bin_fm = b.sb("bin_fm", [128, 32], F32)
    bo_bc = b.sb("bo_bc", [128, D], F32)
    b2_bc = b.sb("b2_bc", [128, D], F32)
    bv_bc = b.sb("bv_bc", [128, D], F32)
    b1_fm = b.sb("b1_fm", [128, 32], F32)
    bq_fm = b.sb("bq_fm", [128, 8], F32)
    bk_fm = b.sb("bk_fm", [128, 8], F32)
    gam_fm = b.sb("gam_fm", [128, 8], F32)
    scale_sb = b.sb("scale_sb", [1, 1], F32)
    bdtrow_fm = b.sb("bdt_fm", [128, 16], F32)
    carry = b.sb("carry", [128, 256], F32)
    stat6 = b.sb("stat6", [128, 12], F32)
    mv2 = b.sb("mv2", [128, 2], F32)
    mu1 = b.sb("mu1", [128, 16], F32)
    ss1 = b.sb("ss1", [128, 16], F32)
    r1a = b.sb("r1a", [128, 16], F32)
    tmp1 = b.sb("tmp1", [128, 1], F32)
    xres = b.sb("xres", [128, 2, D], F32)
    xntok = b.sb("xntok", [128, 2, D], BF16)
    xr_last = {0: None, 1: None}

    def xr_load(sl, src_slice):
        if xr_last[sl] is not None:
            sem, v = xr_last[sl]
            sy.wait(sem, v)
        sy.wait(st, st.n)
        sy.dma_start(out=xres[:, sl, :], in_=src_slice).inc(lxrp[sl], 16)

    def xr_used(sl, sem):
        xr_last[sl] = (sem, sem.n)
    WxT = b.sb("WxT", [128, 16 * 96], BF16)
    WdtT = b.sb("WdtT", [64, ED], BF16)

    sy.dma_start(out=eye[:, :], in_=h_eye).inc(lx, 16)
    for dst, src in [(cb_fm, h_convb), (bdtrow_fm, h_bdt), (Dp_fm, h_Dp),
                     (b1_fm, h_b1), (bq_fm, h_bq), (bk_fm, h_bk),
                     (gam_fm, h_gamma)]:
        sy.dma_start(out=dst[:, :], in_=src.rearrange("(j p) -> p j", p=128), allow_slow_non_contiguous=True).inc(lx, 16)
    sy.dma_start(out=A_fm[:, :, :], in_=h_Alog.rearrange("(c p) n -> p c n", p=128)).inc(lx, 16)
    sy.dma_start(out=convw_fm[:, :, :], in_=h_convw.rearrange("(c p) k -> p c k", p=128)).inc(lx, 16)
    sy.dma_start(out=bo_bc[:, :], in_=h_bout.rearrange("(o d) -> o d", o=1).broadcast_to((128, D))).inc(lx, 16)
    sy.dma_start(out=b2_bc[:, :], in_=h_b2.rearrange("(o d) -> o d", o=1).broadcast_to((128, D))).inc(lx, 16)
    sy.dma_start(out=bv_bc[:, :], in_=h_bv.rearrange("(o d) -> o d", o=1).broadcast_to((128, D))).inc(lx, 16)
    sy.dma_start(out=scale_sb[:, :], in_=h_scale).inc(lx, 16)

    ac.wait(lx)
    ac.activation(out=A_fm[:, :, :], in_=A_fm[:, :, :], func=AF.Exp)
    ac.dr()
    ac.op("mul", A_fm[:, :, :], A_fm[:, :, :], -1.0)
    ve.wait(lx)
    ve.memset(carry[:, :], 0.0)
    ve.dr()

    xr_load(0, hx[0:128, :])
    xr_load(1, hx[128:256, :])
    PREP1 = {}
    with contextlib.ExitStack() as prep:
        pb = lambda n, s, d: prep.enter_context(nc.sbuf_tensor(n, list(s), d))
        pps = lambda n, s, d: prep.enter_context(nc.psum_tensor(n, list(s), d))
        wnat = pb("wnat", [128, 2, 1024], BF16)
        wnatB = pb("wnatB", [128, 8, 4096], BF16)
        wstage = pb("wstage", [128, 2, 1024], BF16)
        beta_bf = pb("beta_bf", [128, 8], BF16)
        binrow = pb("binrow", [1, 2 * ED], F32)
        p_tp = pps("p_tp", [128, 2, 1024], BF16)
        p_br = pps("p_br", [1, 2048], F32)
        ltp0 = b.sem("ltp0"); ltp1 = b.sem("ltp1"); ltpp = (ltp0, ltp1)
        lnb = b.sem("lnb"); lnc = b.sem("lnc")
        lbr = b.sem("lbr")
        ttp = b.sem("ttp"); etp = b.sem("etp")
        wtp0 = b.sem("wtp0"); wtp1 = b.sem("wtp1"); wtpp = (wtp0, wtp1)
        btp = b.sem("btp")

        sy.dma_start(out=binrow[:, :], in_=h_bin.rearrange("(o f) -> o f", o=1)).inc(lbr, 16)
        gp.wait(lx)
        gp.dma_start(out=beta_bf[:, :], in_=h_beta.rearrange("(j p) -> p j", p=128), allow_slow_non_contiguous=True).inc(lnb, 16)

        P1TT = []; P1ET = []; P1WB = []; P1BT = []

        def prep_core(i, src_slice, cache_row, beta=False):
            sl = i % 2
            if i >= 2:
                gp.wait(ttp, P1TT[i - 2])
            gp.dma_start(out=wnat[:, sl, :], in_=src_slice).inc(ltpp[sl], 16)
            te.wait(ltpp[sl])
            if i >= 2:
                te.wait(etp, P1ET[i - 2])
            tok = None
            for k in range(8):
                tok = te.transpose(out=p_tp[:, sl, k * 128:(k + 1) * 128],
                                   in_=wnat[:, sl, k * 128:(k + 1) * 128],
                                   identity=eye[:, :])
            tok.inc(ttp, 8)
            P1TT.append(ttp.n)
            ac.wait(ttp, P1TT[i])
            if i >= 2:
                ac.wait(wtpp[sl], P1WB[i - 2][1])
                if beta:
                    ac.wait(btp, P1BT[i - 2])
            ac.activation(out=wstage[:, sl, :], in_=p_tp[:, sl, :], func=AF.Copy).inc(etp, 1)
            P1ET.append(etp.n)
            if beta:
                te.wait(etp, P1ET[i])
                if i == 0:
                    te.wait(lnb, 16)
                tok = None
                for k in range(8):
                    tok = te.matmul(out=p_br[0:1, (i % 16) * 128:(i % 16) * 128 + 128],
                                    lhsT=beta_bf[:, k:k + 1],
                                    rhs=wstage[:, sl, k * 128:(k + 1) * 128],
                                    start=((i % 4 == 0) and k == 0),
                                    stop=((i % 4 == 3) and k == 7))
                tok.inc(btp, 1)
                if i % 16 == 15:
                    ve.wait(btp, btp.n)
                    ve.wait(lbr)
                    half = i // 16
                    ve.tensor_add(out=binrow[0:1, half * 2048:(half + 1) * 2048],
                                  in0=binrow[0:1, half * 2048:(half + 1) * 2048],
                                  in1=p_br[0:1, :]).inc(pv, 1)
                    te.wait(pv, pv.n)
            P1BT.append(btp.n)
            sy.wait(etp, P1ET[i])
            sy.dma_start(out=cache_row, in_=wstage[:, sl, :]).inc(wtpp[sl], 16)
            P1WB.append((wtpp[sl], wtpp[sl].n))

        for i in range(32):
            prep_core(i, h_Win[i * 128:(i + 1) * 128, :], s_WinT[i, :, :], beta=True)

        def prepB(src_h, nj, ncols, cache):
            for k in range(8):
                gp.dma_start(out=wnatB[:, k, 0:ncols],
                             in_=src_h[k * 128:(k + 1) * 128, :]).inc(lnb, 16)
            te.wait(lnb, lnb.n)
            for j in range(nj):
                i = len(P1TT)
                sl = i % 2
                te.wait(etp, P1ET[i - 2])
                tok = None
                for k in range(8):
                    tok = te.transpose(out=p_tp[:, sl, k * 128:(k + 1) * 128],
                                       in_=wnatB[:, k, j * 128:(j + 1) * 128],
                                       identity=eye[:, :])
                tok.inc(ttp, 8)
                P1TT.append(ttp.n)
                ac.wait(ttp, P1TT[i])
                ac.wait(wtpp[sl], P1WB[i - 2][1])
                if i - 2 < len(P1BT):
                    ac.wait(btp, P1BT[min(i - 2, len(P1BT) - 1)])
                ac.activation(out=wstage[:, sl, :], in_=p_tp[:, sl, :], func=AF.Copy).inc(etp, 1)
                P1ET.append(etp.n)
                sy.wait(etp, P1ET[i])
                sy.dma_start(out=cache[j, :, :], in_=wstage[:, sl, :]).inc(wtpp[sl], 16)
                P1WB.append((wtpp[sl], wtpp[sl].n))

        prepB(h_Wout, 16, 2048, s_WoutT)

        sy.wait(pv)
        sy.dma_start(out=s_brow, in_=binrow[0:1, :]).inc(st, 16)
        sy.wait(st)
        sy.dma_start(out=bin_fm[:, :], in_=s_brow.rearrange("o (j p) -> p (o j)", p=128), allow_slow_non_contiguous=True).inc(lx, 16)

        wxnat = pb("wxnat", [96, ED], BF16)
        wdtnat = pb("wdtnat", [128, 16, RDT], BF16)
        gp.dma_start(out=wxnat[:, :], in_=h_Wx).inc(lnc, 16)
        gp.dma_start(out=wdtnat[:, :, :],
                     in_=h_Wdt.rearrange("(c p) r -> p c r", p=128)).inc(lnc, 16)
        te.wait(lnc)
        for grp in range(2):
            te.wait(etp)
            tok = None
            for cc in range(8):
                c = grp * 8 + cc
                tok = te.transpose(out=p_tp[:, 0, cc * 128:cc * 128 + 96],
                                   in_=wxnat[0:96, c * 128:(c + 1) * 128],
                                   identity=eye[0:96, 0:96])
            tok.inc(ttp, 8)
            ac.wait(ttp)
            tok = None
            for cc in range(8):
                c = grp * 8 + cc
                tok = ac.activation(out=WxT[:, c * 96:(c + 1) * 96],
                                    in_=p_tp[:, 0, cc * 128:cc * 128 + 96],
                                    func=AF.Copy)
            tok.inc(etp, 1)
        for grp in range(2):
            te.wait(etp)
            tok = None
            for cc in range(8):
                c = grp * 8 + cc
                tok = te.transpose(out=p_tp[0:64, 0, cc * 128:(cc + 1) * 128],
                                   in_=wdtnat[:, c, :],
                                   identity=eye[:, :])
            tok.inc(ttp, 8)
            ac.wait(ttp)
            tok = ac.activation(out=WdtT[:, grp * 1024:(grp + 1) * 1024],
                                in_=p_tp[0:64, 0, :], func=AF.Copy)
            tok.inc(etp, 1)
        PREP1.update({"etp": (etp, etp.n), "ttp": (ttp, ttp.n),
                      "btp": (btp, btp.n), "pv": (pv, pv.n),
                      "win": P1WB[0:32], "wout": P1WB[32:48]})


    def emit_prep2():
        with contextlib.ExitStack() as prep:
            pb = lambda n, s, d: prep.enter_context(nc.sbuf_tensor(n, list(s), d))
            pps = lambda n, s, d: prep.enter_context(nc.psum_tensor(n, list(s), d))
            wnat = pb("p2_wnat", [128, 2, 1024], BF16)
            wnatB = pb("p2_wnatB", [128, 8, 4096], BF16)
            wstage = pb("p2_wstage", [128, 2, 1024], BF16)
            p_tp = pps("p2_ptp", [128, 2, 1024], BF16)
            ltp0 = b.sem("p2ltp0"); ltp1 = b.sem("p2ltp1"); ltpp = (ltp0, ltp1)
            lnb = b.sem("p2lnb")
            ttp = b.sem("p2ttp"); etp = b.sem("p2etp")
            wtp0 = b.sem("p2wtp0"); wtp1 = b.sem("p2wtp1"); wtpp = (wtp0, wtp1)
            tt_toks = []; et_toks = []; wb_toks = []

            def prep_row(ci, in_tile, cache_row):
                sl = ci % 2
                if ci >= 2:
                    te.wait(etp, et_toks[ci - 2])
                tok = None
                for k in range(8):
                    tok = te.transpose(out=p_tp[:, sl, k * 128:(k + 1) * 128],
                                       in_=in_tile[k],
                                       identity=eye[:, :])
                tok.inc(ttp, 8)
                tt_toks.append(ttp.n)
                ac.wait(ttp, tt_toks[ci])
                if ci >= 2:
                    ac.wait(wtpp[sl], wb_toks[ci - 2][1])
                ac.activation(out=wstage[:, sl, :], in_=p_tp[:, sl, :],
                              func=AF.Copy).inc(etp, 1)
                et_toks.append(etp.n)
                sy.wait(etp, et_toks[ci])
                sy.dma_start(out=cache_row, in_=wstage[:, sl, :]).inc(wtpp[sl], 16)
                wb_toks.append((wtpp[sl], wtpp[sl].n))

            ci = 0
            for i in range(32):
                sl = i % 2
                if i >= 2:
                    gp.wait(ttp, tt_toks[i - 2])
                gp.dma_start(out=wnat[:, sl, :],
                             in_=h_W1[i * 128:(i + 1) * 128, :]).inc(ltpp[sl], 16)
                te.wait(ltpp[sl])
                prep_row(ci, [wnat[:, sl, k * 128:(k + 1) * 128] for k in range(8)],
                         s_W1T[i, :, :])
                ci += 1
            for k in range(8):
                gp.dma_start(out=wnatB[:, k, :],
                             in_=h_W2[k * 128:(k + 1) * 128, :]).inc(lnb, 16)
            te.wait(lnb, lnb.n)
            for j in range(32):
                prep_row(ci, [wnatB[:, k, j * 128:(j + 1) * 128] for k in range(8)],
                         s_W2T[j, :, :])
                ci += 1
        return (wb_toks[0:32], wb_toks[32:64], (ttp, ttp.n))

    def emit_ln(xsrc, tiles, xn_fm, kind, lnps_bf, preloaded=0):
        for j, i in enumerate(tiles):
            sl = j % 2
            if j >= preloaded:
                xr_load(sl, xsrc[i * 128:(i + 1) * 128, :])
            ve.wait(lxrp[sl])
            ve.bn_stats(out=stat6[:, 0:6], in_=xres[:, sl, 0:512])
            ve.bn_stats(out=stat6[:, 6:12], in_=xres[:, sl, 512:1024])
            ve.dr()
            ve.bn_aggr(out=mv2[:, :], in_=stat6[:, :])
            ve.dr()
            ve.tensor_copy(out=mu1[:, i:i + 1], in_=mv2[:, 0:1])
            if kind == "mamba":
                ve.tensor_scalar(out=tmp1[:, 0:1], in0=mv2[:, 1:2], scalar1=1.0 + EPS,
                                 scalar2=EPS * EPS, op0=ALU.mult, op1=ALU.add)
                ve.dr()
                ve.reciprocal(out=tmp1[:, 0:1], in_=tmp1[:, 0:1])
                ve.dr()
                ve.tensor_scalar(out=r1a[:, i:i + 1], in0=mv2[:, 1:2], scalar1=EPS,
                                 scalar2=None, op0=ALU.add)
                ve.dr()
                ve.reciprocal(out=r1a[:, i:i + 1], in_=r1a[:, i:i + 1]).inc(s_q1, 1)
                ac.wait(s_q1)
                ac.op("sqrt", ss1[:, i:i + 1], tmp1[:, 0:1])
                ac.dr()
                ac.op("sqrt", r1a[:, i:i + 1], r1a[:, i:i + 1]).inc(s_q2, 1)
            else:
                ve.tensor_scalar(out=tmp1[:, 0:1], in0=mv2[:, 1:2], scalar1=EPS,
                                 scalar2=None, op0=ALU.add)
                ve.dr()
                ve.reciprocal(out=tmp1[:, 0:1], in_=tmp1[:, 0:1]).inc(s_q1, 1)
                ac.wait(s_q1)
                ac.op("sqrt", ss1[:, i:i + 1], tmp1[:, 0:1]).inc(s_q2, 1)
            ve.wait(s_q2)
            if j >= 2:
                ve.wait(l_tp, l_tp.n - 8)
            ve.tensor_scalar(out=xntok[:, sl, :], in0=xres[:, sl, :],
                             scalar1=mu1[:, i:i + 1],
                             scalar2=ss1[:, i:i + 1],
                             op0=ALU.subtract, op1=ALU.mult).inc(l_ap, 1)
            xr_used(sl, l_ap)
            te.wait(l_ap)
            te.wait(l_ev, l_ev.n - (0 if j >= 1 else 0))
            if j == 0:
                te.wait(l_cv)
                te.wait(l_ox, l_ox.n)
                te.wait(s_q1, s_q1.n)
                te.wait(PREP1["etp"][0], PREP1["etp"][1])
                te.wait(PREP1["pv"][0], PREP1["pv"][1])
                te.wait(lx, lx.n); ve.wait(lx, lx.n); ac.wait(lx, lx.n)
            tok = None
            for k in range(8):
                tok = te.transpose(out=lnps_bf[:, k * 128:(k + 1) * 128],
                                   in_=xntok[:, sl, k * 128:(k + 1) * 128],
                                   identity=eye[:, :])
            tok.inc(l_tp, 8)
            ac.wait(l_tp)
            if kind == "mamba":
                tok = None
                for k in range(8):
                    tok = ac.activation(out=xn_fm[:, k, j * 128:(j + 1) * 128],
                                        in_=lnps_bf[:, k * 128:(k + 1) * 128],
                                        func=AF.Copy, scale=gam_fm[:, k:k + 1])
                tok.inc(l_ev, 1)
            else:
                ac.activation(out=xn_fm[:, :, j * 128:(j + 1) * 128],
                              in_=lnps_bf.rearrange("p (k c) -> p k c", k=8),
                              func=AF.Copy).inc(l_ev, 1)
        return l_ev

    def barrier(sem):
        for qq in (ve, te, ac, gp, sy):
            qq.wait(sem)

    def emit_mamba():
        TQ = T // 4
        NTQ = NT // 4
        tch = [(s0, min(512, TQ - s0)) for s0 in range(0, TQ, 512)]
        with contextlib.ExitStack() as ms:
            mb = lambda n, s, d: ms.enter_context(nc.sbuf_tensor(n, list(s), d))
            xn_fm = mb("m_xn", [128, 2, 8, TQ], BF16)
            xc = mb("m_xc", [128, 2, 16, TQ], BF16)
            tail = mb("m_tail", [128, 16, 14], BF16)
            dtl = mb("m_dtl", [64, 2, TQ], BF16)
            bcB = mb("m_bcB", [128, 16, TQ], BF16)
            bcC = mb("m_bcC", [128, 16, TQ], BF16)
            delta = mb("m_delta", [128, 2, TQ], F32)
            dxb = mb("m_dxb", [128, TQ], BF16)
            dAb = mb("m_dA", [128, 2, TQ], F32)
            dBxall = mb("m_dBall", [128, 16, TQ], BF16)
            hall = mb("m_hall", [128, 16, TQ], BF16)
            gall = mb("m_gall", [128, 16, TQ], BF16)
            szb = mb("m_sz", [128, 3, TQ], BF16)
            txd = mb("m_txd", [128, D], F32)
            yfull = mb("m_yf", [128, 2, TQ], BF16)
            yfr = mb("m_yfr", [128, 2, TQ], BF16)
            xinp = mb("m_xinp", [128, 2, 14 + TQ], BF16)
            diag = mb("m_diag", [128, KC, 128], BF16)
            bcrows = mb("m_bcr", [32, TQ], BF16)
            wz = mb("m_wz", [128, 2, 1024], BF16)
            wzz = mb("m_wzz", [128, 2, 1024], BF16)
            woutt = mb("m_wouta", [128, 2, 1024], BF16)
            pA = ms.enter_context(nc.psum_tensor("m_pA", [128, 4, 512], F32))
            pB = ms.enter_context(nc.psum_tensor("m_pB", [128, 4, 512], F32))
            lnps_bf = pB[:, 2:4, :].rearrange("p a b -> p (a b)").bitcast(BF16)[:, 0:1024]
            l_fx = b.sem("lfx"); l_yfd0 = b.sem("lyfd0"); l_yfd1 = b.sem("lyfd1")
            yfdp = (l_yfd0, l_yfd1)
            wz_last = {0: None, 1: None}
            wzz_last = {0: None, 1: None}
            EPI_YM = {}; EPI_SZ = {}

            gwz0 = b.sem("gwz0"); gwz1 = b.sem("gwz1"); gwzp = (gwz0, gwz1)

            def wz_load(sl, src_row, pre=None):
                if wz_last[sl] is not None:
                    sem, v = wz_last[sl]
                    gp.wait(sem, v)
                if pre is not None:
                    for t in pre:
                        gp.wait(t[0], t[1])
                gp.dma_start(out=wz[:, sl, :], in_=src_row).inc(gwzp[sl], 16)

            ve.wait(PREP1["ttp"][0], PREP1["ttp"][1])
            ve.memset(tail[:, :, :], 0.0)
            ve.dr()

            FT = {}

            def front_A(h, c):
                hh = h % 2
                sl = c % 2
                pre = None
                if h == 0:
                    pre = [PREP1["win"][c]]
                    if c < 2:
                        pre.append(PREP1["ttp"])
                wz_load(sl, s_WinT[c, :, :], pre=pre)
                te.wait(gwzp[sl])
                if c >= 2:
                    te.wait(l_xin, FT[("xin", h, c - 2)])
                elif ("xin", h - 1, c + 14) in FT:
                    te.wait(l_xin, FT[("xin", h - 1, c + 14)])
                if c == 0:
                    te.wait(l_ev, l_ev.n)
                    te.wait(l_ox, l_ox.n)
                tok = None
                for k in range(8):
                    tok = te.matmul(out=pA[:, 2 + sl, 0:TQ],
                                    lhsT=wz[:, sl, k * 128:(k + 1) * 128],
                                    rhs=xn_fm[:, hh, k, 0:TQ],
                                    start=(k == 0), stop=(k == 7))
                tok.inc(l_wi, 1)
                FT[("wi", h, c)] = l_wi.n
                wz_last[sl] = (l_wi, l_wi.n)

            def front_B(h, c):
                sl = c % 2
                ac.wait(l_wi, FT[("wi", h, c)])
                if c >= 2:
                    ac.wait(l_cm, FT[("cm", h, c - 2)])
                elif ("cm", h - 1, c + 14) in FT:
                    ac.wait(l_cm, FT[("cm", h - 1, c + 14)])
                ac.activation(out=xinp[:, sl, 14:14 + TQ], in_=pA[:, 2 + sl, 0:TQ],
                              func=AF.Identity, bias=bin_fm[:, c:c + 1]).inc(l_xin, 1)
                FT[("xin", h, c)] = l_xin.n
                ac.activation(out=xinp[:, sl, 0:14], in_=tail[:, c, :], func=AF.Copy)
                ac.dr()
                ac.activation(out=tail[:, c, :], in_=xinp[:, sl, TQ:TQ + 14],
                              func=AF.Copy).inc(l_tl, 1)
                FT[("tl", h, c)] = l_tl.n
                if c >= 1:
                    gp.wait(l_cm, FT[("cm", h, c - 1)])
                elif ("cm", h - 1, 15) in FT:
                    gp.wait(l_cm, FT[("cm", h - 1, 15)])
                if h == 0 and c == 0:
                    gp.wait(PREP1["ttp"][0], PREP1["ttp"][1])
                    gp.wait(lx, lx.n)
                gp.tensor_mul(out=diag[:, :, :],
                              in0=convw_fm[:, c, :].rearrange("p (k o) -> p k o", o=1).to_broadcast((128, KC, 128)),
                              in1=eye[:, :].rearrange("p (o j) -> p o j", o=1).to_broadcast((128, KC, 128))).inc(l_dg, 1)
                FT[("dg", h, c)] = l_dg.n
                te.wait(l_dg, FT[("dg", h, c)])
                te.wait(l_tl, FT[("tl", h, c)])
                if c >= 2:
                    te.wait(l_cv, FT[("cv", h, c - 2)])
                elif ("cv", h - 1, c + 14) in FT:
                    te.wait(l_cv, FT[("cv", h - 1, c + 14)])
                tok = None
                for k in range(KC):
                    tok = te.matmul(out=pB[:, 2 + sl, 0:TQ],
                                    lhsT=diag[:, k, :],
                                    rhs=xinp[:, sl, k:k + TQ],
                                    start=(k == 0), stop=(k == KC - 1))
                tok.inc(l_cm, 1)
                FT[("cm", h, c)] = l_cm.n

            def front_C(h, c):
                hh = h % 2
                sl = c % 2
                ac.wait(l_cm, FT[("cm", h, c)])
                if h >= 2:
                    ac.wait(s_dx, 16 * (h - 1))
                ac.activation(out=xc[:, hh, c, :], in_=pB[:, 2 + sl, 0:TQ], func=AF.Silu,
                              bias=cb_fm[:, c:c + 1]).inc(l_cv, 1)
                FT[("cv", h, c)] = l_cv.n

            def front_staged(h, c):
                if c < 16:
                    front_A(h, c)
                if 1 <= c <= 16:
                    front_B(h, c - 1)
                if c >= 2:
                    front_C(h, c - 2)

            BC_TOK = {}

            def front_tail(h):
                hh = h % 2
                if h == 0:
                    te.wait(PREP1["etp"][0], PREP1["etp"][1])
                te.wait(l_cv, l_cv.n)
                te.wait(l_fx, l_fx.n)
                tok = None
                for k in range(16):
                    tok = te.matmul(out=pA[0:96, 2, 0:TQ],
                                    lhsT=WxT[:, k * 96:(k + 1) * 96],
                                    rhs=xc[:, hh, k, 0:TQ],
                                    start=(k == 0), stop=(k == 15))
                tok.inc(s_q4, 1)
                ac.wait(s_q4)
                ac.activation(out=dtl[:, hh, :], in_=pA[0:64, 2, 0:TQ], func=AF.Copy)
                ac.dr()
                ac.activation(out=bcrows[:, :], in_=pA[64:96, 2, 0:TQ], func=AF.Copy).inc(l_fx, 1)
                sy.wait(l_fx)
                sy.dma_start(out=s_rows[:, h * TQ:(h + 1) * TQ], in_=bcrows[:, :]).inc(s_ev, 16)
                gp.wait(s_ev, s_ev.n)
                gp.wait(s_dB, s_dB.n)
                gp.wait(s_g, s_g.n)
                for n in range(16):
                    gp.dma_start(out=bcB[:, n, :],
                                 in_=s_rows[n:n + 1, h * TQ:(h + 1) * TQ].broadcast_to((128, TQ))).inc(lg, 16)
                    gp.dma_start(out=bcC[:, n, :],
                                 in_=s_rows[16 + n:17 + n, h * TQ:(h + 1) * TQ].broadcast_to((128, TQ))).inc(lg, 16)
                BC_TOK[h] = lg.n

            def front(h):
                for c in range(18):
                    front_staged(h, c)
                front_tail(h)

            DTOK = {}

            def emit_delta(h, c):
                hh = h % 2
                sl = c % 2
                te.wait(s_sz, s_sz.n)
                if c == 0:
                    te.wait(l_fx, l_fx.n)
                    if h == 0:
                        te.wait(PREP1["etp"][0], PREP1["etp"][1])
                    if h >= 1:
                        te.wait(l_ox, l_ox.n)
                tok = te.matmul(out=pA[:, sl, 0:TQ],
                                lhsT=WdtT[:, c * 128:(c + 1) * 128],
                                rhs=dtl[:, hh, 0:TQ],
                                start=True, stop=True)
                tok.inc(s_dmm, 1)
                DTOK[c] = s_dmm.n

            def scan(h):
                hh = h % 2
                emit_delta(h, 0)
                for c in range(16):
                    sl = c % 2
                    ac.wait(s_dmm, DTOK[c])
                    ac.wait(s_dl, s_dl.n)
                    ac.activation(out=txd[:, 512:512 + TQ], in_=pA[:, sl, 0:TQ], func=AF.Abs)
                    ac.dr()
                    ac.activation(out=txd[:, 512:512 + TQ], in_=txd[:, 512:512 + TQ], func=AF.Exp, scale=-1.0)
                    ac.dr()
                    ac.activation(out=txd[:, 512:512 + TQ], in_=txd[:, 512:512 + TQ], func=AF.Ln, bias=1.0)
                    ac.dr()
                    ac.wait(s_dx, s_dx.n)
                    ac.activation(out=delta[:, sl, 0:TQ], in_=pA[:, sl, 0:TQ],
                                  func=AF.Relu).inc(s_sp, 1)
                    ve.wait(s_sp)
                    if c == 0:
                        ve.wait(lg, BC_TOK[h])
                    ve.tensor_add(out=delta[:, sl, :], in0=delta[:, sl, :],
                                  in1=txd[:, 512:512 + TQ]).inc(s_dl, 1)
                    if wzz_last[sl] is not None:
                        sy.wait(wzz_last[sl][0], wzz_last[sl][1])
                    if h == 0:
                        sy.wait(PREP1["win"][16 + c][0], PREP1["win"][16 + c][1])
                        if c == 0:
                            sy.wait(PREP1["btp"][0], PREP1["btp"][1])
                    sy.dma_start(out=wzz[:, sl, :], in_=s_WinT[16 + c, :, :]).inc(s_wzz[sl], 16)
                    te.wait(s_wzz[sl])
                    te.wait(s_sp)
                    tok = None
                    for k in range(8):
                        tok = te.matmul(out=pA[:, sl, 0:TQ],
                                        lhsT=wzz[:, sl, k * 128:(k + 1) * 128],
                                        rhs=xn_fm[:, hh, k, 0:TQ],
                                        start=(k == 0), stop=(k == 7))
                    tok.inc(s_zm, 1)
                    wzz_last[sl] = (s_zm, s_zm.n)
                    if c < 15:
                        emit_delta(h, c + 1)
                    ve.dr()
                    ve.tensor_mul(out=dxb[:, :], in0=delta[:, sl, :],
                                  in1=xc[:, hh, c, :]).inc(s_dx, 1)
                    ve.dr()
                    ve.tensor_mul(out=dBxall[:, :, :],
                                  in0=dxb[:, :].rearrange("p (o t) -> p o t", o=1).to_broadcast((128, 16, TQ)),
                                  in1=bcB[:, :, :]).inc(s_dB, 1)
                    ve.dr()
                    for n in range(16):
                        nn = n % 2
                        gidx = c * 16 + n
                        ac.wait(s_dl)
                        if gidx >= 2:
                            ac.wait(s_sc, s_sc.n - 1)
                        ac.activation(out=dAb[:, nn, :], in_=delta[:, sl, :], func=AF.Exp,
                                      scale=A_fm[:, c, n:n + 1]).inc(s_dA, 1)
                        ve.wait(s_dA)
                        ve.tensor_tensor_scan(out=hall[:, n, :], data0=dAb[:, nn, :],
                                              data1=dBxall[:, n, :],
                                              initial=carry[:, gidx % 256:gidx % 256 + 1],
                                              op0=ALU.mult, op1=ALU.add).inc(s_sc, 1)
                    ac.wait(s_zm, s_zm.n)
                    ac.wait(s_yf, s_yf.n)
                    ac.activation(out=szb[:, c % 3, 0:TQ], in_=pA[:, sl, 0:TQ], func=AF.Silu,
                                  bias=bin_fm[:, 16 + c:17 + c]).inc(s_sz, 1)
                    EPI_SZ[c] = s_sz.n
                    ve.dr()
                    ve.tensor_copy(out=carry[:, c * 16:c * 16 + 16],
                                   in_=hall[:, :, TQ - 1:TQ])
                    ve.wait(s_ym, s_ym.n)
                    ve.tensor_mul(out=gall[:, :, :], in0=hall[:, :, :],
                                  in1=bcC[:, :, :]).inc(s_g, 1)
                    te.wait(s_g)
                    te.wait(s_yf, s_yf.n - 1)
                    if c == 0 and h >= 1:
                        te.wait(l_ox, l_ox.n)
                    tok = None
                    for n in range(16):
                        tok = te.matmul(out=pB[:, sl, 0:TQ],
                                        lhsT=eye[:, :],
                                        rhs=gall[:, n, :],
                                        start=(n == 0), stop=(n == 15))
                    tok.inc(s_ym, 1)
                    EPI_YM[c] = s_ym.n
                    if c >= 1:
                        epilogue_ve(h, c - 1)
                    if c == 15:
                        epilogue_ve(h, 15)
                    if h < 3:
                        front_staged(h + 1, c)

            def epilogue_ve(h, c):
                hh = h % 2
                sl = c % 2
                ve.wait(s_ym, EPI_YM[c])
                ve.wait(s_sz, EPI_SZ[c])
                ve.dr()
                ve.scalar_tensor_tensor(out=txd[:, 0:TQ], in0=xc[:, hh, c, :],
                                        scalar=Dp_fm[:, c:c + 1], in1=pB[:, sl, 0:TQ],
                                        op0=ALU.mult, op1=ALU.add)
                ve.dr()
                ve.wait(yfdp[sl], yfdp[sl].n)
                ve.tensor_mul(out=yfull[:, sl, :], in0=txd[:, 0:TQ],
                              in1=szb[:, c % 3, 0:TQ]).inc(s_yf, 1)
                sy.wait(s_yf)
                sy.dma_start(out=s_yfc[c, :, :], in_=yfull[:, sl, :]).inc(yfdp[sl], 16)

            def wout(h):
                ptile = [pA[:, 0:2, :], pA[:, 2:4, :], pB[:, 0:2, :], pB[:, 2:4, :]]
                ptile = [p.rearrange("p a b -> p (a b)") for p in ptile]
                te.wait(s_yf)
                te.wait(s_sz); te.wait(s_sp)
                te.wait(l_xin); te.wait(l_cv); te.wait(l_ev); te.wait(l_fx)
                sy.wait(l_yfd0); sy.wait(l_yfd1)
                for k in range(16):
                    ksl = k % 2
                    if wz_last[ksl] is not None:
                        sem, v = wz_last[ksl]
                        sy.wait(sem, v)
                    if h == 0:
                        sy.wait(PREP1["wout"][k][0], PREP1["wout"][k][1])
                    sy.dma_start(out=woutt[:, ksl, :], in_=s_WoutT[k, :, :]).inc(s_wzp[ksl], 16)
                    sy.dma_start(out=yfr[:, ksl, :], in_=s_yfc[k, :, :]).inc(s_wzp[ksl], 16)
                    te.wait(s_wzp[ksl])
                    tok = None
                    for tt in range(NTQ):
                        for t0 in range(2):
                            tok = te.matmul(out=ptile[tt][:, t0 * 512:(t0 + 1) * 512],
                                            lhsT=yfr[:, ksl, tt * 128:(tt + 1) * 128],
                                            rhs=woutt[:, ksl, t0 * 512:(t0 + 1) * 512],
                                            start=(k == 0), stop=(k == 15))
                    tok.inc(l_om, 1)
                    wz_last[ksl] = (l_om, l_om.n)
                for tt in range(NTQ):
                    sl = tt % 2
                    i = h * NTQ + tt
                    xr_load(sl, hx[i * 128:(i + 1) * 128, :])
                    ve.wait(l_om); ve.wait(lxrp[sl])
                    ve.tensor_scalar(out=txd[:, :], in0=xres[:, sl, :],
                                     scalar1=mu1[:, i:i + 1],
                                     scalar2=r1a[:, i:i + 1],
                                     op0=ALU.subtract, op1=ALU.mult)
                    ve.dr()
                    ve.tensor_add(out=txd[:, :], in0=txd[:, :], in1=xres[:, sl, :])
                    ve.dr()
                    ve.tensor_add(out=txd[:, :], in0=txd[:, :], in1=bo_bc[:, :])
                    ve.dr()
                    ve.tensor_add(out=xres[:, sl, :], in0=txd[:, :],
                                  in1=ptile[tt][:, 0:1024]).inc(l_ox, 1)
                    xr_used(sl, l_ox)
                    sy.wait(l_ox)
                    sy.dma_start(out=s_x[i * 128:(i + 1) * 128, :],
                                 in_=xres[:, sl, :]).inc(st, 16)

            def ln_q(h):
                tiles = list(range(h * NTQ, (h + 1) * NTQ))
                return emit_ln(hx, tiles, xn_fm[:, h % 2, :, :], "mamba", lnps_bf,
                               preloaded=(2 if h == 0 else 0))

            l_ev0 = ln_q(0)
            barrier(l_ev0)
            front(0)
            for h in range(4):
                if h < 3:
                    ln_q(h + 1)
                scan(h)
                if h < 3:
                    front_staged(h + 1, 16)
                    front_staged(h + 1, 17)
                    front_tail(h + 1)
                wout(h)
            barrier(st)

    def emit_ffn(tag, is_last, prep_toks=None):
        THF = T // 4
        NTHF = NT // 4
        fch = [(s0, min(512, THF - s0)) for s0 in range(0, THF, 512)]
        with contextlib.ExitStack() as fs:
            fb = lambda n, s, d: fs.enter_context(nc.sbuf_tensor(n, list(s), d))
            xn_fm = fb("f_xn" + tag, [128, 2, 8, THF], BF16)
            hT = fb("f_h" + tag, [128, 32, THF], BF16)
            w2a = fb("f_w2" + tag, [128, 32, 1024], BF16)
            w1s = fb("f_w1" + tag, [128, 3, 1024], BF16)
            ltc = fb("f_ltc" + tag, [128, 8, 16], BF16)
            ev1 = fb("f_ev1" + tag, [1, 4096], BF16)
            ev4 = fb("f_ev4" + tag, [1, 4096], BF16)
            ufl = fb("f_ufl" + tag, [1, 4 * THF], BF16)
            vfl = fb("f_vfl" + tag, [1, 16 * THF], BF16)
            vsb = fb("f_vsb" + tag, [4, THF], BF16)
            usb = fb("f_usb" + tag, [4, THF], BF16)
            pA = fs.enter_context(nc.psum_tensor("f_pA" + tag, [128, 2, 1024], F32))
            pR = fs.enter_context(nc.psum_tensor("f_pR" + tag, [4, 1024], F32))
            pLN = fs.enter_context(nc.psum_tensor("f_pLN" + tag, [128, 1024], BF16))
            pRow = pLN[0:1, :]

            W1TOK = {}
            w1sem = (s_wzp[0], s_wzp[1], s_wzz[0])
            l_ev = emit_ln(s_x, list(range(NTHF)), xn_fm[:, 0, :, :], "plain", pLN[:, :])
            lntok = {0: l_ev.n}
            ptoks = prep_toks

            def load_w2a():
                if ptoks is not None:
                    gp.wait(ptoks[2][0], ptoks[2][1])
                    gp.wait(ptoks[1][30][0], ptoks[1][30][1])
                    gp.wait(ptoks[1][31][0], ptoks[1][31][1])
                for k in range(32):
                    gp.dma_start(out=w2a[:, k, :], in_=s_W2T[k, :, :]).inc(l_w2, 16)
            for h in range(4):
                xh = xn_fm[:, h % 2, :, :]
                te.wait(l_ev, lntok[h])
                if h == 0:
                    ve.wait(l_ev, lntok[0])
                    ve.tensor_copy(out=ltc[:, :, :], in_=xh[:, :, 1:17]).inc(l_lt, 1)
                    te.wait(l_lt)
                    for dstrow, toks in ((ev1, (0, 2, 4, 6)), (ev4, (8, 10, 12, 14))):
                        for ii, tk in enumerate(toks):
                            te.wait(s_q1)
                            tok = None
                            for k in range(8):
                                tok = te.transpose(out=pRow[0:1, k * 128:(k + 1) * 128],
                                                   in_=ltc[:, k, tk:tk + 1],
                                                   identity=eye[:, :])
                            tok.inc(l_tr, 1)
                            ac.wait(l_tr)
                            ac.activation(out=dstrow[0:1, ii * 1024:(ii + 1) * 1024],
                                          in_=pRow[0:1, :], func=AF.Copy,
                                          scale=scale_sb[0:1, 0:1]).inc(s_q1, 1)
                if h == 0:
                    te.wait(s_q1)
                te.wait(s_q2, s_q2.n)
                tok = None
                for k in range(8):
                    for (t0, tw) in fch:
                        tok = te.matmul(out=pR[:, t0:t0 + tw],
                                        lhsT=ltc[:, k, 1:8:2],
                                        rhs=xh[:, k, t0:t0 + tw],
                                        start=(k == 0), stop=(k == 7))
                tok.inc(l_u, 1)
                ac.wait(l_u)
                if h >= 1:
                    ac.wait(lg, lg.n)
                ac.activation(out=usb[:, :], in_=pR[0:4, 0:THF], func=AF.Copy).inc(s_q1, 1)
                gp.wait(s_q1)
                for ii in range(4):
                    gp.dma_start(out=ufl[0:1, ii * THF:(ii + 1) * THF],
                                 in_=usb[ii:ii + 1, :]).inc(lg, 16)
                if h == 0:
                    load_w2a()
                for c in range(32):
                    sl = c % 2
                    sl3 = c % 3
                    if c >= 3:
                        sy.wait(s_q3, W1TOK[c - 3])
                    if ptoks is not None and h == 0:
                        sy.wait(ptoks[0][c][0], ptoks[0][c][1])
                    sy.dma_start(out=w1s[:, sl3, :], in_=s_W1T[c, :, :]).inc(w1sem[sl3], 16)
                    te.wait(w1sem[sl3])
                    if c >= 2:
                        te.wait(l_h, l_h.n - 1)
                    tok = None
                    for k in range(8):
                        for (t0, tw) in fch:
                            tok = te.matmul(out=pA[:, sl, t0:t0 + tw],
                                            lhsT=w1s[:, sl3, k * 128:(k + 1) * 128],
                                            rhs=xh[:, k, t0:t0 + tw],
                                            start=(k == 0), stop=False)
                    if c == 0:
                        te.wait(lg, lg.n)
                    ii = c // 8
                    for ti, (t0, tw) in enumerate(fch):
                        tok = te.matmul(out=pA[:, sl, t0:t0 + tw],
                                        lhsT=ev1[0:1, ii * 1024 + (c % 8) * 128:ii * 1024 + (c % 8) * 128 + 128],
                                        rhs=ufl[0:1, ii * THF + t0:ii * THF + t0 + tw],
                                        start=False, stop=True)
                    tok.inc(s_q3, 1)
                    W1TOK[c] = s_q3.n
                    ac.wait(s_q3)
                    ac.activation(out=hT[:, c, :], in_=pA[:, sl, 0:THF], func=AF.Gelu,
                                  bias=b1_fm[:, c:c + 1]).inc(l_h, 1)
                    if c % 8 == 7:
                        qi = c // 8
                        if qi == 0:
                            te.wait(s_q1, s_q1.n)
                        else:
                            te.wait(lg, lg.n)
                        te.wait(l_h, l_h.n)
                        tok = None
                        for k in range(8):
                            for (t0, tw) in fch:
                                tok = te.matmul(out=pR[:, t0:t0 + tw],
                                                lhsT=ltc[:, k, 9:16:2],
                                                rhs=hT[:, qi * 8 + k, t0:t0 + tw],
                                                start=(k == 0), stop=(k == 7))
                        tok.inc(l_v, 1)
                        ac.wait(l_v)
                        if qi >= 1:
                            ac.wait(lg, lg.n)
                        ac.activation(out=vsb[:, 0:THF], in_=pR[:, 0:THF],
                                      func=AF.Copy).inc(s_q4, 1)
                        gp.wait(s_q4)
                        for ii2 in range(4):
                            gp.dma_start(out=vfl[0:1, (qi * 4 + ii2) * THF:(qi * 4 + ii2 + 1) * THF],
                                         in_=vsb[ii2:ii2 + 1, 0:THF]).inc(lg, 16)
                if h < 3:
                    l_ev = emit_ln(s_x, list(range((h + 1) * NTHF, (h + 2) * NTHF)),
                                   xn_fm[:, (h + 1) % 2, :, :], "plain", pLN[:, :])
                    lntok[h + 1] = l_ev.n
                te.wait(l_h, l_h.n)
                te.wait(lg, lg.n)
                for tt in range(NTHF):
                    sl = tt % 2
                    i = h * NTHF + tt
                    xr_load(sl, s_x[i * 128:(i + 1) * 128, :])
                    te.wait(l_w2)
                    if tt >= 2:
                        te.wait(l_ox, l_ox.n - 1)
                    tok = None
                    for k in range(32):
                        for t0 in range(2):
                            tok = te.matmul(out=pA[:, sl, t0 * 512:(t0 + 1) * 512],
                                            lhsT=hT[:, k, tt * 128:(tt + 1) * 128],
                                            rhs=w2a[:, k, t0 * 512:(t0 + 1) * 512],
                                            start=(k == 0), stop=False)
                    for ii in range(4):
                        for qi in range(4):
                            tok = te.matmul(out=pA[:, sl, ii * 256:(ii + 1) * 256],
                                            lhsT=vfl[0:1, (qi * 4 + ii) * THF + tt * 128:(qi * 4 + ii) * THF + tt * 128 + 128],
                                            rhs=ev4[0:1, ii * 1024 + qi:ii * 1024 + qi + 1021:4],
                                            start=False, stop=(qi == 3 and ii % 2 == 1))
                    tok.inc(l_o, 1)
                    ve.wait(l_o); ve.wait(lxrp[sl])
                    ve.tensor_add(out=xres[:, sl, :], in0=xres[:, sl, :], in1=b2_bc[:, :])
                    ve.dr()
                    ve.tensor_add(out=xres[:, sl, :], in0=xres[:, sl, :],
                                  in1=pA[:, sl, 0:1024]).inc(l_ox, 1)
                    xr_used(sl, l_ox)
                    sy.wait(l_ox)
                    dst = h_out if is_last else s_x
                    sy.dma_start(out=dst[i * 128:(i + 1) * 128, :],
                                 in_=xres[:, sl, :]).inc(st, 16)
                if h == 3:
                    barrier(st)

    def emit_attn():
        THA = T // 4
        NTHA = NT // 4
        fch = [(s0, min(512, THA - s0)) for s0 in range(0, THA, 512)]
        with contextlib.ExitStack() as ats:
            ab = lambda n, s, d: ats.enter_context(nc.sbuf_tensor(n, list(s), d))
            xn_fm = ab("a_xn", [128, 8, THA], BF16)
            wqn = ab("a_wq", [128, 8, 1024], BF16)
            wkn = ab("a_wk", [128, 8, 1024], BF16)
            wvn = ab("a_wv", [128, 8, 1024], BF16)
            qf = ab("a_q", [128, 8, THA], BF16)
            kf = ab("a_k", [128, 8, THA], BF16)
            vt = ab("a_v", [128, NTHA, 1024], BF16)
            qh = ab("a_qh", [64, 8, THA], BF16)
            kh = ab("a_kh", [64, 8, THA], BF16)
            vh = ab("a_vh", [64, NTHA, 1024], BF16)
            pex = ab("a_pex", [128, 2, 1024], BF16)
            sub32 = ab("a_sub", [128, 1024], F32)
            ptb = ab("a_ptb", [64, 2, 2048], BF16)
            nm = ab("a_nm", [128, 2, 16], F32)
            lsum = ab("a_l", [128, 2, 16], F32)
            linv = ab("a_linv", [128, 2, 16], F32)
            pA = ats.enter_context(nc.psum_tensor("a_pA", [128, 2, 1024], F32))
            pT = ats.enter_context(nc.psum_tensor("a_pT", [64, 2048], BF16))
            pO = ats.enter_context(nc.psum_tensor("a_pO", [128, 1024], F32))
            lnps_bf = pA[:, 0, :].bitcast(BF16)[:, 0:1024]

            gp.dma_start(out=wqn[:, :, :], in_=h_Wq.rearrange("(k p) j -> p k j", p=128)).inc(lg, 16)
            gp.dma_start(out=wkn[:, :, :], in_=h_Wk.rearrange("(k p) j -> p k j", p=128)).inc(lg, 16)
            gp.dma_start(out=wvn[:, :, :], in_=h_Wv.rearrange("(k p) j -> p k j", p=128)).inc(lg, 16)

            for h in range(4):
                tiles = list(range(h * NTHA, (h + 1) * NTHA))
                l_ev = emit_ln(s_x, tiles, xn_fm, "plain", lnps_bf)
                barrier(l_ev)
                te.wait(lg)
                it = 0
                for jc in range(8):
                    for wsrc, dst, bias in ((wqn, qf, bq_fm), (wkn, kf, bk_fm)):
                        sl = it % 2
                        if it >= 2:
                            te.wait(l_qe, l_qe.n - 1)
                        tok = None
                        for k in range(8):
                            for (t0, tw) in fch:
                                tok = te.matmul(out=pA[:, sl, t0:t0 + tw],
                                                lhsT=wsrc[:, k, jc * 128:(jc + 1) * 128],
                                                rhs=xn_fm[:, k, t0:t0 + tw],
                                                start=(k == 0), stop=(k == 7))
                        tok.inc(l_qk, 1)
                        ac.wait(l_qk)
                        ac.activation(out=dst[:, jc, :], in_=pA[:, sl, 0:THA], func=AF.Identity,
                                      bias=bias[:, jc:jc + 1]).inc(l_qe, 1)
                        sy.wait(l_qe)
                        sy.dma_start(out=(qh if dst is qf else kh)[0:64, jc, :],
                                     in_=dst[64:128, jc, :]).inc(l_sm, 16)
                        it += 1
                for tt in range(NTHA):
                    sl = tt % 2
                    if tt < 2:
                        te.wait(l_qe)
                    else:
                        te.wait(l_vv, l_vv.n - 1)
                    tok = None
                    for k in range(8):
                        for t0 in range(2):
                            tok = te.matmul(out=pA[:, sl, t0 * 512:(t0 + 1) * 512],
                                            lhsT=xn_fm[:, k, tt * 128:(tt + 1) * 128],
                                            rhs=wvn[:, k, t0 * 512:(t0 + 1) * 512],
                                            start=(k == 0), stop=(k == 7))
                    tok.inc(l_vm, 1)
                    ve.wait(l_vm)
                    ve.tensor_add(out=vt[:, tt, :], in0=pA[:, sl, 0:1024],
                                  in1=bv_bc[:, :]).inc(l_vv, 1)
                    sy.wait(l_vv)
                    sy.dma_start(out=vh[0:64, tt, :], in_=vt[64:128, tt, :]).inc(l_sm, 16)
                if ATTN_STAGE < 2:
                    barrier(l_vv)
                    barrier(l_sm)
                    continue
                vv_q = l_vv.n
                sm_q = l_sm.n
                ex_tok = {}; pn_tok = {}; tt_tok = {}; pt_tok = {}
                av_tok = {}; ox_tok = {}; sc_tok = {}; sub_tok = {}

                def scores(tt):
                    sl = tt % 2
                    if tt < 2:
                        te.wait(l_vv, vv_q)
                        te.wait(l_sm, sm_q)
                        te.wait(l_qe, l_qe.n)
                    else:
                        te.wait(s_q5, sub_tok[tt - 2])
                    tok = None
                    for hd in range(16):
                        ch = hd // 2
                        qsrc = qf if hd % 2 == 0 else qh
                        ksrc = kf if hd % 2 == 0 else kh
                        for ib in range(2):
                            tok = te.matmul(out=pA[ib * 64:(ib + 1) * 64, sl, hd * 64:(hd + 1) * 64],
                                            lhsT=qsrc[0:64, ch, tt * 128 + ib * 64:tt * 128 + ib * 64 + 64],
                                            rhs=ksrc[0:64, ch, tt * 128 + ib * 64:tt * 128 + ib * 64 + 64],
                                            start=True, stop=True)
                    tok.inc(l_sc, 1)
                    sc_tok[tt] = l_sc.n

                def chain(tt):
                    sl = tt % 2
                    i = h * NTHA + tt
                    ve.wait(l_sc, sc_tok[tt])
                    ve.reduce_max(out=nm[:, sl, :],
                                  in_=pA[:, sl, :].rearrange("p (a j) -> p a j", a=16),
                                  axis=AX.X)
                    ve.dr()
                    ve.tensor_sub(out=sub32[:, :].rearrange("p (a j) -> p a j", a=16),
                                  in0=pA[:, sl, :].rearrange("p (a j) -> p a j", a=16),
                                  in1=nm[:, sl, :].rearrange("p (a o) -> p a o", o=1).to_broadcast((128, 16, 64))).inc(s_q5, 1)
                    sub_tok[tt] = s_q5.n
                    ac.wait(s_q5, sub_tok[tt])
                    if tt >= 2:
                        ac.wait(l_tt, tt_tok[tt - 2])
                    ac.activation(out=pex[:, sl, :], in_=sub32[:, :],
                                  func=AF.Exp, scale=0.125).inc(l_ex, 1)
                    ex_tok[tt] = l_ex.n
                    ve.wait(l_ex, ex_tok[tt])
                    ve.reduce_sum(out=lsum[:, sl, :],
                                  in_=pex[:, sl, :].rearrange("p (a j) -> p a j", a=16),
                                  axis=AX.X)
                    ve.dr()
                    ve.reciprocal(out=linv[:, sl, :], in_=lsum[:, sl, :])
                    ve.dr()
                    ve.tensor_mul(out=pex[:, sl, :].rearrange("p (a j) -> p a j", a=16),
                                  in0=linv[:, sl, :].rearrange("p (a o) -> p a o", o=1).to_broadcast((128, 16, 64)),
                                  in1=pex[:, sl, :].rearrange("p (a j) -> p a j", a=16)).inc(l_pn, 1)
                    pn_tok[tt] = l_pn.n
                    if ATTN_STAGE < 3:
                        return
                    te.wait(l_pn, pn_tok[tt])
                    if tt >= 1:
                        te.wait(l_pt, pt_tok[tt - 1])
                    tok = None
                    for ib in range(2):
                        for hd in range(16):
                            tok = te.transpose(out=pT[0:64, (ib * 16 + hd) * 64:(ib * 16 + hd) * 64 + 64],
                                               in_=pex[ib * 64:(ib + 1) * 64, sl, hd * 64:(hd + 1) * 64],
                                               identity=eye[ib * 64:(ib + 1) * 64, ib * 64:(ib + 1) * 64])
                    tok.inc(l_tt, 1)
                    tt_tok[tt] = l_tt.n
                    ac.wait(l_tt, tt_tok[tt])
                    if tt >= 2:
                        ac.wait(l_av, av_tok[tt - 2])
                    ac.activation(out=ptb[0:64, sl, :], in_=pT[0:64, :], func=AF.Copy).inc(l_pt, 1)
                    pt_tok[tt] = l_pt.n
                    if ATTN_STAGE < 4:
                        return
                    te.wait(l_pt, pt_tok[tt])
                    if tt >= 1:
                        te.wait(l_ox, ox_tok[tt - 1])
                    tok = None
                    for ib in range(2):
                        vsrc = vt if ib == 0 else vh
                        for hd in range(16):
                            tok = te.matmul(out=pO[ib * 64:(ib + 1) * 64, hd * 64:(hd + 1) * 64],
                                            lhsT=ptb[0:64, sl, (ib * 16 + hd) * 64:(ib * 16 + hd) * 64 + 64],
                                            rhs=vsrc[0:64, tt, hd * 64:(hd + 1) * 64],
                                            start=True, stop=True)
                    tok.inc(l_av, 1)
                    av_tok[tt] = l_av.n
                    xr_load(sl, s_x[i * 128:(i + 1) * 128, :])
                    ve.wait(l_av, av_tok[tt]); ve.wait(lxrp[sl])
                    ve.tensor_add(out=xres[:, sl, :], in0=xres[:, sl, :],
                                  in1=pO[:, 0:1024]).inc(l_ox, 1)
                    ox_tok[tt] = l_ox.n
                    xr_used(sl, l_ox)
                    sy.wait(l_ox, ox_tok[tt])
                    sy.dma_start(out=s_x[i * 128:(i + 1) * 128, :],
                                 in_=xres[:, sl, :]).inc(st, 16)

                for tt in range(NTHA):
                    scores(tt)
                    if tt >= 1:
                        chain(tt - 1)
                chain(NTHA - 1)
                barrier(st)

    emit_mamba()
    p2toks = emit_prep2()
    if nsub >= 2:
        emit_ffn("a", is_last=(nsub == 2), prep_toks=p2toks)
    if nsub >= 3:
        emit_attn()
    if nsub >= 4:
        emit_ffn("b", is_last=True)
    if nsub < 2 or nsub == 3:
        cpb = b.sb("cpb", [128, 2, D], F32)
        for i in range(NT):
            sl = i % 2
            if i >= 2:
                sy.wait(st, st.n)
            sy.dma_start(out=cpb[:, sl, :], in_=s_x[i * 128:(i + 1) * 128, :]).inc(lxrp[sl], 16)
            sy.wait(lxrp[sl])
            sy.dma_start(out=h_out[i * 128:(i + 1) * 128, :], in_=cpb[:, sl, :]).inc(st, 16)
    sy.wait(st)

    b.finalize()
    es.close()
    return nc


def make_inmaps(inputs, T=2048):
    x = np.ascontiguousarray(np.asarray(inputs["L0_attn_a_x"], dtype=np.float32))
    nB = x.shape[0]
    eyec = np.eye(128).astype(ml_dtypes.bfloat16)
    shared = {
        "gamma": np.asarray(inputs["L0_attn_a_gamma"], np.float32),
        "beta": np.asarray(inputs["L0_attn_a_beta"], np.float32),
        "Win": np.asarray(inputs["L0_attn_a_W_in"], np.float32),
        "bin": np.asarray(inputs["L0_attn_a_b_in"], np.float32),
        "convw": np.asarray(inputs["L0_attn_a_conv_w"], np.float32),
        "convb": np.asarray(inputs["L0_attn_a_conv_b"], np.float32),
        "Wxm": np.asarray(inputs["L0_attn_a_W_x"], np.float32),
        "Wdt": np.asarray(inputs["L0_attn_a_W_dt"], np.float32),
        "bdt": np.asarray(inputs["L0_attn_a_b_dt"], np.float32),
        "Alog": np.asarray(inputs["L0_attn_a_A_log"], np.float32),
        "Dp": np.asarray(inputs["L0_attn_a_D_param"], np.float32),
        "Wout": np.asarray(inputs["L0_attn_a_W_out"], np.float32),
        "bout": np.asarray(inputs["L0_attn_a_b_out"], np.float32),
        "Wq": np.asarray(inputs["L0_attn_b_Wq"], np.float32),
        "bq": np.asarray(inputs["L0_attn_b_bq"], np.float32),
        "Wk": np.asarray(inputs["L0_attn_b_Wk"], np.float32),
        "bk": np.asarray(inputs["L0_attn_b_bk"], np.float32),
        "Wv": np.asarray(inputs["L0_attn_b_Wv"], np.float32),
        "bv": np.asarray(inputs["L0_attn_b_bv"], np.float32),
        "W1": np.asarray(inputs["L0_ffn_W1"], np.float32),
        "b1": np.asarray(inputs["L0_ffn_b1"], np.float32),
        "W2": np.asarray(inputs["L0_ffn_W2"], np.float32),
        "b2": np.asarray(inputs["L0_ffn_b2"], np.float32),
        "fscale": np.asarray(inputs["L0_ffn_scale"], np.float32).reshape(1, 1),
        "eyebf": eyec,
    }
    return [dict(shared, x=np.ascontiguousarray(x[bb, :T, :])) for bb in range(nB)]


def kernel(**inputs):
    T = 2048
    nc = build_program(T=T, nsub=4)
    in_maps = make_inmaps(inputs, T=T)
    res = run_bass_kernel_spmd(nc, in_maps, core_ids=list(range(len(in_maps))))
    return np.stack([np.asarray(r["out"], dtype=np.float32) for r in res.results], axis=0)


# version 34 (final)
# speedup vs baseline: 1.0162x; 1.0054x over previous
import numpy as np
import contextlib
import ml_dtypes
import concourse.bass as bass
import concourse.mybir as mybir
from concourse.bass_utils import run_bass_kernel_spmd

F32 = mybir.dt.float32
BF16 = mybir.dt.bfloat16
AF = mybir.ActivationFunctionType
ALU = mybir.AluOpType
AX = mybir.AxisListType

D = 1024
ED = 2048
NST = 16
KC = 15
RDT = 64
HF = 4096
EPS = 1e-5

ENGS = ("sync", "vector", "scalar", "tensor", "gpsimd")
ATTN_STAGE = 4


class Sem:
    __slots__ = ("h", "n")

    def __init__(self, h):
        self.h = h
        self.n = 0


class Tok:
    __slots__ = ("q", "box")

    def __init__(self, q, box):
        self.q = q
        self.box = box

    def inc(self, sem, k=1):
        sem.n += k
        box = self.box
        self.q.rec.append(lambda e: box["i"].then_inc(sem.h, k))
        return self


class Q:
    def __init__(self, name):
        self.name = name
        self.rec = []

    def op(self, fname, *a, **kw):
        box = {}

        def fn(e, a=a, kw=kw, fname=fname, box=box):
            box["i"] = getattr(e, fname)(*a, **kw)

        self.rec.append(fn)
        return Tok(self, box)

    def __getattr__(self, fname):
        if fname.startswith("_"):
            raise AttributeError(fname)
        return lambda *a, **kw: self.op(fname, *a, **kw)

    def wait(self, sem, n=None):
        v = sem.n if n is None else n
        if v > 0:
            self.rec.append(lambda e, v=v: e.wait_ge(sem.h, v))

    def dr(self):
        self.rec.append(lambda e: e.drain())


class B:
    def __init__(self, nc, es):
        self.nc = nc
        self.es = es
        self.q = {n: Q(n) for n in ENGS}
        self.sy = self.q["sync"]
        self.ve = self.q["vector"]
        self.ac = self.q["scalar"]
        self.te = self.q["tensor"]
        self.gp = self.q["gpsimd"]
        self._semi = 0

    def sem(self, name=None):
        self._semi += 1
        return Sem(self.es.enter_context(self.nc.semaphore(name or "s%d" % self._semi)))

    def sb(self, name, shape, dt):
        return self.es.enter_context(self.nc.sbuf_tensor(name, list(shape), dt))

    def finalize(self):
        with self.nc.Block() as block:
            for name in ENGS:
                rec = self.q[name].rec

                def body(e, rec=rec):
                    for fn in rec:
                        fn(e)

                getattr(block, name)(body)


def build_program(T=2048, nsub=4):
    NT = T // 128
    TH = T // 2
    NTH = NT // 2
    nc = bass.Bass("TRN2", target_bir_lowering=False)

    def dram(name, shape, dt=F32, kind="ExternalInput"):
        return nc.dram_tensor(name, list(shape), dt, kind=kind).ap()

    hx = dram("x", [T, D])
    h_gamma = dram("gamma", [D]); h_beta = dram("beta", [D])
    h_Win = dram("Win", [2 * ED, D]); h_bin = dram("bin", [2 * ED])
    h_convw = dram("convw", [ED, KC]); h_convb = dram("convb", [ED])
    h_Wx = dram("Wxm", [RDT + 2 * NST, ED])
    h_Wdt = dram("Wdt", [ED, RDT]); h_bdt = dram("bdt", [ED])
    h_Alog = dram("Alog", [ED, NST]); h_Dp = dram("Dp", [ED])
    h_Wout = dram("Wout", [D, ED]); h_bout = dram("bout", [D])
    h_Wq = dram("Wq", [D, D]); h_bq = dram("bq", [D])
    h_Wk = dram("Wk", [D, D]); h_bk = dram("bk", [D])
    h_Wv = dram("Wv", [D, D]); h_bv = dram("bv", [D])
    h_W1 = dram("W1", [HF, D]); h_b1 = dram("b1", [HF])
    h_W2 = dram("W2", [D, HF]); h_b2 = dram("b2", [D])
    h_scale = dram("fscale", [1, 1])
    h_eye = dram("eyebf", [128, 128], BF16)
    h_out = dram("out", [T, D], kind="ExternalOutput")
    s_x = dram("s_x", [T, D], kind="Internal")
    s_WinT = dram("s_WinT", [32, 128, 1024], BF16, kind="Internal")
    s_W1T = dram("s_W1T", [32, 128, 1024], BF16, kind="Internal")
    s_W2T = dram("s_W2T", [32, 128, 1024], BF16, kind="Internal")
    s_WoutT = dram("s_WoutT", [16, 128, 1024], BF16, kind="Internal")
    s_rows = dram("s_rows", [32, T], BF16, kind="Internal")
    s_yfc = dram("s_yfc", [16, 128, T // 4], BF16, kind="Internal")
    s_brow = dram("s_brow", [1, 2 * ED], kind="Internal")

    es = contextlib.ExitStack()
    b = B(nc, es)
    sy, ve, ac, te, gp = b.sy, b.ve, b.ac, b.te, b.gp

    lx = b.sem("lx"); lg = b.sem("lg"); st = b.sem("st")
    pv = b.sem("pv")
    s_dmm = b.sem("sdmm"); s_sp = b.sem("ssp"); s_dl = b.sem("sdl")
    s_dx = b.sem("sdx"); s_dA = b.sem("sdA"); s_dB = b.sem("sdB")
    s_sc = b.sem("ssc"); s_g = b.sem("sg"); s_ym = b.sem("sym")
    s_zm = b.sem("szm"); s_sz = b.sem("ssz"); s_yf = b.sem("syf")
    s_wz0 = b.sem("swz0"); s_wz1 = b.sem("swz1"); s_ev = b.sem("sev")
    s_wzp = (s_wz0, s_wz1)
    s_wzz0 = b.sem("swzz0"); s_wzz1 = b.sem("swzz1"); s_wzz = (s_wzz0, s_wzz1)
    lxr0 = b.sem("lxr0"); lxr1 = b.sem("lxr1"); lxrp = (lxr0, lxr1)
    s_q1 = b.sem("sq1"); s_q2 = b.sem("sq2"); s_q3 = b.sem("sq3")
    s_q4 = b.sem("sq4"); s_q5 = b.sem("sq5")
    l_tp = b.sem("ltp_"); l_ap = b.sem("lap_"); l_ev = b.sem("lev_")
    l_wi = b.sem("lwi"); l_cv = b.sem("lcv"); l_dg = b.sem("ldg")
    l_xin = b.sem("lxin"); l_tl = b.sem("ltl"); l_cm = b.sem("lcm")
    l_om = b.sem("lom"); l_ox = b.sem("lox"); l_wo = b.sem("lwo")
    l_w2 = b.sem("lw2"); l_lt = b.sem("llt"); l_tr = b.sem("ltr")
    l_u = b.sem("lu"); l_h = b.sem("lh"); l_v = b.sem("lv"); l_o = b.sem("lo")
    l_qk = b.sem("lqk"); l_qe = b.sem("lqe"); l_vm = b.sem("lvm")
    l_vv = b.sem("lvv"); l_sc = b.sem("lsc"); l_sm = b.sem("lsm")
    l_ex = b.sem("lex"); l_pn = b.sem("lpn"); l_pt = b.sem("lpt")
    l_tt = b.sem("ltt"); l_av = b.sem("lav")

    eye = b.sb("eye", [128, 128], BF16)
    A_fm = b.sb("A_fm", [128, 16, NST], F32)
    convw_fm = b.sb("convw_fm", [128, 16, KC], F32)
    cb_fm = b.sb("cb_fm", [128, 16], F32)
    Dp_fm = b.sb("Dp_fm", [128, 16], F32)
    bin_fm = b.sb("bin_fm", [128, 32], F32)
    bo_bc = b.sb("bo_bc", [128, D], F32)
    b2_bc = b.sb("b2_bc", [128, D], F32)
    bv_bc = b.sb("bv_bc", [128, D], F32)
    b1_fm = b.sb("b1_fm", [128, 32], F32)
    bq_fm = b.sb("bq_fm", [128, 8], F32)
    bk_fm = b.sb("bk_fm", [128, 8], F32)
    gam_fm = b.sb("gam_fm", [128, 8], F32)
    scale_sb = b.sb("scale_sb", [1, 1], F32)
    bdtrow_fm = b.sb("bdt_fm", [128, 16], F32)
    carry = b.sb("carry", [128, 256], F32)
    stat6 = b.sb("stat6", [128, 12], F32)
    mv2 = b.sb("mv2", [128, 2], F32)
    mu1 = b.sb("mu1", [128, 16], F32)
    ss1 = b.sb("ss1", [128, 16], F32)
    r1a = b.sb("r1a", [128, 16], F32)
    tmp1 = b.sb("tmp1", [128, 1], F32)
    xres = b.sb("xres", [128, 2, D], F32)
    xntok = b.sb("xntok", [128, 2, D], BF16)
    xr_last = {0: None, 1: None}

    def xr_load(sl, src_slice):
        if xr_last[sl] is not None:
            sem, v = xr_last[sl]
            sy.wait(sem, v)
        sy.wait(st, st.n)
        sy.dma_start(out=xres[:, sl, :], in_=src_slice).inc(lxrp[sl], 16)

    def xr_used(sl, sem):
        xr_last[sl] = (sem, sem.n)
    WxT = b.sb("WxT", [128, 16 * 96], BF16)
    WdtT = b.sb("WdtT", [64, ED], BF16)

    sy.dma_start(out=eye[:, :], in_=h_eye).inc(lx, 16)
    for dst, src in [(cb_fm, h_convb), (bdtrow_fm, h_bdt), (Dp_fm, h_Dp),
                     (b1_fm, h_b1), (bq_fm, h_bq), (bk_fm, h_bk),
                     (gam_fm, h_gamma)]:
        sy.dma_start(out=dst[:, :], in_=src.rearrange("(j p) -> p j", p=128), allow_slow_non_contiguous=True).inc(lx, 16)
    sy.dma_start(out=A_fm[:, :, :], in_=h_Alog.rearrange("(c p) n -> p c n", p=128)).inc(lx, 16)
    sy.dma_start(out=convw_fm[:, :, :], in_=h_convw.rearrange("(c p) k -> p c k", p=128)).inc(lx, 16)
    sy.dma_start(out=bo_bc[:, :], in_=h_bout.rearrange("(o d) -> o d", o=1).broadcast_to((128, D))).inc(lx, 16)
    sy.dma_start(out=b2_bc[:, :], in_=h_b2.rearrange("(o d) -> o d", o=1).broadcast_to((128, D))).inc(lx, 16)
    sy.dma_start(out=bv_bc[:, :], in_=h_bv.rearrange("(o d) -> o d", o=1).broadcast_to((128, D))).inc(lx, 16)
    sy.dma_start(out=scale_sb[:, :], in_=h_scale).inc(lx, 16)

    ac.wait(lx)
    ac.activation(out=A_fm[:, :, :], in_=A_fm[:, :, :], func=AF.Exp)
    ac.dr()
    ac.op("mul", A_fm[:, :, :], A_fm[:, :, :], -1.0)
    ve.wait(lx)
    ve.memset(carry[:, :], 0.0)
    ve.dr()

    xr_load(0, hx[0:128, :])
    xr_load(1, hx[128:256, :])
    PREP1 = {}
    with contextlib.ExitStack() as prep:
        pb = lambda n, s, d: prep.enter_context(nc.sbuf_tensor(n, list(s), d))
        pps = lambda n, s, d: prep.enter_context(nc.psum_tensor(n, list(s), d))
        wnat = pb("wnat", [128, 2, 1024], BF16)
        wnatB = pb("wnatB", [128, 8, 4096], BF16)
        wstage = pb("wstage", [128, 2, 1024], BF16)
        beta_bf = pb("beta_bf", [128, 8], BF16)
        binrow = pb("binrow", [1, 2 * ED], F32)
        p_tp = pps("p_tp", [128, 2, 1024], BF16)
        p_br = pps("p_br", [1, 2048], F32)
        ltp0 = b.sem("ltp0"); ltp1 = b.sem("ltp1"); ltpp = (ltp0, ltp1)
        lnb = b.sem("lnb"); lnc = b.sem("lnc")
        lbr = b.sem("lbr")
        ttp = b.sem("ttp"); etp = b.sem("etp")
        wtp0 = b.sem("wtp0"); wtp1 = b.sem("wtp1"); wtpp = (wtp0, wtp1)
        btp = b.sem("btp")

        sy.dma_start(out=binrow[:, :], in_=h_bin.rearrange("(o f) -> o f", o=1)).inc(lbr, 16)
        gp.wait(lx)
        gp.dma_start(out=beta_bf[:, :], in_=h_beta.rearrange("(j p) -> p j", p=128), allow_slow_non_contiguous=True).inc(lnb, 16)

        P1TT = []; P1ET = []; P1WB = []; P1BT = []

        def prep_core(i, src_slice, cache_row, beta=False):
            sl = i % 2
            if i >= 2:
                gp.wait(ttp, P1TT[i - 2])
            gp.dma_start(out=wnat[:, sl, :], in_=src_slice).inc(ltpp[sl], 16)
            te.wait(ltpp[sl])
            if i >= 2:
                te.wait(etp, P1ET[i - 2])
            tok = None
            for k in range(8):
                tok = te.transpose(out=p_tp[:, sl, k * 128:(k + 1) * 128],
                                   in_=wnat[:, sl, k * 128:(k + 1) * 128],
                                   identity=eye[:, :])
            tok.inc(ttp, 8)
            P1TT.append(ttp.n)
            ac.wait(ttp, P1TT[i])
            if i >= 2:
                ac.wait(wtpp[sl], P1WB[i - 2][1])
                if beta:
                    ac.wait(btp, P1BT[i - 2])
            ac.activation(out=wstage[:, sl, :], in_=p_tp[:, sl, :], func=AF.Copy).inc(etp, 1)
            P1ET.append(etp.n)
            if beta:
                te.wait(etp, P1ET[i])
                if i == 0:
                    te.wait(lnb, 16)
                tok = None
                for k in range(8):
                    tok = te.matmul(out=p_br[0:1, (i % 16) * 128:(i % 16) * 128 + 128],
                                    lhsT=beta_bf[:, k:k + 1],
                                    rhs=wstage[:, sl, k * 128:(k + 1) * 128],
                                    start=((i % 4 == 0) and k == 0),
                                    stop=((i % 4 == 3) and k == 7))
                tok.inc(btp, 1)
                if i % 16 == 15:
                    ve.wait(btp, btp.n)
                    ve.wait(lbr)
                    half = i // 16
                    ve.tensor_add(out=binrow[0:1, half * 2048:(half + 1) * 2048],
                                  in0=binrow[0:1, half * 2048:(half + 1) * 2048],
                                  in1=p_br[0:1, :]).inc(pv, 1)
                    te.wait(pv, pv.n)
            P1BT.append(btp.n)
            sy.wait(etp, P1ET[i])
            sy.dma_start(out=cache_row, in_=wstage[:, sl, :]).inc(wtpp[sl], 16)
            P1WB.append((wtpp[sl], wtpp[sl].n))

        for i in range(32):
            prep_core(i, h_Win[i * 128:(i + 1) * 128, :], s_WinT[i, :, :], beta=True)

        def prepB(src_h, nj, ncols, cache):
            for k in range(8):
                gp.dma_start(out=wnatB[:, k, 0:ncols],
                             in_=src_h[k * 128:(k + 1) * 128, :]).inc(lnb, 16)
            te.wait(lnb, lnb.n)
            for j in range(nj):
                i = len(P1TT)
                sl = i % 2
                te.wait(etp, P1ET[i - 2])
                tok = None
                for k in range(8):
                    tok = te.transpose(out=p_tp[:, sl, k * 128:(k + 1) * 128],
                                       in_=wnatB[:, k, j * 128:(j + 1) * 128],
                                       identity=eye[:, :])
                tok.inc(ttp, 8)
                P1TT.append(ttp.n)
                ac.wait(ttp, P1TT[i])
                ac.wait(wtpp[sl], P1WB[i - 2][1])
                if i - 2 < len(P1BT):
                    ac.wait(btp, P1BT[min(i - 2, len(P1BT) - 1)])
                ac.activation(out=wstage[:, sl, :], in_=p_tp[:, sl, :], func=AF.Copy).inc(etp, 1)
                P1ET.append(etp.n)
                sy.wait(etp, P1ET[i])
                sy.dma_start(out=cache[j, :, :], in_=wstage[:, sl, :]).inc(wtpp[sl], 16)
                P1WB.append((wtpp[sl], wtpp[sl].n))

        prepB(h_Wout, 16, 2048, s_WoutT)

        sy.wait(pv)
        sy.dma_start(out=s_brow, in_=binrow[0:1, :]).inc(st, 16)
        sy.wait(st)
        sy.dma_start(out=bin_fm[:, :], in_=s_brow.rearrange("o (j p) -> p (o j)", p=128), allow_slow_non_contiguous=True).inc(lx, 16)

        wxnat = pb("wxnat", [96, ED], BF16)
        wdtnat = pb("wdtnat", [128, 16, RDT], BF16)
        gp.dma_start(out=wxnat[:, :], in_=h_Wx).inc(lnc, 16)
        gp.dma_start(out=wdtnat[:, :, :],
                     in_=h_Wdt.rearrange("(c p) r -> p c r", p=128)).inc(lnc, 16)
        te.wait(lnc)
        for grp in range(2):
            te.wait(etp)
            tok = None
            for cc in range(8):
                c = grp * 8 + cc
                tok = te.transpose(out=p_tp[:, 0, cc * 128:cc * 128 + 96],
                                   in_=wxnat[0:96, c * 128:(c + 1) * 128],
                                   identity=eye[0:96, 0:96])
            tok.inc(ttp, 8)
            ac.wait(ttp)
            tok = None
            for cc in range(8):
                c = grp * 8 + cc
                tok = ac.activation(out=WxT[:, c * 96:(c + 1) * 96],
                                    in_=p_tp[:, 0, cc * 128:cc * 128 + 96],
                                    func=AF.Copy)
            tok.inc(etp, 1)
        for grp in range(2):
            te.wait(etp)
            tok = None
            for cc in range(8):
                c = grp * 8 + cc
                tok = te.transpose(out=p_tp[0:64, 0, cc * 128:(cc + 1) * 128],
                                   in_=wdtnat[:, c, :],
                                   identity=eye[:, :])
            tok.inc(ttp, 8)
            ac.wait(ttp)
            tok = ac.activation(out=WdtT[:, grp * 1024:(grp + 1) * 1024],
                                in_=p_tp[0:64, 0, :], func=AF.Copy)
            tok.inc(etp, 1)
        PREP1.update({"etp": (etp, etp.n), "ttp": (ttp, ttp.n),
                      "btp": (btp, btp.n), "pv": (pv, pv.n),
                      "win": P1WB[0:32], "wout": P1WB[32:48]})


    def emit_prep2():
        with contextlib.ExitStack() as prep:
            pb = lambda n, s, d: prep.enter_context(nc.sbuf_tensor(n, list(s), d))
            pps = lambda n, s, d: prep.enter_context(nc.psum_tensor(n, list(s), d))
            wnat = pb("p2_wnat", [128, 2, 1024], BF16)
            wnatB = pb("p2_wnatB", [128, 8, 4096], BF16)
            wstage = pb("p2_wstage", [128, 2, 1024], BF16)
            p_tp = pps("p2_ptp", [128, 2, 1024], BF16)
            ltp0 = b.sem("p2ltp0"); ltp1 = b.sem("p2ltp1"); ltpp = (ltp0, ltp1)
            lnb = b.sem("p2lnb")
            ttp = b.sem("p2ttp"); etp = b.sem("p2etp")
            wtp0 = b.sem("p2wtp0"); wtp1 = b.sem("p2wtp1"); wtpp = (wtp0, wtp1)
            tt_toks = []; et_toks = []; wb_toks = []

            def prep_row(ci, in_tile, cache_row):
                sl = ci % 2
                if ci >= 2:
                    te.wait(etp, et_toks[ci - 2])
                tok = None
                for k in range(8):
                    tok = te.transpose(out=p_tp[:, sl, k * 128:(k + 1) * 128],
                                       in_=in_tile[k],
                                       identity=eye[:, :])
                tok.inc(ttp, 8)
                tt_toks.append(ttp.n)
                ac.wait(ttp, tt_toks[ci])
                if ci >= 2:
                    ac.wait(wtpp[sl], wb_toks[ci - 2][1])
                ac.activation(out=wstage[:, sl, :], in_=p_tp[:, sl, :],
                              func=AF.Copy).inc(etp, 1)
                et_toks.append(etp.n)
                sy.wait(etp, et_toks[ci])
                sy.dma_start(out=cache_row, in_=wstage[:, sl, :]).inc(wtpp[sl], 16)
                wb_toks.append((wtpp[sl], wtpp[sl].n))

            ci = 0
            for i in range(32):
                sl = i % 2
                if i >= 2:
                    gp.wait(ttp, tt_toks[i - 2])
                gp.dma_start(out=wnat[:, sl, :],
                             in_=h_W1[i * 128:(i + 1) * 128, :]).inc(ltpp[sl], 16)
                te.wait(ltpp[sl])
                prep_row(ci, [wnat[:, sl, k * 128:(k + 1) * 128] for k in range(8)],
                         s_W1T[i, :, :])
                ci += 1
            for k in range(8):
                gp.dma_start(out=wnatB[:, k, :],
                             in_=h_W2[k * 128:(k + 1) * 128, :]).inc(lnb, 16)
            te.wait(lnb, lnb.n)
            for j in range(32):
                prep_row(ci, [wnatB[:, k, j * 128:(j + 1) * 128] for k in range(8)],
                         s_W2T[j, :, :])
                ci += 1
        return (wb_toks[0:32], wb_toks[32:64], (ttp, ttp.n))

    def emit_ln(xsrc, tiles, xn_fm, kind, lnps_bf, preloaded=0):
        for j, i in enumerate(tiles):
            sl = j % 2
            if j >= preloaded:
                xr_load(sl, xsrc[i * 128:(i + 1) * 128, :])
            ve.wait(lxrp[sl])
            ve.bn_stats(out=stat6[:, 0:6], in_=xres[:, sl, 0:512])
            ve.bn_stats(out=stat6[:, 6:12], in_=xres[:, sl, 512:1024])
            ve.dr()
            ve.bn_aggr(out=mv2[:, :], in_=stat6[:, :])
            ve.dr()
            ve.tensor_copy(out=mu1[:, i:i + 1], in_=mv2[:, 0:1])
            if kind == "mamba":
                ve.tensor_scalar(out=tmp1[:, 0:1], in0=mv2[:, 1:2], scalar1=1.0 + EPS,
                                 scalar2=EPS * EPS, op0=ALU.mult, op1=ALU.add)
                ve.dr()
                ve.reciprocal(out=tmp1[:, 0:1], in_=tmp1[:, 0:1])
                ve.dr()
                ve.tensor_scalar(out=r1a[:, i:i + 1], in0=mv2[:, 1:2], scalar1=EPS,
                                 scalar2=None, op0=ALU.add)
                ve.dr()
                ve.reciprocal(out=r1a[:, i:i + 1], in_=r1a[:, i:i + 1]).inc(s_q1, 1)
                ac.wait(s_q1)
                ac.op("sqrt", ss1[:, i:i + 1], tmp1[:, 0:1])
                ac.dr()
                ac.op("sqrt", r1a[:, i:i + 1], r1a[:, i:i + 1]).inc(s_q2, 1)
            else:
                ve.tensor_scalar(out=tmp1[:, 0:1], in0=mv2[:, 1:2], scalar1=EPS,
                                 scalar2=None, op0=ALU.add)
                ve.dr()
                ve.reciprocal(out=tmp1[:, 0:1], in_=tmp1[:, 0:1]).inc(s_q1, 1)
                ac.wait(s_q1)
                ac.op("sqrt", ss1[:, i:i + 1], tmp1[:, 0:1]).inc(s_q2, 1)
            ve.wait(s_q2)
            if j >= 2:
                ve.wait(l_tp, l_tp.n - 8)
            ve.tensor_scalar(out=xntok[:, sl, :], in0=xres[:, sl, :],
                             scalar1=mu1[:, i:i + 1],
                             scalar2=ss1[:, i:i + 1],
                             op0=ALU.subtract, op1=ALU.mult).inc(l_ap, 1)
            xr_used(sl, l_ap)
            te.wait(l_ap)
            te.wait(l_ev, l_ev.n - (0 if j >= 1 else 0))
            if j == 0:
                te.wait(l_cv)
                te.wait(l_ox, l_ox.n)
                te.wait(s_q1, s_q1.n)
                te.wait(PREP1["etp"][0], PREP1["etp"][1])
                te.wait(PREP1["pv"][0], PREP1["pv"][1])
                te.wait(lx, lx.n); ve.wait(lx, lx.n); ac.wait(lx, lx.n)
            tok = None
            for k in range(8):
                tok = te.transpose(out=lnps_bf[:, k * 128:(k + 1) * 128],
                                   in_=xntok[:, sl, k * 128:(k + 1) * 128],
                                   identity=eye[:, :])
            tok.inc(l_tp, 8)
            ac.wait(l_tp)
            if kind == "mamba":
                tok = None
                for k in range(8):
                    tok = ac.activation(out=xn_fm[:, k, j * 128:(j + 1) * 128],
                                        in_=lnps_bf[:, k * 128:(k + 1) * 128],
                                        func=AF.Copy, scale=gam_fm[:, k:k + 1])
                tok.inc(l_ev, 1)
            else:
                ac.activation(out=xn_fm[:, :, j * 128:(j + 1) * 128],
                              in_=lnps_bf.rearrange("p (k c) -> p k c", k=8),
                              func=AF.Copy).inc(l_ev, 1)
        return l_ev

    def barrier(sem):
        for qq in (ve, te, ac, gp, sy):
            qq.wait(sem)

    def emit_mamba():
        TQ = T // 4
        NTQ = NT // 4
        tch = [(s0, min(512, TQ - s0)) for s0 in range(0, TQ, 512)]
        with contextlib.ExitStack() as ms:
            mb = lambda n, s, d: ms.enter_context(nc.sbuf_tensor(n, list(s), d))
            xn_fm = mb("m_xn", [128, 2, 8, TQ], BF16)
            xc = mb("m_xc", [128, 2, 16, TQ], BF16)
            tail = mb("m_tail", [128, 16, 14], BF16)
            dtl = mb("m_dtl", [64, 2, TQ], BF16)
            bcB = mb("m_bcB", [128, 16, TQ], BF16)
            bcC = mb("m_bcC", [128, 16, TQ], BF16)
            delta = mb("m_delta", [128, 2, TQ], F32)
            dxb = mb("m_dxb", [128, TQ], BF16)
            dAb = mb("m_dA", [128, 2, TQ], F32)
            dBxall = mb("m_dBall", [128, 16, TQ], BF16)
            hall = mb("m_hall", [128, 16, TQ], BF16)
            gall = mb("m_gall", [128, 16, TQ], BF16)
            szb = mb("m_sz", [128, 3, TQ], BF16)
            txd = mb("m_txd", [128, D], F32)
            yfull = mb("m_yf", [128, 2, TQ], BF16)
            yfr = mb("m_yfr", [128, 2, TQ], BF16)
            xinp = mb("m_xinp", [128, 2, 14 + TQ], BF16)
            diag = mb("m_diag", [128, KC, 128], BF16)
            bcrows = mb("m_bcr", [32, TQ], BF16)
            wz = mb("m_wz", [128, 2, 1024], BF16)
            wzz = mb("m_wzz", [128, 2, 1024], BF16)
            woutt = mb("m_wouta", [128, 2, 1024], BF16)
            pA = ms.enter_context(nc.psum_tensor("m_pA", [128, 4, 512], F32))
            pB = ms.enter_context(nc.psum_tensor("m_pB", [128, 4, 512], F32))
            lnps_bf = pB[:, 2:4, :].rearrange("p a b -> p (a b)").bitcast(BF16)[:, 0:1024]
            l_fx = b.sem("lfx"); l_yfd0 = b.sem("lyfd0"); l_yfd1 = b.sem("lyfd1")
            yfdp = (l_yfd0, l_yfd1)
            wz_last = {0: None, 1: None}
            wzz_last = {0: None, 1: None}
            EPI_YM = {}; EPI_SZ = {}

            gwz0 = b.sem("gwz0"); gwz1 = b.sem("gwz1"); gwzp = (gwz0, gwz1)

            def wz_load(sl, src_row, pre=None):
                if wz_last[sl] is not None:
                    sem, v = wz_last[sl]
                    gp.wait(sem, v)
                if pre is not None:
                    for t in pre:
                        gp.wait(t[0], t[1])
                gp.dma_start(out=wz[:, sl, :], in_=src_row).inc(gwzp[sl], 16)

            ve.wait(PREP1["ttp"][0], PREP1["ttp"][1])
            ve.memset(tail[:, :, :], 0.0)
            ve.dr()

            FT = {}

            def front_A(h, c):
                hh = h % 2
                sl = c % 2
                pre = None
                if h == 0:
                    pre = [PREP1["win"][c]]
                    if c < 2:
                        pre.append(PREP1["ttp"])
                wz_load(sl, s_WinT[c, :, :], pre=pre)
                te.wait(gwzp[sl])
                if c >= 2:
                    te.wait(l_xin, FT[("xin", h, c - 2)])
                elif ("xin", h - 1, c + 14) in FT:
                    te.wait(l_xin, FT[("xin", h - 1, c + 14)])
                if c == 0:
                    te.wait(l_ev, l_ev.n)
                    te.wait(l_ox, l_ox.n)
                tok = None
                for k in range(8):
                    tok = te.matmul(out=pA[:, 2 + sl, 0:TQ],
                                    lhsT=wz[:, sl, k * 128:(k + 1) * 128],
                                    rhs=xn_fm[:, hh, k, 0:TQ],
                                    start=(k == 0), stop=(k == 7))
                tok.inc(l_wi, 1)
                FT[("wi", h, c)] = l_wi.n
                wz_last[sl] = (l_wi, l_wi.n)

            def front_B(h, c):
                sl = c % 2
                ac.wait(l_wi, FT[("wi", h, c)])
                if c >= 2:
                    ac.wait(l_cm, FT[("cm", h, c - 2)])
                elif ("cm", h - 1, c + 14) in FT:
                    ac.wait(l_cm, FT[("cm", h - 1, c + 14)])
                ac.activation(out=xinp[:, sl, 14:14 + TQ], in_=pA[:, 2 + sl, 0:TQ],
                              func=AF.Identity, bias=bin_fm[:, c:c + 1]).inc(l_xin, 1)
                FT[("xin", h, c)] = l_xin.n
                ac.activation(out=xinp[:, sl, 0:14], in_=tail[:, c, :], func=AF.Copy)
                ac.dr()
                ac.activation(out=tail[:, c, :], in_=xinp[:, sl, TQ:TQ + 14],
                              func=AF.Copy).inc(l_tl, 1)
                FT[("tl", h, c)] = l_tl.n
                if c >= 1:
                    gp.wait(l_cm, FT[("cm", h, c - 1)])
                elif ("cm", h - 1, 15) in FT:
                    gp.wait(l_cm, FT[("cm", h - 1, 15)])
                if h == 0 and c == 0:
                    gp.wait(PREP1["ttp"][0], PREP1["ttp"][1])
                    gp.wait(lx, lx.n)
                gp.tensor_mul(out=diag[:, :, :],
                              in0=convw_fm[:, c, :].rearrange("p (k o) -> p k o", o=1).to_broadcast((128, KC, 128)),
                              in1=eye[:, :].rearrange("p (o j) -> p o j", o=1).to_broadcast((128, KC, 128))).inc(l_dg, 1)
                FT[("dg", h, c)] = l_dg.n
                te.wait(l_dg, FT[("dg", h, c)])
                te.wait(l_tl, FT[("tl", h, c)])
                if c >= 2:
                    te.wait(l_cv, FT[("cv", h, c - 2)])
                elif ("cv", h - 1, c + 14) in FT:
                    te.wait(l_cv, FT[("cv", h - 1, c + 14)])
                tok = None
                for k in range(KC):
                    tok = te.matmul(out=pB[:, 2 + sl, 0:TQ],
                                    lhsT=diag[:, k, :],
                                    rhs=xinp[:, sl, k:k + TQ],
                                    start=(k == 0), stop=(k == KC - 1))
                tok.inc(l_cm, 1)
                FT[("cm", h, c)] = l_cm.n

            def front_C(h, c):
                hh = h % 2
                sl = c % 2
                ac.wait(l_cm, FT[("cm", h, c)])
                if h >= 2:
                    ac.wait(s_dx, 16 * (h - 1))
                ac.activation(out=xc[:, hh, c, :], in_=pB[:, 2 + sl, 0:TQ], func=AF.Silu,
                              bias=cb_fm[:, c:c + 1]).inc(l_cv, 1)
                FT[("cv", h, c)] = l_cv.n

            def front_staged(h, c):
                if c < 16:
                    front_A(h, c)
                if 1 <= c <= 16:
                    front_B(h, c - 1)
                if c >= 2:
                    front_C(h, c - 2)

            BC_TOK = {}

            def front_tail(h):
                hh = h % 2
                if h == 0:
                    te.wait(PREP1["etp"][0], PREP1["etp"][1])
                te.wait(l_cv, l_cv.n)
                te.wait(l_fx, l_fx.n)
                tok = None
                for k in range(16):
                    tok = te.matmul(out=pA[0:96, 2, 0:TQ],
                                    lhsT=WxT[:, k * 96:(k + 1) * 96],
                                    rhs=xc[:, hh, k, 0:TQ],
                                    start=(k == 0), stop=(k == 15))
                tok.inc(s_q4, 1)
                ac.wait(s_q4)
                ac.activation(out=dtl[:, hh, :], in_=pA[0:64, 2, 0:TQ], func=AF.Copy)
                ac.dr()
                ac.activation(out=bcrows[:, :], in_=pA[64:96, 2, 0:TQ], func=AF.Copy).inc(l_fx, 1)
                sy.wait(l_fx)
                sy.dma_start(out=s_rows[:, h * TQ:(h + 1) * TQ], in_=bcrows[:, :]).inc(s_ev, 16)
                gp.wait(s_ev, s_ev.n)
                gp.wait(s_dB, s_dB.n)
                gp.wait(s_g, s_g.n)
                if h == 0:
                    gp.wait(PREP1["wout"][14][0], PREP1["wout"][14][1])
                    gp.wait(PREP1["wout"][15][0], PREP1["wout"][15][1])
                for n in range(16):
                    gp.dma_start(out=bcB[:, n, :],
                                 in_=s_rows[n:n + 1, h * TQ:(h + 1) * TQ].broadcast_to((128, TQ))).inc(lg, 16)
                    gp.dma_start(out=bcC[:, n, :],
                                 in_=s_rows[16 + n:17 + n, h * TQ:(h + 1) * TQ].broadcast_to((128, TQ))).inc(lg, 16)
                BC_TOK[h] = lg.n

            def front(h):
                for c in range(18):
                    front_staged(h, c)
                front_tail(h)

            DTOK = {}

            def emit_delta(h, c):
                hh = h % 2
                sl = c % 2
                te.wait(s_sz, s_sz.n)
                if c == 0:
                    te.wait(l_fx, l_fx.n)
                    if h == 0:
                        te.wait(PREP1["etp"][0], PREP1["etp"][1])
                    if h >= 1:
                        te.wait(l_ox, l_ox.n)
                tok = te.matmul(out=pA[:, sl, 0:TQ],
                                lhsT=WdtT[:, c * 128:(c + 1) * 128],
                                rhs=dtl[:, hh, 0:TQ],
                                start=True, stop=True)
                tok.inc(s_dmm, 1)
                DTOK[c] = s_dmm.n

            def scan(h):
                hh = h % 2
                emit_delta(h, 0)
                for c in range(16):
                    sl = c % 2
                    ac.wait(s_dmm, DTOK[c])
                    ac.wait(s_dl, s_dl.n)
                    ac.activation(out=txd[:, 512:512 + TQ], in_=pA[:, sl, 0:TQ], func=AF.Abs)
                    ac.dr()
                    ac.activation(out=txd[:, 512:512 + TQ], in_=txd[:, 512:512 + TQ], func=AF.Exp, scale=-1.0)
                    ac.dr()
                    ac.activation(out=txd[:, 512:512 + TQ], in_=txd[:, 512:512 + TQ], func=AF.Ln, bias=1.0)
                    ac.dr()
                    ac.wait(s_dx, s_dx.n)
                    ac.activation(out=delta[:, sl, 0:TQ], in_=pA[:, sl, 0:TQ],
                                  func=AF.Relu).inc(s_sp, 1)
                    ve.wait(s_sp)
                    if c == 0:
                        ve.wait(lg, BC_TOK[h])
                    ve.tensor_add(out=delta[:, sl, :], in0=delta[:, sl, :],
                                  in1=txd[:, 512:512 + TQ]).inc(s_dl, 1)
                    if wzz_last[sl] is not None:
                        sy.wait(wzz_last[sl][0], wzz_last[sl][1])
                    if h == 0:
                        sy.wait(PREP1["win"][16 + c][0], PREP1["win"][16 + c][1])
                        if c == 0:
                            sy.wait(PREP1["btp"][0], PREP1["btp"][1])
                    sy.dma_start(out=wzz[:, sl, :], in_=s_WinT[16 + c, :, :]).inc(s_wzz[sl], 16)
                    te.wait(s_wzz[sl])
                    te.wait(s_sp)
                    tok = None
                    for k in range(8):
                        tok = te.matmul(out=pA[:, sl, 0:TQ],
                                        lhsT=wzz[:, sl, k * 128:(k + 1) * 128],
                                        rhs=xn_fm[:, hh, k, 0:TQ],
                                        start=(k == 0), stop=(k == 7))
                    tok.inc(s_zm, 1)
                    wzz_last[sl] = (s_zm, s_zm.n)
                    if c < 15:
                        emit_delta(h, c + 1)
                    ve.dr()
                    ve.tensor_mul(out=dxb[:, :], in0=delta[:, sl, :],
                                  in1=xc[:, hh, c, :]).inc(s_dx, 1)
                    ve.dr()
                    ve.tensor_mul(out=dBxall[:, :, :],
                                  in0=dxb[:, :].rearrange("p (o t) -> p o t", o=1).to_broadcast((128, 16, TQ)),
                                  in1=bcB[:, :, :]).inc(s_dB, 1)
                    ve.dr()
                    for n in range(16):
                        nn = n % 2
                        gidx = c * 16 + n
                        ac.wait(s_dl)
                        if gidx >= 2:
                            ac.wait(s_sc, s_sc.n - 1)
                        ac.activation(out=dAb[:, nn, :], in_=delta[:, sl, :], func=AF.Exp,
                                      scale=A_fm[:, c, n:n + 1]).inc(s_dA, 1)
                        ve.wait(s_dA)
                        ve.tensor_tensor_scan(out=hall[:, n, :], data0=dAb[:, nn, :],
                                              data1=dBxall[:, n, :],
                                              initial=carry[:, gidx % 256:gidx % 256 + 1],
                                              op0=ALU.mult, op1=ALU.add).inc(s_sc, 1)
                    ac.wait(s_zm, s_zm.n)
                    ac.wait(s_yf, s_yf.n)
                    ac.activation(out=szb[:, c % 3, 0:TQ], in_=pA[:, sl, 0:TQ], func=AF.Silu,
                                  bias=bin_fm[:, 16 + c:17 + c]).inc(s_sz, 1)
                    EPI_SZ[c] = s_sz.n
                    ve.dr()
                    ve.tensor_copy(out=carry[:, c * 16:c * 16 + 16],
                                   in_=hall[:, :, TQ - 1:TQ])
                    ve.wait(s_ym, s_ym.n)
                    ve.tensor_mul(out=gall[:, :, :], in0=hall[:, :, :],
                                  in1=bcC[:, :, :]).inc(s_g, 1)
                    te.wait(s_g)
                    te.wait(s_yf, s_yf.n - 1)
                    if c == 0 and h >= 1:
                        te.wait(l_ox, l_ox.n)
                    tok = None
                    for n in range(16):
                        tok = te.matmul(out=pB[:, sl, 0:TQ],
                                        lhsT=eye[:, :],
                                        rhs=gall[:, n, :],
                                        start=(n == 0), stop=(n == 15))
                    tok.inc(s_ym, 1)
                    EPI_YM[c] = s_ym.n
                    if c >= 1:
                        epilogue_ve(h, c - 1)
                    if c == 15:
                        epilogue_ve(h, 15)
                    if h < 3:
                        front_staged(h + 1, c)

            def epilogue_ve(h, c):
                hh = h % 2
                sl = c % 2
                ve.wait(s_ym, EPI_YM[c])
                ve.wait(s_sz, EPI_SZ[c])
                ve.dr()
                ve.scalar_tensor_tensor(out=txd[:, 0:TQ], in0=xc[:, hh, c, :],
                                        scalar=Dp_fm[:, c:c + 1], in1=pB[:, sl, 0:TQ],
                                        op0=ALU.mult, op1=ALU.add)
                ve.dr()
                ve.wait(yfdp[sl], yfdp[sl].n)
                ve.tensor_mul(out=yfull[:, sl, :], in0=txd[:, 0:TQ],
                              in1=szb[:, c % 3, 0:TQ]).inc(s_yf, 1)
                sy.wait(s_yf)
                sy.dma_start(out=s_yfc[c, :, :], in_=yfull[:, sl, :]).inc(yfdp[sl], 16)

            def wout(h):
                ptile = [pA[:, 0:2, :], pA[:, 2:4, :], pB[:, 0:2, :], pB[:, 2:4, :]]
                ptile = [p.rearrange("p a b -> p (a b)") for p in ptile]
                te.wait(s_yf)
                te.wait(s_sz); te.wait(s_sp)
                te.wait(l_xin); te.wait(l_cv); te.wait(l_ev); te.wait(l_fx)
                sy.wait(l_yfd0); sy.wait(l_yfd1)
                for k in range(16):
                    ksl = k % 2
                    if wz_last[ksl] is not None:
                        sem, v = wz_last[ksl]
                        sy.wait(sem, v)
                    if h == 0:
                        sy.wait(PREP1["wout"][k][0], PREP1["wout"][k][1])
                    sy.dma_start(out=woutt[:, ksl, :], in_=s_WoutT[k, :, :]).inc(s_wzp[ksl], 16)
                    sy.dma_start(out=yfr[:, ksl, :], in_=s_yfc[k, :, :]).inc(s_wzp[ksl], 16)
                    te.wait(s_wzp[ksl])
                    tok = None
                    for tt in range(NTQ):
                        for t0 in range(2):
                            tok = te.matmul(out=ptile[tt][:, t0 * 512:(t0 + 1) * 512],
                                            lhsT=yfr[:, ksl, tt * 128:(tt + 1) * 128],
                                            rhs=woutt[:, ksl, t0 * 512:(t0 + 1) * 512],
                                            start=(k == 0), stop=(k == 15))
                    tok.inc(l_om, 1)
                    wz_last[ksl] = (l_om, l_om.n)
                for tt in range(NTQ):
                    sl = tt % 2
                    i = h * NTQ + tt
                    xr_load(sl, hx[i * 128:(i + 1) * 128, :])
                    ve.wait(l_om); ve.wait(lxrp[sl])
                    ve.tensor_scalar(out=txd[:, :], in0=xres[:, sl, :],
                                     scalar1=mu1[:, i:i + 1],
                                     scalar2=r1a[:, i:i + 1],
                                     op0=ALU.subtract, op1=ALU.mult)
                    ve.dr()
                    ve.tensor_add(out=txd[:, :], in0=txd[:, :], in1=xres[:, sl, :])
                    ve.dr()
                    ve.tensor_add(out=txd[:, :], in0=txd[:, :], in1=bo_bc[:, :])
                    ve.dr()
                    ve.tensor_add(out=xres[:, sl, :], in0=txd[:, :],
                                  in1=ptile[tt][:, 0:1024]).inc(l_ox, 1)
                    xr_used(sl, l_ox)
                    sy.wait(l_ox)
                    sy.dma_start(out=s_x[i * 128:(i + 1) * 128, :],
                                 in_=xres[:, sl, :]).inc(st, 16)

            def ln_q(h):
                tiles = list(range(h * NTQ, (h + 1) * NTQ))
                return emit_ln(hx, tiles, xn_fm[:, h % 2, :, :], "mamba", lnps_bf,
                               preloaded=(2 if h == 0 else 0))

            l_ev0 = ln_q(0)
            barrier(l_ev0)
            front(0)
            for h in range(4):
                if h < 3:
                    ln_q(h + 1)
                scan(h)
                if h < 3:
                    front_staged(h + 1, 16)
                    front_staged(h + 1, 17)
                    front_tail(h + 1)
                wout(h)
            barrier(st)

    def emit_ffn(tag, is_last, prep_toks=None):
        THF = T // 4
        NTHF = NT // 4
        fch = [(s0, min(512, THF - s0)) for s0 in range(0, THF, 512)]
        with contextlib.ExitStack() as fs:
            fb = lambda n, s, d: fs.enter_context(nc.sbuf_tensor(n, list(s), d))
            xn_fm = fb("f_xn" + tag, [128, 2, 8, THF], BF16)
            hT = fb("f_h" + tag, [128, 32, THF], BF16)
            w2a = fb("f_w2" + tag, [128, 32, 1024], BF16)
            w1s = fb("f_w1" + tag, [128, 3, 1024], BF16)
            ltc = fb("f_ltc" + tag, [128, 8, 16], BF16)
            ev1 = fb("f_ev1" + tag, [1, 4096], BF16)
            ev4 = fb("f_ev4" + tag, [1, 4096], BF16)
            ufl = fb("f_ufl" + tag, [1, 4 * THF], BF16)
            vfl = fb("f_vfl" + tag, [1, 16 * THF], BF16)
            vsb = fb("f_vsb" + tag, [4, THF], BF16)
            usb = fb("f_usb" + tag, [4, THF], BF16)
            pA = fs.enter_context(nc.psum_tensor("f_pA" + tag, [128, 2, 1024], F32))
            pR = fs.enter_context(nc.psum_tensor("f_pR" + tag, [4, 1024], F32))
            pLN = fs.enter_context(nc.psum_tensor("f_pLN" + tag, [128, 1024], BF16))
            pRow = pLN[0:1, :]

            W1TOK = {}
            w1sem = (s_wzp[0], s_wzp[1], s_wzz[0])
            l_ev = emit_ln(s_x, list(range(NTHF)), xn_fm[:, 0, :, :], "plain", pLN[:, :])
            lntok = {0: l_ev.n}
            ptoks = prep_toks

            def load_w2a():
                if ptoks is not None:
                    gp.wait(ptoks[2][0], ptoks[2][1])
                    gp.wait(ptoks[1][30][0], ptoks[1][30][1])
                    gp.wait(ptoks[1][31][0], ptoks[1][31][1])
                for k in range(32):
                    gp.dma_start(out=w2a[:, k, :], in_=s_W2T[k, :, :]).inc(l_w2, 16)
            for h in range(4):
                xh = xn_fm[:, h % 2, :, :]
                te.wait(l_ev, lntok[h])
                if h == 0:
                    ve.wait(l_ev, lntok[0])
                    ve.tensor_copy(out=ltc[:, :, :], in_=xh[:, :, 1:17]).inc(l_lt, 1)
                    te.wait(l_lt)
                    for dstrow, toks in ((ev1, (0, 2, 4, 6)), (ev4, (8, 10, 12, 14))):
                        for ii, tk in enumerate(toks):
                            te.wait(s_q1)
                            tok = None
                            for k in range(8):
                                tok = te.transpose(out=pRow[0:1, k * 128:(k + 1) * 128],
                                                   in_=ltc[:, k, tk:tk + 1],
                                                   identity=eye[:, :])
                            tok.inc(l_tr, 1)
                            ac.wait(l_tr)
                            ac.activation(out=dstrow[0:1, ii * 1024:(ii + 1) * 1024],
                                          in_=pRow[0:1, :], func=AF.Copy,
                                          scale=scale_sb[0:1, 0:1]).inc(s_q1, 1)
                if h == 0:
                    te.wait(s_q1)
                te.wait(s_q2, s_q2.n)
                tok = None
                for k in range(8):
                    for (t0, tw) in fch:
                        tok = te.matmul(out=pR[:, t0:t0 + tw],
                                        lhsT=ltc[:, k, 1:8:2],
                                        rhs=xh[:, k, t0:t0 + tw],
                                        start=(k == 0), stop=(k == 7))
                tok.inc(l_u, 1)
                ac.wait(l_u)
                if h >= 1:
                    ac.wait(lg, lg.n)
                ac.activation(out=usb[:, :], in_=pR[0:4, 0:THF], func=AF.Copy).inc(s_q1, 1)
                gp.wait(s_q1)
                for ii in range(4):
                    gp.dma_start(out=ufl[0:1, ii * THF:(ii + 1) * THF],
                                 in_=usb[ii:ii + 1, :]).inc(lg, 16)
                if h == 0:
                    load_w2a()
                for c in range(32):
                    sl = c % 2
                    sl3 = c % 3
                    if c >= 3:
                        sy.wait(s_q3, W1TOK[c - 3])
                    if ptoks is not None and h == 0:
                        sy.wait(ptoks[0][c][0], ptoks[0][c][1])
                    sy.dma_start(out=w1s[:, sl3, :], in_=s_W1T[c, :, :]).inc(w1sem[sl3], 16)
                    te.wait(w1sem[sl3])
                    if c >= 2:
                        te.wait(l_h, l_h.n - 1)
                    tok = None
                    for k in range(8):
                        for (t0, tw) in fch:
                            tok = te.matmul(out=pA[:, sl, t0:t0 + tw],
                                            lhsT=w1s[:, sl3, k * 128:(k + 1) * 128],
                                            rhs=xh[:, k, t0:t0 + tw],
                                            start=(k == 0), stop=False)
                    if c == 0:
                        te.wait(lg, lg.n)
                    ii = c // 8
                    for ti, (t0, tw) in enumerate(fch):
                        tok = te.matmul(out=pA[:, sl, t0:t0 + tw],
                                        lhsT=ev1[0:1, ii * 1024 + (c % 8) * 128:ii * 1024 + (c % 8) * 128 + 128],
                                        rhs=ufl[0:1, ii * THF + t0:ii * THF + t0 + tw],
                                        start=False, stop=True)
                    tok.inc(s_q3, 1)
                    W1TOK[c] = s_q3.n
                    ac.wait(s_q3)
                    ac.activation(out=hT[:, c, :], in_=pA[:, sl, 0:THF], func=AF.Gelu,
                                  bias=b1_fm[:, c:c + 1]).inc(l_h, 1)
                    if c % 8 == 7:
                        qi = c // 8
                        if qi == 0:
                            te.wait(s_q1, s_q1.n)
                        else:
                            te.wait(lg, lg.n)
                        te.wait(l_h, l_h.n)
                        tok = None
                        for k in range(8):
                            for (t0, tw) in fch:
                                tok = te.matmul(out=pR[:, t0:t0 + tw],
                                                lhsT=ltc[:, k, 9:16:2],
                                                rhs=hT[:, qi * 8 + k, t0:t0 + tw],
                                                start=(k == 0), stop=(k == 7))
                        tok.inc(l_v, 1)
                        ac.wait(l_v)
                        if qi >= 1:
                            ac.wait(lg, lg.n)
                        ac.activation(out=vsb[:, 0:THF], in_=pR[:, 0:THF],
                                      func=AF.Copy).inc(s_q4, 1)
                        gp.wait(s_q4)
                        for ii2 in range(4):
                            gp.dma_start(out=vfl[0:1, (qi * 4 + ii2) * THF:(qi * 4 + ii2 + 1) * THF],
                                         in_=vsb[ii2:ii2 + 1, 0:THF]).inc(lg, 16)
                if h < 3:
                    l_ev = emit_ln(s_x, list(range((h + 1) * NTHF, (h + 2) * NTHF)),
                                   xn_fm[:, (h + 1) % 2, :, :], "plain", pLN[:, :])
                    lntok[h + 1] = l_ev.n
                te.wait(l_h, l_h.n)
                te.wait(lg, lg.n)
                for tt in range(NTHF):
                    sl = tt % 2
                    i = h * NTHF + tt
                    xr_load(sl, s_x[i * 128:(i + 1) * 128, :])
                    te.wait(l_w2)
                    if tt >= 2:
                        te.wait(l_ox, l_ox.n - 1)
                    tok = None
                    for k in range(32):
                        for t0 in range(2):
                            tok = te.matmul(out=pA[:, sl, t0 * 512:(t0 + 1) * 512],
                                            lhsT=hT[:, k, tt * 128:(tt + 1) * 128],
                                            rhs=w2a[:, k, t0 * 512:(t0 + 1) * 512],
                                            start=(k == 0), stop=False)
                    for ii in range(4):
                        for qi in range(4):
                            tok = te.matmul(out=pA[:, sl, ii * 256:(ii + 1) * 256],
                                            lhsT=vfl[0:1, (qi * 4 + ii) * THF + tt * 128:(qi * 4 + ii) * THF + tt * 128 + 128],
                                            rhs=ev4[0:1, ii * 1024 + qi:ii * 1024 + qi + 1021:4],
                                            start=False, stop=(qi == 3 and ii % 2 == 1))
                    tok.inc(l_o, 1)
                    ve.wait(l_o); ve.wait(lxrp[sl])
                    ve.tensor_add(out=xres[:, sl, :], in0=xres[:, sl, :], in1=b2_bc[:, :])
                    ve.dr()
                    ve.tensor_add(out=xres[:, sl, :], in0=xres[:, sl, :],
                                  in1=pA[:, sl, 0:1024]).inc(l_ox, 1)
                    xr_used(sl, l_ox)
                    sy.wait(l_ox)
                    dst = h_out if is_last else s_x
                    sy.dma_start(out=dst[i * 128:(i + 1) * 128, :],
                                 in_=xres[:, sl, :]).inc(st, 16)
                if h == 3:
                    barrier(st)

    def emit_attn():
        THA = T // 4
        NTHA = NT // 4
        fch = [(s0, min(512, THA - s0)) for s0 in range(0, THA, 512)]
        with contextlib.ExitStack() as ats:
            ab = lambda n, s, d: ats.enter_context(nc.sbuf_tensor(n, list(s), d))
            xn_fm = ab("a_xn", [128, 8, THA], BF16)
            wqn = ab("a_wq", [128, 8, 1024], BF16)
            wkn = ab("a_wk", [128, 8, 1024], BF16)
            wvn = ab("a_wv", [128, 8, 1024], BF16)
            qf = ab("a_q", [128, 8, THA], BF16)
            kf = ab("a_k", [128, 8, THA], BF16)
            vt = ab("a_v", [128, NTHA, 1024], BF16)
            qh = ab("a_qh", [64, 8, THA], BF16)
            kh = ab("a_kh", [64, 8, THA], BF16)
            vh = ab("a_vh", [64, NTHA, 1024], BF16)
            pex = ab("a_pex", [128, 2, 1024], BF16)
            sub32 = ab("a_sub", [128, 1024], F32)
            ptb = ab("a_ptb", [64, 2, 2048], BF16)
            nm = ab("a_nm", [128, 2, 16], F32)
            lsum = ab("a_l", [128, 2, 16], F32)
            linv = ab("a_linv", [128, 2, 16], F32)
            pA = ats.enter_context(nc.psum_tensor("a_pA", [128, 2, 1024], F32))
            pT = ats.enter_context(nc.psum_tensor("a_pT", [64, 2048], BF16))
            pO = ats.enter_context(nc.psum_tensor("a_pO", [128, 1024], F32))
            lnps_bf = pA[:, 0, :].bitcast(BF16)[:, 0:1024]

            gp.dma_start(out=wqn[:, :, :], in_=h_Wq.rearrange("(k p) j -> p k j", p=128)).inc(lg, 16)
            gp.dma_start(out=wkn[:, :, :], in_=h_Wk.rearrange("(k p) j -> p k j", p=128)).inc(lg, 16)
            gp.dma_start(out=wvn[:, :, :], in_=h_Wv.rearrange("(k p) j -> p k j", p=128)).inc(lg, 16)

            for h in range(4):
                tiles = list(range(h * NTHA, (h + 1) * NTHA))
                l_ev = emit_ln(s_x, tiles, xn_fm, "plain", lnps_bf)
                barrier(l_ev)
                te.wait(lg)
                it = 0
                for jc in range(8):
                    for wsrc, dst, bias in ((wqn, qf, bq_fm), (wkn, kf, bk_fm)):
                        sl = it % 2
                        if it >= 2:
                            te.wait(l_qe, l_qe.n - 1)
                        tok = None
                        for k in range(8):
                            for (t0, tw) in fch:
                                tok = te.matmul(out=pA[:, sl, t0:t0 + tw],
                                                lhsT=wsrc[:, k, jc * 128:(jc + 1) * 128],
                                                rhs=xn_fm[:, k, t0:t0 + tw],
                                                start=(k == 0), stop=(k == 7))
                        tok.inc(l_qk, 1)
                        ac.wait(l_qk)
                        ac.activation(out=dst[:, jc, :], in_=pA[:, sl, 0:THA], func=AF.Identity,
                                      bias=bias[:, jc:jc + 1]).inc(l_qe, 1)
                        sy.wait(l_qe)
                        sy.dma_start(out=(qh if dst is qf else kh)[0:64, jc, :],
                                     in_=dst[64:128, jc, :]).inc(l_sm, 16)
                        it += 1
                for tt in range(NTHA):
                    sl = tt % 2
                    if tt < 2:
                        te.wait(l_qe)
                    else:
                        te.wait(l_vv, l_vv.n - 1)
                    tok = None
                    for k in range(8):
                        for t0 in range(2):
                            tok = te.matmul(out=pA[:, sl, t0 * 512:(t0 + 1) * 512],
                                            lhsT=xn_fm[:, k, tt * 128:(tt + 1) * 128],
                                            rhs=wvn[:, k, t0 * 512:(t0 + 1) * 512],
                                            start=(k == 0), stop=(k == 7))
                    tok.inc(l_vm, 1)
                    ve.wait(l_vm)
                    ve.tensor_add(out=vt[:, tt, :], in0=pA[:, sl, 0:1024],
                                  in1=bv_bc[:, :]).inc(l_vv, 1)
                    sy.wait(l_vv)
                    sy.dma_start(out=vh[0:64, tt, :], in_=vt[64:128, tt, :]).inc(l_sm, 16)
                if ATTN_STAGE < 2:
                    barrier(l_vv)
                    barrier(l_sm)
                    continue
                vv_q = l_vv.n
                sm_q = l_sm.n
                ex_tok = {}; pn_tok = {}; tt_tok = {}; pt_tok = {}
                av_tok = {}; ox_tok = {}; sc_tok = {}; sub_tok = {}

                def scores(tt):
                    sl = tt % 2
                    if tt < 2:
                        te.wait(l_vv, vv_q)
                        te.wait(l_sm, sm_q)
                        te.wait(l_qe, l_qe.n)
                    else:
                        te.wait(s_q5, sub_tok[tt - 2])
                    tok = None
                    for hd in range(16):
                        ch = hd // 2
                        qsrc = qf if hd % 2 == 0 else qh
                        ksrc = kf if hd % 2 == 0 else kh
                        for ib in range(2):
                            tok = te.matmul(out=pA[ib * 64:(ib + 1) * 64, sl, hd * 64:(hd + 1) * 64],
                                            lhsT=qsrc[0:64, ch, tt * 128 + ib * 64:tt * 128 + ib * 64 + 64],
                                            rhs=ksrc[0:64, ch, tt * 128 + ib * 64:tt * 128 + ib * 64 + 64],
                                            start=True, stop=True)
                    tok.inc(l_sc, 1)
                    sc_tok[tt] = l_sc.n

                def chain(tt):
                    sl = tt % 2
                    i = h * NTHA + tt
                    ve.wait(l_sc, sc_tok[tt])
                    ve.reduce_max(out=nm[:, sl, :],
                                  in_=pA[:, sl, :].rearrange("p (a j) -> p a j", a=16),
                                  axis=AX.X)
                    ve.dr()
                    ve.tensor_sub(out=sub32[:, :].rearrange("p (a j) -> p a j", a=16),
                                  in0=pA[:, sl, :].rearrange("p (a j) -> p a j", a=16),
                                  in1=nm[:, sl, :].rearrange("p (a o) -> p a o", o=1).to_broadcast((128, 16, 64))).inc(s_q5, 1)
                    sub_tok[tt] = s_q5.n
                    ac.wait(s_q5, sub_tok[tt])
                    if tt >= 2:
                        ac.wait(l_tt, tt_tok[tt - 2])
                    ac.activation(out=pex[:, sl, :], in_=sub32[:, :],
                                  func=AF.Exp, scale=0.125).inc(l_ex, 1)
                    ex_tok[tt] = l_ex.n
                    ve.wait(l_ex, ex_tok[tt])
                    ve.reduce_sum(out=lsum[:, sl, :],
                                  in_=pex[:, sl, :].rearrange("p (a j) -> p a j", a=16),
                                  axis=AX.X)
                    ve.dr()
                    ve.reciprocal(out=linv[:, sl, :], in_=lsum[:, sl, :])
                    ve.dr()
                    ve.tensor_mul(out=pex[:, sl, :].rearrange("p (a j) -> p a j", a=16),
                                  in0=linv[:, sl, :].rearrange("p (a o) -> p a o", o=1).to_broadcast((128, 16, 64)),
                                  in1=pex[:, sl, :].rearrange("p (a j) -> p a j", a=16)).inc(l_pn, 1)
                    pn_tok[tt] = l_pn.n
                    if ATTN_STAGE < 3:
                        return
                    te.wait(l_pn, pn_tok[tt])
                    if tt >= 1:
                        te.wait(l_pt, pt_tok[tt - 1])
                    tok = None
                    for ib in range(2):
                        for hd in range(16):
                            tok = te.transpose(out=pT[0:64, (ib * 16 + hd) * 64:(ib * 16 + hd) * 64 + 64],
                                               in_=pex[ib * 64:(ib + 1) * 64, sl, hd * 64:(hd + 1) * 64],
                                               identity=eye[ib * 64:(ib + 1) * 64, ib * 64:(ib + 1) * 64])
                    tok.inc(l_tt, 1)
                    tt_tok[tt] = l_tt.n
                    ac.wait(l_tt, tt_tok[tt])
                    if tt >= 2:
                        ac.wait(l_av, av_tok[tt - 2])
                    ac.activation(out=ptb[0:64, sl, :], in_=pT[0:64, :], func=AF.Copy).inc(l_pt, 1)
                    pt_tok[tt] = l_pt.n
                    if ATTN_STAGE < 4:
                        return
                    te.wait(l_pt, pt_tok[tt])
                    if tt >= 1:
                        te.wait(l_ox, ox_tok[tt - 1])
                    tok = None
                    for ib in range(2):
                        vsrc = vt if ib == 0 else vh
                        for hd in range(16):
                            tok = te.matmul(out=pO[ib * 64:(ib + 1) * 64, hd * 64:(hd + 1) * 64],
                                            lhsT=ptb[0:64, sl, (ib * 16 + hd) * 64:(ib * 16 + hd) * 64 + 64],
                                            rhs=vsrc[0:64, tt, hd * 64:(hd + 1) * 64],
                                            start=True, stop=True)
                    tok.inc(l_av, 1)
                    av_tok[tt] = l_av.n
                    xr_load(sl, s_x[i * 128:(i + 1) * 128, :])
                    ve.wait(l_av, av_tok[tt]); ve.wait(lxrp[sl])
                    ve.tensor_add(out=xres[:, sl, :], in0=xres[:, sl, :],
                                  in1=pO[:, 0:1024]).inc(l_ox, 1)
                    ox_tok[tt] = l_ox.n
                    xr_used(sl, l_ox)
                    sy.wait(l_ox, ox_tok[tt])
                    sy.dma_start(out=s_x[i * 128:(i + 1) * 128, :],
                                 in_=xres[:, sl, :]).inc(st, 16)

                for tt in range(NTHA):
                    scores(tt)
                    if tt >= 1:
                        chain(tt - 1)
                chain(NTHA - 1)
                barrier(st)

    emit_mamba()
    p2toks = emit_prep2()
    if nsub >= 2:
        emit_ffn("a", is_last=(nsub == 2), prep_toks=p2toks)
    if nsub >= 3:
        emit_attn()
    if nsub >= 4:
        emit_ffn("b", is_last=True)
    if nsub < 2 or nsub == 3:
        cpb = b.sb("cpb", [128, 2, D], F32)
        for i in range(NT):
            sl = i % 2
            if i >= 2:
                sy.wait(st, st.n)
            sy.dma_start(out=cpb[:, sl, :], in_=s_x[i * 128:(i + 1) * 128, :]).inc(lxrp[sl], 16)
            sy.wait(lxrp[sl])
            sy.dma_start(out=h_out[i * 128:(i + 1) * 128, :], in_=cpb[:, sl, :]).inc(st, 16)
    sy.wait(st)

    b.finalize()
    es.close()
    return nc


def make_inmaps(inputs, T=2048):
    x = np.ascontiguousarray(np.asarray(inputs["L0_attn_a_x"], dtype=np.float32))
    nB = x.shape[0]
    eyec = np.eye(128).astype(ml_dtypes.bfloat16)
    shared = {
        "gamma": np.asarray(inputs["L0_attn_a_gamma"], np.float32),
        "beta": np.asarray(inputs["L0_attn_a_beta"], np.float32),
        "Win": np.asarray(inputs["L0_attn_a_W_in"], np.float32),
        "bin": np.asarray(inputs["L0_attn_a_b_in"], np.float32),
        "convw": np.asarray(inputs["L0_attn_a_conv_w"], np.float32),
        "convb": np.asarray(inputs["L0_attn_a_conv_b"], np.float32),
        "Wxm": np.asarray(inputs["L0_attn_a_W_x"], np.float32),
        "Wdt": np.asarray(inputs["L0_attn_a_W_dt"], np.float32),
        "bdt": np.asarray(inputs["L0_attn_a_b_dt"], np.float32),
        "Alog": np.asarray(inputs["L0_attn_a_A_log"], np.float32),
        "Dp": np.asarray(inputs["L0_attn_a_D_param"], np.float32),
        "Wout": np.asarray(inputs["L0_attn_a_W_out"], np.float32),
        "bout": np.asarray(inputs["L0_attn_a_b_out"], np.float32),
        "Wq": np.asarray(inputs["L0_attn_b_Wq"], np.float32),
        "bq": np.asarray(inputs["L0_attn_b_bq"], np.float32),
        "Wk": np.asarray(inputs["L0_attn_b_Wk"], np.float32),
        "bk": np.asarray(inputs["L0_attn_b_bk"], np.float32),
        "Wv": np.asarray(inputs["L0_attn_b_Wv"], np.float32),
        "bv": np.asarray(inputs["L0_attn_b_bv"], np.float32),
        "W1": np.asarray(inputs["L0_ffn_W1"], np.float32),
        "b1": np.asarray(inputs["L0_ffn_b1"], np.float32),
        "W2": np.asarray(inputs["L0_ffn_W2"], np.float32),
        "b2": np.asarray(inputs["L0_ffn_b2"], np.float32),
        "fscale": np.asarray(inputs["L0_ffn_scale"], np.float32).reshape(1, 1),
        "eyebf": eyec,
    }
    return [dict(shared, x=np.ascontiguousarray(x[bb, :T, :])) for bb in range(nB)]


def kernel(**inputs):
    T = 2048
    nc = build_program(T=T, nsub=4)
    in_maps = make_inmaps(inputs, T=T)
    res = run_bass_kernel_spmd(nc, in_maps, core_ids=list(range(len(in_maps))))
    return np.stack([np.asarray(r["out"], dtype=np.float32) for r in res.results], axis=0)


# version 35
# speedup vs baseline: 1.0194x; 1.0031x over previous
import numpy as np
import contextlib
import ml_dtypes
import concourse.bass as bass
import concourse.mybir as mybir
from concourse.bass_utils import run_bass_kernel_spmd

F32 = mybir.dt.float32
BF16 = mybir.dt.bfloat16
AF = mybir.ActivationFunctionType
ALU = mybir.AluOpType
AX = mybir.AxisListType

D = 1024
ED = 2048
NST = 16
KC = 15
RDT = 64
HF = 4096
EPS = 1e-5

ENGS = ("sync", "vector", "scalar", "tensor", "gpsimd")
ATTN_STAGE = 4


class Sem:
    __slots__ = ("h", "n")

    def __init__(self, h):
        self.h = h
        self.n = 0


class Tok:
    __slots__ = ("q", "box")

    def __init__(self, q, box):
        self.q = q
        self.box = box

    def inc(self, sem, k=1):
        sem.n += k
        box = self.box
        self.q.rec.append(lambda e: box["i"].then_inc(sem.h, k))
        return self


class Q:
    def __init__(self, name):
        self.name = name
        self.rec = []

    def op(self, fname, *a, **kw):
        box = {}

        def fn(e, a=a, kw=kw, fname=fname, box=box):
            box["i"] = getattr(e, fname)(*a, **kw)

        self.rec.append(fn)
        return Tok(self, box)

    def __getattr__(self, fname):
        if fname.startswith("_"):
            raise AttributeError(fname)
        return lambda *a, **kw: self.op(fname, *a, **kw)

    def wait(self, sem, n=None):
        v = sem.n if n is None else n
        if v > 0:
            self.rec.append(lambda e, v=v: e.wait_ge(sem.h, v))

    def dr(self):
        self.rec.append(lambda e: e.drain())


class B:
    def __init__(self, nc, es):
        self.nc = nc
        self.es = es
        self.q = {n: Q(n) for n in ENGS}
        self.sy = self.q["sync"]
        self.ve = self.q["vector"]
        self.ac = self.q["scalar"]
        self.te = self.q["tensor"]
        self.gp = self.q["gpsimd"]
        self._semi = 0

    def sem(self, name=None):
        self._semi += 1
        return Sem(self.es.enter_context(self.nc.semaphore(name or "s%d" % self._semi)))

    def sb(self, name, shape, dt):
        return self.es.enter_context(self.nc.sbuf_tensor(name, list(shape), dt))

    def finalize(self):
        with self.nc.Block() as block:
            for name in ENGS:
                rec = self.q[name].rec

                def body(e, rec=rec):
                    for fn in rec:
                        fn(e)

                getattr(block, name)(body)


def build_program(T=2048, nsub=4):
    NT = T // 128
    TH = T // 2
    NTH = NT // 2
    nc = bass.Bass("TRN2", target_bir_lowering=False)

    def dram(name, shape, dt=F32, kind="ExternalInput"):
        return nc.dram_tensor(name, list(shape), dt, kind=kind).ap()

    hx = dram("x", [T, D])
    h_gamma = dram("gamma", [D]); h_beta = dram("beta", [D])
    h_Win = dram("Win", [2 * ED, D]); h_bin = dram("bin", [2 * ED])
    h_convw = dram("convw", [ED, KC]); h_convb = dram("convb", [ED])
    h_Wx = dram("Wxm", [RDT + 2 * NST, ED])
    h_Wdt = dram("Wdt", [ED, RDT]); h_bdt = dram("bdt", [ED])
    h_Alog = dram("Alog", [ED, NST]); h_Dp = dram("Dp", [ED])
    h_Wout = dram("Wout", [D, ED]); h_bout = dram("bout", [D])
    h_Wq = dram("Wq", [D, D]); h_bq = dram("bq", [D])
    h_Wk = dram("Wk", [D, D]); h_bk = dram("bk", [D])
    h_Wv = dram("Wv", [D, D]); h_bv = dram("bv", [D])
    h_W1 = dram("W1", [HF, D]); h_b1 = dram("b1", [HF])
    h_W2 = dram("W2", [D, HF]); h_b2 = dram("b2", [D])
    h_scale = dram("fscale", [1, 1])
    h_eye = dram("eyebf", [128, 128], BF16)
    h_out = dram("out", [T, D], kind="ExternalOutput")
    s_x = dram("s_x", [T, D], kind="Internal")
    s_WinT = dram("s_WinT", [32, 128, 1024], BF16, kind="Internal")
    s_W1T = dram("s_W1T", [32, 128, 1024], BF16, kind="Internal")
    s_W2T = dram("s_W2T", [32, 128, 1024], BF16, kind="Internal")
    s_WoutT = dram("s_WoutT", [16, 128, 1024], BF16, kind="Internal")
    s_rows = dram("s_rows", [32, T], BF16, kind="Internal")
    s_yfc = dram("s_yfc", [16, 128, T // 4], BF16, kind="Internal")
    s_brow = dram("s_brow", [1, 2 * ED], kind="Internal")

    es = contextlib.ExitStack()
    b = B(nc, es)
    sy, ve, ac, te, gp = b.sy, b.ve, b.ac, b.te, b.gp

    lx = b.sem("lx"); lg = b.sem("lg"); st = b.sem("st")
    pv = b.sem("pv")
    s_dmm = b.sem("sdmm"); s_sp = b.sem("ssp"); s_dl = b.sem("sdl")
    s_dx = b.sem("sdx"); s_dA = b.sem("sdA"); s_dB = b.sem("sdB")
    s_sc = b.sem("ssc"); s_g = b.sem("sg"); s_ym = b.sem("sym")
    s_zm = b.sem("szm"); s_sz = b.sem("ssz"); s_yf = b.sem("syf")
    s_wz0 = b.sem("swz0"); s_wz1 = b.sem("swz1"); s_ev = b.sem("sev")
    s_wzp = (s_wz0, s_wz1)
    s_wzz0 = b.sem("swzz0"); s_wzz1 = b.sem("swzz1"); s_wzz = (s_wzz0, s_wzz1)
    lxr0 = b.sem("lxr0"); lxr1 = b.sem("lxr1"); lxrp = (lxr0, lxr1)
    s_q1 = b.sem("sq1"); s_q2 = b.sem("sq2"); s_q3 = b.sem("sq3")
    s_q4 = b.sem("sq4"); s_q5 = b.sem("sq5")
    l_tp = b.sem("ltp_"); l_ap = b.sem("lap_"); l_ev = b.sem("lev_")
    l_wi = b.sem("lwi"); l_cv = b.sem("lcv"); l_dg = b.sem("ldg")
    l_xin = b.sem("lxin"); l_tl = b.sem("ltl"); l_cm = b.sem("lcm")
    l_om = b.sem("lom"); l_ox = b.sem("lox"); l_wo = b.sem("lwo")
    l_w2 = b.sem("lw2"); l_lt = b.sem("llt"); l_tr = b.sem("ltr")
    l_u = b.sem("lu"); l_h = b.sem("lh"); l_v = b.sem("lv"); l_o = b.sem("lo")
    l_qk = b.sem("lqk"); l_qe = b.sem("lqe"); l_vm = b.sem("lvm")
    l_vv = b.sem("lvv"); l_sc = b.sem("lsc"); l_sm = b.sem("lsm")
    l_ex = b.sem("lex"); l_pn = b.sem("lpn"); l_pt = b.sem("lpt")
    l_tt = b.sem("ltt"); l_av = b.sem("lav")

    eye = b.sb("eye", [128, 128], BF16)
    A_fm = b.sb("A_fm", [128, 16, NST], F32)
    convw_fm = b.sb("convw_fm", [128, 16, KC], F32)
    cb_fm = b.sb("cb_fm", [128, 16], F32)
    Dp_fm = b.sb("Dp_fm", [128, 16], F32)
    bin_fm = b.sb("bin_fm", [128, 32], F32)
    bo_bc = b.sb("bo_bc", [128, D], F32)
    b2_bc = b.sb("b2_bc", [128, D], F32)
    bv_bc = b.sb("bv_bc", [128, D], F32)
    b1_fm = b.sb("b1_fm", [128, 32], F32)
    bq_fm = b.sb("bq_fm", [128, 8], F32)
    bk_fm = b.sb("bk_fm", [128, 8], F32)
    gam_fm = b.sb("gam_fm", [128, 8], F32)
    scale_sb = b.sb("scale_sb", [1, 1], F32)
    bdtrow_fm = b.sb("bdt_fm", [128, 16], F32)
    carry = b.sb("carry", [128, 256], F32)
    stat6 = b.sb("stat6", [128, 12], F32)
    mv2 = b.sb("mv2", [128, 2], F32)
    mu1 = b.sb("mu1", [128, 16], F32)
    ss1 = b.sb("ss1", [128, 16], F32)
    r1a = b.sb("r1a", [128, 16], F32)
    tmp1 = b.sb("tmp1", [128, 1], F32)
    xres = b.sb("xres", [128, 2, D], F32)
    xntok = b.sb("xntok", [128, 2, D], BF16)
    xr_last = {0: None, 1: None}

    def xr_load(sl, src_slice):
        if xr_last[sl] is not None:
            sem, v = xr_last[sl]
            sy.wait(sem, v)
        sy.wait(st, st.n)
        sy.dma_start(out=xres[:, sl, :], in_=src_slice).inc(lxrp[sl], 16)

    def xr_used(sl, sem):
        xr_last[sl] = (sem, sem.n)
    WxT = b.sb("WxT", [128, 16 * 96], BF16)
    WdtT = b.sb("WdtT", [64, ED], BF16)

    sy.dma_start(out=eye[:, :], in_=h_eye).inc(lx, 16)
    for dst, src in [(cb_fm, h_convb), (bdtrow_fm, h_bdt), (Dp_fm, h_Dp),
                     (b1_fm, h_b1), (bq_fm, h_bq), (bk_fm, h_bk),
                     (gam_fm, h_gamma)]:
        sy.dma_start(out=dst[:, :], in_=src.rearrange("(j p) -> p j", p=128), allow_slow_non_contiguous=True).inc(lx, 16)
    sy.dma_start(out=A_fm[:, :, :], in_=h_Alog.rearrange("(c p) n -> p c n", p=128)).inc(lx, 16)
    sy.dma_start(out=convw_fm[:, :, :], in_=h_convw.rearrange("(c p) k -> p c k", p=128)).inc(lx, 16)
    sy.dma_start(out=bo_bc[:, :], in_=h_bout.rearrange("(o d) -> o d", o=1).broadcast_to((128, D))).inc(lx, 16)
    sy.dma_start(out=b2_bc[:, :], in_=h_b2.rearrange("(o d) -> o d", o=1).broadcast_to((128, D))).inc(lx, 16)
    sy.dma_start(out=bv_bc[:, :], in_=h_bv.rearrange("(o d) -> o d", o=1).broadcast_to((128, D))).inc(lx, 16)
    sy.dma_start(out=scale_sb[:, :], in_=h_scale).inc(lx, 16)

    ac.wait(lx)
    ac.activation(out=A_fm[:, :, :], in_=A_fm[:, :, :], func=AF.Exp)
    ac.dr()
    ac.op("mul", A_fm[:, :, :], A_fm[:, :, :], -1.0)
    ve.wait(lx)
    ve.memset(carry[:, :], 0.0)
    ve.dr()

    xr_load(0, hx[0:128, :])
    xr_load(1, hx[128:256, :])
    PREP1 = {}
    with contextlib.ExitStack() as prep:
        pb = lambda n, s, d: prep.enter_context(nc.sbuf_tensor(n, list(s), d))
        pps = lambda n, s, d: prep.enter_context(nc.psum_tensor(n, list(s), d))
        wnat = pb("wnat", [128, 2, 1024], BF16)
        wnatB = pb("wnatB", [128, 8, 4096], BF16)
        wstage = pb("wstage", [128, 2, 1024], BF16)
        beta_bf = pb("beta_bf", [128, 8], BF16)
        binrow = pb("binrow", [1, 2 * ED], F32)
        p_tp = pps("p_tp", [128, 2, 1024], BF16)
        p_br = pps("p_br", [1, 2048], F32)
        ltp0 = b.sem("ltp0"); ltp1 = b.sem("ltp1"); ltpp = (ltp0, ltp1)
        lnb = b.sem("lnb"); lnc = b.sem("lnc")
        lbr = b.sem("lbr")
        ttp = b.sem("ttp"); etp = b.sem("etp")
        wtp0 = b.sem("wtp0"); wtp1 = b.sem("wtp1"); wtpp = (wtp0, wtp1)
        btp = b.sem("btp")

        sy.dma_start(out=binrow[:, :], in_=h_bin.rearrange("(o f) -> o f", o=1)).inc(lbr, 16)
        gp.wait(lx)
        gp.dma_start(out=beta_bf[:, :], in_=h_beta.rearrange("(j p) -> p j", p=128), allow_slow_non_contiguous=True).inc(lnb, 16)

        P1TT = []; P1ET = []; P1WB = []; P1BT = []

        def prep_core(i, src_slice, cache_row, beta=False):
            sl = i % 2
            if i >= 2:
                gp.wait(ttp, P1TT[i - 2])
            gp.dma_start(out=wnat[:, sl, :], in_=src_slice).inc(ltpp[sl], 16)
            te.wait(ltpp[sl])
            if i >= 2:
                te.wait(etp, P1ET[i - 2])
            tok = None
            for k in range(8):
                tok = te.transpose(out=p_tp[:, sl, k * 128:(k + 1) * 128],
                                   in_=wnat[:, sl, k * 128:(k + 1) * 128],
                                   identity=eye[:, :])
            tok.inc(ttp, 8)
            P1TT.append(ttp.n)
            ac.wait(ttp, P1TT[i])
            if i >= 2:
                ac.wait(wtpp[sl], P1WB[i - 2][1])
                if beta:
                    ac.wait(btp, P1BT[i - 2])
            ac.activation(out=wstage[:, sl, :], in_=p_tp[:, sl, :], func=AF.Copy).inc(etp, 1)
            P1ET.append(etp.n)
            if beta:
                te.wait(etp, P1ET[i])
                if i == 0:
                    te.wait(lnb, 16)
                tok = None
                for k in range(8):
                    tok = te.matmul(out=p_br[0:1, (i % 16) * 128:(i % 16) * 128 + 128],
                                    lhsT=beta_bf[:, k:k + 1],
                                    rhs=wstage[:, sl, k * 128:(k + 1) * 128],
                                    start=((i % 4 == 0) and k == 0),
                                    stop=((i % 4 == 3) and k == 7))
                tok.inc(btp, 1)
                if i % 16 == 15:
                    ve.wait(btp, btp.n)
                    ve.wait(lbr)
                    half = i // 16
                    ve.tensor_add(out=binrow[0:1, half * 2048:(half + 1) * 2048],
                                  in0=binrow[0:1, half * 2048:(half + 1) * 2048],
                                  in1=p_br[0:1, :]).inc(pv, 1)
                    te.wait(pv, pv.n)
            P1BT.append(btp.n)
            sy.wait(etp, P1ET[i])
            sy.dma_start(out=cache_row, in_=wstage[:, sl, :]).inc(wtpp[sl], 16)
            P1WB.append((wtpp[sl], wtpp[sl].n))

        for i in range(32):
            prep_core(i, h_Win[i * 128:(i + 1) * 128, :], s_WinT[i, :, :], beta=True)

        def prepB(src_h, nj, ncols, cache):
            for k in range(8):
                gp.dma_start(out=wnatB[:, k, 0:ncols],
                             in_=src_h[k * 128:(k + 1) * 128, :]).inc(lnb, 16)
            te.wait(lnb, lnb.n)
            for j in range(nj):
                i = len(P1TT)
                sl = i % 2
                te.wait(etp, P1ET[i - 2])
                tok = None
                for k in range(8):
                    tok = te.transpose(out=p_tp[:, sl, k * 128:(k + 1) * 128],
                                       in_=wnatB[:, k, j * 128:(j + 1) * 128],
                                       identity=eye[:, :])
                tok.inc(ttp, 8)
                P1TT.append(ttp.n)
                ac.wait(ttp, P1TT[i])
                ac.wait(wtpp[sl], P1WB[i - 2][1])
                if i - 2 < len(P1BT):
                    ac.wait(btp, P1BT[min(i - 2, len(P1BT) - 1)])
                ac.activation(out=wstage[:, sl, :], in_=p_tp[:, sl, :], func=AF.Copy).inc(etp, 1)
                P1ET.append(etp.n)
                sy.wait(etp, P1ET[i])
                sy.dma_start(out=cache[j, :, :], in_=wstage[:, sl, :]).inc(wtpp[sl], 16)
                P1WB.append((wtpp[sl], wtpp[sl].n))

        prepB(h_Wout, 16, 2048, s_WoutT)

        sy.wait(pv)
        sy.dma_start(out=s_brow, in_=binrow[0:1, :]).inc(st, 16)
        sy.wait(st)
        sy.dma_start(out=bin_fm[:, :], in_=s_brow.rearrange("o (j p) -> p (o j)", p=128), allow_slow_non_contiguous=True).inc(lx, 16)

        wxnat = pb("wxnat", [96, ED], BF16)
        wdtnat = pb("wdtnat", [128, 16, RDT], BF16)
        gp.dma_start(out=wxnat[:, :], in_=h_Wx).inc(lnc, 16)
        gp.dma_start(out=wdtnat[:, :, :],
                     in_=h_Wdt.rearrange("(c p) r -> p c r", p=128)).inc(lnc, 16)
        te.wait(lnc)
        for grp in range(2):
            te.wait(etp)
            tok = None
            for cc in range(8):
                c = grp * 8 + cc
                tok = te.transpose(out=p_tp[:, 0, cc * 128:cc * 128 + 96],
                                   in_=wxnat[0:96, c * 128:(c + 1) * 128],
                                   identity=eye[0:96, 0:96])
            tok.inc(ttp, 8)
            ac.wait(ttp)
            tok = None
            for cc in range(8):
                c = grp * 8 + cc
                tok = ac.activation(out=WxT[:, c * 96:(c + 1) * 96],
                                    in_=p_tp[:, 0, cc * 128:cc * 128 + 96],
                                    func=AF.Copy)
            tok.inc(etp, 1)
        for grp in range(2):
            te.wait(etp)
            tok = None
            for cc in range(8):
                c = grp * 8 + cc
                tok = te.transpose(out=p_tp[0:64, 0, cc * 128:(cc + 1) * 128],
                                   in_=wdtnat[:, c, :],
                                   identity=eye[:, :])
            tok.inc(ttp, 8)
            ac.wait(ttp)
            tok = ac.activation(out=WdtT[:, grp * 1024:(grp + 1) * 1024],
                                in_=p_tp[0:64, 0, :], func=AF.Copy)
            tok.inc(etp, 1)
        PREP1.update({"etp": (etp, etp.n), "ttp": (ttp, ttp.n),
                      "btp": (btp, btp.n), "pv": (pv, pv.n),
                      "win": P1WB[0:32], "wout": P1WB[32:48]})


    def emit_prep2():
        with contextlib.ExitStack() as prep:
            pb = lambda n, s, d: prep.enter_context(nc.sbuf_tensor(n, list(s), d))
            pps = lambda n, s, d: prep.enter_context(nc.psum_tensor(n, list(s), d))
            wnat = pb("p2_wnat", [128, 2, 1024], BF16)
            wnatB = pb("p2_wnatB", [128, 8, 4096], BF16)
            wstage = pb("p2_wstage", [128, 2, 1024], BF16)
            p_tp = pps("p2_ptp", [128, 2, 1024], BF16)
            ltp0 = b.sem("p2ltp0"); ltp1 = b.sem("p2ltp1"); ltpp = (ltp0, ltp1)
            lnb = b.sem("p2lnb")
            ttp = b.sem("p2ttp"); etp = b.sem("p2etp")
            wtp0 = b.sem("p2wtp0"); wtp1 = b.sem("p2wtp1"); wtpp = (wtp0, wtp1)
            tt_toks = []; et_toks = []; wb_toks = []

            def prep_row(ci, in_tile, cache_row):
                sl = ci % 2
                if ci >= 2:
                    te.wait(etp, et_toks[ci - 2])
                tok = None
                for k in range(8):
                    tok = te.transpose(out=p_tp[:, sl, k * 128:(k + 1) * 128],
                                       in_=in_tile[k],
                                       identity=eye[:, :])
                tok.inc(ttp, 8)
                tt_toks.append(ttp.n)
                ac.wait(ttp, tt_toks[ci])
                if ci >= 2:
                    ac.wait(wtpp[sl], wb_toks[ci - 2][1])
                ac.activation(out=wstage[:, sl, :], in_=p_tp[:, sl, :],
                              func=AF.Copy).inc(etp, 1)
                et_toks.append(etp.n)
                sy.wait(etp, et_toks[ci])
                sy.dma_start(out=cache_row, in_=wstage[:, sl, :]).inc(wtpp[sl], 16)
                wb_toks.append((wtpp[sl], wtpp[sl].n))

            ci = 0
            for i in range(32):
                sl = i % 2
                if i >= 2:
                    gp.wait(ttp, tt_toks[i - 2])
                gp.dma_start(out=wnat[:, sl, :],
                             in_=h_W1[i * 128:(i + 1) * 128, :]).inc(ltpp[sl], 16)
                te.wait(ltpp[sl])
                prep_row(ci, [wnat[:, sl, k * 128:(k + 1) * 128] for k in range(8)],
                         s_W1T[i, :, :])
                ci += 1
            for k in range(8):
                gp.dma_start(out=wnatB[:, k, :],
                             in_=h_W2[k * 128:(k + 1) * 128, :]).inc(lnb, 16)
            te.wait(lnb, lnb.n)
            for j in range(32):
                prep_row(ci, [wnatB[:, k, j * 128:(j + 1) * 128] for k in range(8)],
                         s_W2T[j, :, :])
                ci += 1
        return (wb_toks[0:32], wb_toks[32:64], (ttp, ttp.n))

    def emit_ln(xsrc, tiles, xn_fm, kind, lnps_bf, preloaded=0):
        for j, i in enumerate(tiles):
            sl = j % 2
            if j >= preloaded:
                xr_load(sl, xsrc[i * 128:(i + 1) * 128, :])
            ve.wait(lxrp[sl])
            ve.bn_stats(out=stat6[:, 0:6], in_=xres[:, sl, 0:512])
            ve.bn_stats(out=stat6[:, 6:12], in_=xres[:, sl, 512:1024])
            ve.dr()
            ve.bn_aggr(out=mv2[:, :], in_=stat6[:, :])
            ve.dr()
            ve.tensor_copy(out=mu1[:, i:i + 1], in_=mv2[:, 0:1])
            if kind == "mamba":
                ve.tensor_scalar(out=tmp1[:, 0:1], in0=mv2[:, 1:2], scalar1=1.0 + EPS,
                                 scalar2=EPS * EPS, op0=ALU.mult, op1=ALU.add)
                ve.dr()
                ve.reciprocal(out=tmp1[:, 0:1], in_=tmp1[:, 0:1])
                ve.dr()
                ve.tensor_scalar(out=r1a[:, i:i + 1], in0=mv2[:, 1:2], scalar1=EPS,
                                 scalar2=None, op0=ALU.add)
                ve.dr()
                ve.reciprocal(out=r1a[:, i:i + 1], in_=r1a[:, i:i + 1]).inc(s_q1, 1)
                ac.wait(s_q1)
                ac.op("sqrt", ss1[:, i:i + 1], tmp1[:, 0:1])
                ac.dr()
                ac.op("sqrt", r1a[:, i:i + 1], r1a[:, i:i + 1]).inc(s_q2, 1)
            else:
                ve.tensor_scalar(out=tmp1[:, 0:1], in0=mv2[:, 1:2], scalar1=EPS,
                                 scalar2=None, op0=ALU.add)
                ve.dr()
                ve.reciprocal(out=tmp1[:, 0:1], in_=tmp1[:, 0:1]).inc(s_q1, 1)
                ac.wait(s_q1)
                ac.op("sqrt", ss1[:, i:i + 1], tmp1[:, 0:1]).inc(s_q2, 1)
            ve.wait(s_q2)
            if j >= 2:
                ve.wait(l_tp, l_tp.n - 8)
            ve.tensor_scalar(out=xntok[:, sl, :], in0=xres[:, sl, :],
                             scalar1=mu1[:, i:i + 1],
                             scalar2=ss1[:, i:i + 1],
                             op0=ALU.subtract, op1=ALU.mult).inc(l_ap, 1)
            xr_used(sl, l_ap)
            te.wait(l_ap)
            te.wait(l_ev, l_ev.n - (0 if j >= 1 else 0))
            if j == 0:
                te.wait(l_cv)
                te.wait(l_ox, l_ox.n)
                te.wait(s_q1, s_q1.n)
                te.wait(PREP1["etp"][0], PREP1["etp"][1])
                te.wait(PREP1["pv"][0], PREP1["pv"][1])
                te.wait(lx, lx.n); ve.wait(lx, lx.n); ac.wait(lx, lx.n)
            tok = None
            for k in range(8):
                tok = te.transpose(out=lnps_bf[:, k * 128:(k + 1) * 128],
                                   in_=xntok[:, sl, k * 128:(k + 1) * 128],
                                   identity=eye[:, :])
            tok.inc(l_tp, 8)
            ac.wait(l_tp)
            if kind == "mamba":
                tok = None
                for k in range(8):
                    tok = ac.activation(out=xn_fm[:, k, j * 128:(j + 1) * 128],
                                        in_=lnps_bf[:, k * 128:(k + 1) * 128],
                                        func=AF.Copy, scale=gam_fm[:, k:k + 1])
                tok.inc(l_ev, 1)
            else:
                ac.activation(out=xn_fm[:, :, j * 128:(j + 1) * 128],
                              in_=lnps_bf.rearrange("p (k c) -> p k c", k=8),
                              func=AF.Copy).inc(l_ev, 1)
        return l_ev

    def barrier(sem):
        for qq in (ve, te, ac, gp, sy):
            qq.wait(sem)

    def emit_mamba():
        TQ = T // 4
        NTQ = NT // 4
        tch = [(s0, min(512, TQ - s0)) for s0 in range(0, TQ, 512)]
        with contextlib.ExitStack() as ms:
            mb = lambda n, s, d: ms.enter_context(nc.sbuf_tensor(n, list(s), d))
            xn_fm = mb("m_xn", [128, 2, 8, TQ], BF16)
            xc = mb("m_xc", [128, 2, 16, TQ], BF16)
            tail = mb("m_tail", [128, 16, 14], BF16)
            dtl = mb("m_dtl", [64, 2, TQ], BF16)
            bcB = mb("m_bcB", [128, 16, TQ], BF16)
            bcC = mb("m_bcC", [128, 16, TQ], BF16)
            delta = mb("m_delta", [128, 2, TQ], F32)
            dxb = mb("m_dxb", [128, TQ], BF16)
            dAb = mb("m_dA", [128, 2, TQ], F32)
            dBxall = mb("m_dBall", [128, 16, TQ], BF16)
            hall = mb("m_hall", [128, 16, TQ], BF16)
            gall = mb("m_gall", [128, 16, TQ], BF16)
            szb = mb("m_sz", [128, 3, TQ], BF16)
            txd = mb("m_txd", [128, D], F32)
            yfull = mb("m_yf", [128, 2, TQ], BF16)
            yfr = mb("m_yfr", [128, 2, TQ], BF16)
            xinp = mb("m_xinp", [128, 2, 14 + TQ], BF16)
            diag = mb("m_diag", [128, KC, 128], BF16)
            bcrows = mb("m_bcr", [32, TQ], BF16)
            wz = mb("m_wz", [128, 2, 1024], BF16)
            wzz = mb("m_wzz", [128, 2, 1024], BF16)
            woutt = mb("m_wouta", [128, 2, 1024], BF16)
            pA = ms.enter_context(nc.psum_tensor("m_pA", [128, 4, 512], F32))
            pB = ms.enter_context(nc.psum_tensor("m_pB", [128, 4, 512], F32))
            lnps_bf = pB[:, 2:4, :].rearrange("p a b -> p (a b)").bitcast(BF16)[:, 0:1024]
            l_fx = b.sem("lfx"); l_yfd0 = b.sem("lyfd0"); l_yfd1 = b.sem("lyfd1")
            yfdp = (l_yfd0, l_yfd1)
            wz_last = {0: None, 1: None}
            wzz_last = {0: None, 1: None}
            EPI_YM = {}; EPI_SZ = {}

            gwz0 = b.sem("gwz0"); gwz1 = b.sem("gwz1"); gwzp = (gwz0, gwz1)

            def wz_load(sl, src_row, pre=None):
                if wz_last[sl] is not None:
                    sem, v = wz_last[sl]
                    gp.wait(sem, v)
                if pre is not None:
                    for t in pre:
                        gp.wait(t[0], t[1])
                gp.dma_start(out=wz[:, sl, :], in_=src_row).inc(gwzp[sl], 16)

            ve.wait(PREP1["ttp"][0], PREP1["ttp"][1])
            ve.memset(tail[:, :, :], 0.0)
            ve.dr()

            FT = {}

            def front_A(h, c):
                hh = h % 2
                sl = c % 2
                pre = None
                if h == 0:
                    pre = [PREP1["win"][c]]
                    if c < 2:
                        pre.append(PREP1["ttp"])
                wz_load(sl, s_WinT[c, :, :], pre=pre)
                te.wait(gwzp[sl])
                if c >= 2:
                    te.wait(l_xin, FT[("xin", h, c - 2)])
                elif ("xin", h - 1, c + 14) in FT:
                    te.wait(l_xin, FT[("xin", h - 1, c + 14)])
                if c == 0:
                    te.wait(l_ev, l_ev.n)
                    te.wait(l_ox, l_ox.n)
                tok = None
                for k in range(8):
                    tok = te.matmul(out=pA[:, 2 + sl, 0:TQ],
                                    lhsT=wz[:, sl, k * 128:(k + 1) * 128],
                                    rhs=xn_fm[:, hh, k, 0:TQ],
                                    start=(k == 0), stop=(k == 7))
                tok.inc(l_wi, 1)
                FT[("wi", h, c)] = l_wi.n
                wz_last[sl] = (l_wi, l_wi.n)

            def front_B(h, c):
                sl = c % 2
                ac.wait(l_wi, FT[("wi", h, c)])
                if c >= 2:
                    ac.wait(l_cm, FT[("cm", h, c - 2)])
                elif ("cm", h - 1, c + 14) in FT:
                    ac.wait(l_cm, FT[("cm", h - 1, c + 14)])
                ac.activation(out=xinp[:, sl, 14:14 + TQ], in_=pA[:, 2 + sl, 0:TQ],
                              func=AF.Identity, bias=bin_fm[:, c:c + 1]).inc(l_xin, 1)
                FT[("xin", h, c)] = l_xin.n
                ac.activation(out=xinp[:, sl, 0:14], in_=tail[:, c, :], func=AF.Copy)
                ac.dr()
                ac.activation(out=tail[:, c, :], in_=xinp[:, sl, TQ:TQ + 14],
                              func=AF.Copy).inc(l_tl, 1)
                FT[("tl", h, c)] = l_tl.n
                if c >= 1:
                    gp.wait(l_cm, FT[("cm", h, c - 1)])
                elif ("cm", h - 1, 15) in FT:
                    gp.wait(l_cm, FT[("cm", h - 1, 15)])
                if h == 0 and c == 0:
                    gp.wait(PREP1["ttp"][0], PREP1["ttp"][1])
                    gp.wait(lx, lx.n)
                gp.tensor_mul(out=diag[:, :, :],
                              in0=convw_fm[:, c, :].rearrange("p (k o) -> p k o", o=1).to_broadcast((128, KC, 128)),
                              in1=eye[:, :].rearrange("p (o j) -> p o j", o=1).to_broadcast((128, KC, 128))).inc(l_dg, 1)
                FT[("dg", h, c)] = l_dg.n
                te.wait(l_dg, FT[("dg", h, c)])
                te.wait(l_tl, FT[("tl", h, c)])
                if c >= 2:
                    te.wait(l_cv, FT[("cv", h, c - 2)])
                elif ("cv", h - 1, c + 14) in FT:
                    te.wait(l_cv, FT[("cv", h - 1, c + 14)])
                tok = None
                for k in range(KC):
                    tok = te.matmul(out=pB[:, 2 + sl, 0:TQ],
                                    lhsT=diag[:, k, :],
                                    rhs=xinp[:, sl, k:k + TQ],
                                    start=(k == 0), stop=(k == KC - 1))
                tok.inc(l_cm, 1)
                FT[("cm", h, c)] = l_cm.n

            def front_C(h, c):
                hh = h % 2
                sl = c % 2
                ac.wait(l_cm, FT[("cm", h, c)])
                if h >= 2:
                    ac.wait(s_dx, 16 * (h - 1))
                ac.activation(out=xc[:, hh, c, :], in_=pB[:, 2 + sl, 0:TQ], func=AF.Silu,
                              bias=cb_fm[:, c:c + 1]).inc(l_cv, 1)
                FT[("cv", h, c)] = l_cv.n

            def front_staged(h, c):
                if c < 16:
                    front_A(h, c)
                if 1 <= c <= 16:
                    front_B(h, c - 1)
                if c >= 2:
                    front_C(h, c - 2)

            BC_TOK = {}

            def front_tail(h):
                hh = h % 2
                if h == 0:
                    te.wait(PREP1["etp"][0], PREP1["etp"][1])
                te.wait(l_cv, l_cv.n)
                te.wait(l_fx, l_fx.n)
                tok = None
                for k in range(16):
                    tok = te.matmul(out=pA[0:96, 2, 0:TQ],
                                    lhsT=WxT[:, k * 96:(k + 1) * 96],
                                    rhs=xc[:, hh, k, 0:TQ],
                                    start=(k == 0), stop=(k == 15))
                tok.inc(s_q4, 1)
                ac.wait(s_q4)
                ac.activation(out=dtl[:, hh, :], in_=pA[0:64, 2, 0:TQ], func=AF.Copy)
                ac.dr()
                ac.activation(out=bcrows[:, :], in_=pA[64:96, 2, 0:TQ], func=AF.Copy).inc(l_fx, 1)
                sy.wait(l_fx)
                sy.dma_start(out=s_rows[:, h * TQ:(h + 1) * TQ], in_=bcrows[:, :]).inc(s_ev, 16)
                gp.wait(s_ev, s_ev.n)
                gp.wait(s_dB, s_dB.n)
                gp.wait(s_g, s_g.n)
                if h == 0:
                    gp.wait(PREP1["wout"][14][0], PREP1["wout"][14][1])
                    gp.wait(PREP1["wout"][15][0], PREP1["wout"][15][1])
                for n in range(16):
                    gp.dma_start(out=bcB[:, n, :],
                                 in_=s_rows[n:n + 1, h * TQ:(h + 1) * TQ].broadcast_to((128, TQ))).inc(lg, 16)
                    gp.dma_start(out=bcC[:, n, :],
                                 in_=s_rows[16 + n:17 + n, h * TQ:(h + 1) * TQ].broadcast_to((128, TQ))).inc(lg, 16)
                BC_TOK[h] = lg.n

            def front(h):
                for c in range(18):
                    front_staged(h, c)
                front_tail(h)

            DTOK = {}

            def emit_delta(h, c):
                hh = h % 2
                sl = c % 2
                te.wait(s_sz, s_sz.n)
                if c == 0:
                    te.wait(l_fx, l_fx.n)
                    if h == 0:
                        te.wait(PREP1["etp"][0], PREP1["etp"][1])
                    if h >= 1:
                        te.wait(l_ox, l_ox.n)
                tok = te.matmul(out=pA[:, sl, 0:TQ],
                                lhsT=WdtT[:, c * 128:(c + 1) * 128],
                                rhs=dtl[:, hh, 0:TQ],
                                start=True, stop=True)
                tok.inc(s_dmm, 1)
                DTOK[c] = s_dmm.n

            def scan(h):
                hh = h % 2
                emit_delta(h, 0)
                for c in range(16):
                    sl = c % 2
                    ac.wait(s_dmm, DTOK[c])
                    ac.wait(s_dl, s_dl.n)
                    ac.activation(out=txd[:, 512:512 + TQ], in_=pA[:, sl, 0:TQ], func=AF.Abs)
                    ac.dr()
                    ac.activation(out=txd[:, 512:512 + TQ], in_=txd[:, 512:512 + TQ], func=AF.Exp, scale=-1.0)
                    ac.dr()
                    ac.activation(out=txd[:, 512:512 + TQ], in_=txd[:, 512:512 + TQ], func=AF.Ln, bias=1.0)
                    ac.dr()
                    ac.wait(s_dx, s_dx.n)
                    ac.activation(out=delta[:, sl, 0:TQ], in_=pA[:, sl, 0:TQ],
                                  func=AF.Relu).inc(s_sp, 1)
                    ve.wait(s_sp)
                    if c == 0:
                        ve.wait(lg, BC_TOK[h])
                    ve.tensor_add(out=delta[:, sl, :], in0=delta[:, sl, :],
                                  in1=txd[:, 512:512 + TQ]).inc(s_dl, 1)
                    if wzz_last[sl] is not None:
                        sy.wait(wzz_last[sl][0], wzz_last[sl][1])
                    if h == 0:
                        sy.wait(PREP1["win"][16 + c][0], PREP1["win"][16 + c][1])
                        if c == 0:
                            sy.wait(PREP1["btp"][0], PREP1["btp"][1])
                    sy.dma_start(out=wzz[:, sl, :], in_=s_WinT[16 + c, :, :]).inc(s_wzz[sl], 16)
                    te.wait(s_wzz[sl])
                    te.wait(s_sp)
                    tok = None
                    for k in range(8):
                        tok = te.matmul(out=pA[:, sl, 0:TQ],
                                        lhsT=wzz[:, sl, k * 128:(k + 1) * 128],
                                        rhs=xn_fm[:, hh, k, 0:TQ],
                                        start=(k == 0), stop=(k == 7))
                    tok.inc(s_zm, 1)
                    wzz_last[sl] = (s_zm, s_zm.n)
                    if c < 15:
                        emit_delta(h, c + 1)
                    ve.dr()
                    ve.tensor_mul(out=dxb[:, :], in0=delta[:, sl, :],
                                  in1=xc[:, hh, c, :]).inc(s_dx, 1)
                    ve.dr()
                    ve.tensor_mul(out=dBxall[:, :, :],
                                  in0=dxb[:, :].rearrange("p (o t) -> p o t", o=1).to_broadcast((128, 16, TQ)),
                                  in1=bcB[:, :, :]).inc(s_dB, 1)
                    ve.dr()
                    for n in range(16):
                        nn = n % 2
                        gidx = c * 16 + n
                        ac.wait(s_dl)
                        if gidx >= 2:
                            ac.wait(s_sc, s_sc.n - 1)
                        ac.activation(out=dAb[:, nn, :], in_=delta[:, sl, :], func=AF.Exp,
                                      scale=A_fm[:, c, n:n + 1]).inc(s_dA, 1)
                        ve.wait(s_dA)
                        ve.tensor_tensor_scan(out=hall[:, n, :], data0=dAb[:, nn, :],
                                              data1=dBxall[:, n, :],
                                              initial=carry[:, gidx % 256:gidx % 256 + 1],
                                              op0=ALU.mult, op1=ALU.add).inc(s_sc, 1)
                    ac.wait(s_zm, s_zm.n)
                    ac.wait(s_yf, s_yf.n)
                    ac.activation(out=szb[:, c % 3, 0:TQ], in_=pA[:, sl, 0:TQ], func=AF.Silu,
                                  bias=bin_fm[:, 16 + c:17 + c]).inc(s_sz, 1)
                    EPI_SZ[c] = s_sz.n
                    ve.dr()
                    ve.tensor_copy(out=carry[:, c * 16:c * 16 + 16],
                                   in_=hall[:, :, TQ - 1:TQ])
                    ve.wait(s_ym, s_ym.n)
                    ve.tensor_mul(out=gall[:, :, :], in0=hall[:, :, :],
                                  in1=bcC[:, :, :]).inc(s_g, 1)
                    te.wait(s_g)
                    te.wait(s_yf, s_yf.n - 1)
                    if c == 0 and h >= 1:
                        te.wait(l_ox, l_ox.n)
                    tok = None
                    for n in range(16):
                        tok = te.matmul(out=pB[:, sl, 0:TQ],
                                        lhsT=eye[:, :],
                                        rhs=gall[:, n, :],
                                        start=(n == 0), stop=(n == 15))
                    tok.inc(s_ym, 1)
                    EPI_YM[c] = s_ym.n
                    if c >= 1:
                        epilogue_ve(h, c - 1)
                    if c == 15:
                        epilogue_ve(h, 15)
                    if h < 3:
                        front_staged(h + 1, c)

            def epilogue_ve(h, c):
                hh = h % 2
                sl = c % 2
                ve.wait(s_ym, EPI_YM[c])
                ve.wait(s_sz, EPI_SZ[c])
                ve.dr()
                ve.scalar_tensor_tensor(out=txd[:, 0:TQ], in0=xc[:, hh, c, :],
                                        scalar=Dp_fm[:, c:c + 1], in1=pB[:, sl, 0:TQ],
                                        op0=ALU.mult, op1=ALU.add)
                ve.dr()
                ve.wait(yfdp[sl], yfdp[sl].n)
                ve.tensor_mul(out=yfull[:, sl, :], in0=txd[:, 0:TQ],
                              in1=szb[:, c % 3, 0:TQ]).inc(s_yf, 1)
                sy.wait(s_yf)
                sy.dma_start(out=s_yfc[c, :, :], in_=yfull[:, sl, :]).inc(yfdp[sl], 16)

            def wout(h):
                ptile = [pA[:, 0:2, :], pA[:, 2:4, :], pB[:, 0:2, :], pB[:, 2:4, :]]
                ptile = [p.rearrange("p a b -> p (a b)") for p in ptile]
                te.wait(s_yf)
                te.wait(s_sz); te.wait(s_sp)
                te.wait(l_xin); te.wait(l_cv); te.wait(l_ev); te.wait(l_fx)
                sy.wait(l_yfd0); sy.wait(l_yfd1)
                for k in range(16):
                    ksl = k % 2
                    if wz_last[ksl] is not None:
                        sem, v = wz_last[ksl]
                        sy.wait(sem, v)
                    if h == 0:
                        sy.wait(PREP1["wout"][k][0], PREP1["wout"][k][1])
                    sy.dma_start(out=woutt[:, ksl, :], in_=s_WoutT[k, :, :]).inc(s_wzp[ksl], 16)
                    sy.dma_start(out=yfr[:, ksl, :], in_=s_yfc[k, :, :]).inc(s_wzp[ksl], 16)
                    te.wait(s_wzp[ksl])
                    tok = None
                    for tt in range(NTQ):
                        for t0 in range(2):
                            tok = te.matmul(out=ptile[tt][:, t0 * 512:(t0 + 1) * 512],
                                            lhsT=yfr[:, ksl, tt * 128:(tt + 1) * 128],
                                            rhs=woutt[:, ksl, t0 * 512:(t0 + 1) * 512],
                                            start=(k == 0), stop=(k == 15))
                    tok.inc(l_om, 1)
                    wz_last[ksl] = (l_om, l_om.n)
                for tt in range(NTQ):
                    sl = tt % 2
                    i = h * NTQ + tt
                    xr_load(sl, hx[i * 128:(i + 1) * 128, :])
                    ve.wait(l_om); ve.wait(lxrp[sl])
                    ve.tensor_scalar(out=txd[:, :], in0=xres[:, sl, :],
                                     scalar1=mu1[:, i:i + 1],
                                     scalar2=r1a[:, i:i + 1],
                                     op0=ALU.subtract, op1=ALU.mult)
                    ve.dr()
                    ve.tensor_add(out=txd[:, :], in0=txd[:, :], in1=xres[:, sl, :])
                    ve.dr()
                    ve.tensor_add(out=txd[:, :], in0=txd[:, :], in1=bo_bc[:, :])
                    ve.dr()
                    ve.tensor_add(out=xres[:, sl, :], in0=txd[:, :],
                                  in1=ptile[tt][:, 0:1024]).inc(l_ox, 1)
                    xr_used(sl, l_ox)
                    sy.wait(l_ox)
                    sy.dma_start(out=s_x[i * 128:(i + 1) * 128, :],
                                 in_=xres[:, sl, :]).inc(st, 16)

            def ln_q(h):
                tiles = list(range(h * NTQ, (h + 1) * NTQ))
                return emit_ln(hx, tiles, xn_fm[:, h % 2, :, :], "mamba", lnps_bf,
                               preloaded=(2 if h == 0 else 0))

            l_ev0 = ln_q(0)
            barrier(l_ev0)
            front(0)
            for h in range(4):
                if h < 3:
                    ln_q(h + 1)
                scan(h)
                if h < 3:
                    front_staged(h + 1, 16)
                    front_staged(h + 1, 17)
                    front_tail(h + 1)
                wout(h)
            barrier(st)

    def emit_ffn(tag, is_last, prep_toks=None, preloaded=0):
        THF = T // 4
        NTHF = NT // 4
        fch = [(s0, min(512, THF - s0)) for s0 in range(0, THF, 512)]
        with contextlib.ExitStack() as fs:
            fb = lambda n, s, d: fs.enter_context(nc.sbuf_tensor(n, list(s), d))
            xn_fm = fb("f_xn" + tag, [128, 2, 8, THF], BF16)
            hT = fb("f_h" + tag, [128, 32, THF], BF16)
            w2a = fb("f_w2" + tag, [128, 32, 1024], BF16)
            w1s = fb("f_w1" + tag, [128, 3, 1024], BF16)
            ltc = fb("f_ltc" + tag, [128, 8, 16], BF16)
            ev1 = fb("f_ev1" + tag, [1, 4096], BF16)
            ev4 = fb("f_ev4" + tag, [1, 4096], BF16)
            ufl = fb("f_ufl" + tag, [1, 4 * THF], BF16)
            vfl = fb("f_vfl" + tag, [1, 16 * THF], BF16)
            vsb = fb("f_vsb" + tag, [4, THF], BF16)
            usb = fb("f_usb" + tag, [4, THF], BF16)
            pA = fs.enter_context(nc.psum_tensor("f_pA" + tag, [128, 2, 1024], F32))
            pR = fs.enter_context(nc.psum_tensor("f_pR" + tag, [4, 1024], F32))
            pLN = fs.enter_context(nc.psum_tensor("f_pLN" + tag, [128, 1024], BF16))
            pRow = pLN[0:1, :]

            W1TOK = {}
            w1sem = (s_wzp[0], s_wzp[1], s_wzz[0])
            l_ev = emit_ln(s_x, list(range(NTHF)), xn_fm[:, 0, :, :], "plain", pLN[:, :],
                           preloaded=preloaded)
            lntok = {0: l_ev.n}
            ptoks = prep_toks

            def load_w2a():
                if ptoks is not None:
                    gp.wait(ptoks[2][0], ptoks[2][1])
                    gp.wait(ptoks[1][30][0], ptoks[1][30][1])
                    gp.wait(ptoks[1][31][0], ptoks[1][31][1])
                for k in range(32):
                    gp.dma_start(out=w2a[:, k, :], in_=s_W2T[k, :, :]).inc(l_w2, 16)
            for h in range(4):
                xh = xn_fm[:, h % 2, :, :]
                te.wait(l_ev, lntok[h])
                if h == 0:
                    ve.wait(l_ev, lntok[0])
                    ve.tensor_copy(out=ltc[:, :, :], in_=xh[:, :, 1:17]).inc(l_lt, 1)
                    te.wait(l_lt)
                    for dstrow, toks in ((ev1, (0, 2, 4, 6)), (ev4, (8, 10, 12, 14))):
                        for ii, tk in enumerate(toks):
                            te.wait(s_q1)
                            tok = None
                            for k in range(8):
                                tok = te.transpose(out=pRow[0:1, k * 128:(k + 1) * 128],
                                                   in_=ltc[:, k, tk:tk + 1],
                                                   identity=eye[:, :])
                            tok.inc(l_tr, 1)
                            ac.wait(l_tr)
                            ac.activation(out=dstrow[0:1, ii * 1024:(ii + 1) * 1024],
                                          in_=pRow[0:1, :], func=AF.Copy,
                                          scale=scale_sb[0:1, 0:1]).inc(s_q1, 1)
                if h == 0:
                    te.wait(s_q1)
                te.wait(s_q2, s_q2.n)
                tok = None
                for k in range(8):
                    for (t0, tw) in fch:
                        tok = te.matmul(out=pR[:, t0:t0 + tw],
                                        lhsT=ltc[:, k, 1:8:2],
                                        rhs=xh[:, k, t0:t0 + tw],
                                        start=(k == 0), stop=(k == 7))
                tok.inc(l_u, 1)
                ac.wait(l_u)
                if h >= 1:
                    ac.wait(lg, lg.n)
                ac.activation(out=usb[:, :], in_=pR[0:4, 0:THF], func=AF.Copy).inc(s_q1, 1)
                gp.wait(s_q1)
                for ii in range(4):
                    gp.dma_start(out=ufl[0:1, ii * THF:(ii + 1) * THF],
                                 in_=usb[ii:ii + 1, :]).inc(lg, 16)
                if h == 0:
                    load_w2a()
                for c in range(32):
                    sl = c % 2
                    sl3 = c % 3
                    if c >= 3:
                        sy.wait(s_q3, W1TOK[c - 3])
                    if ptoks is not None and h == 0:
                        sy.wait(ptoks[0][c][0], ptoks[0][c][1])
                    sy.dma_start(out=w1s[:, sl3, :], in_=s_W1T[c, :, :]).inc(w1sem[sl3], 16)
                    te.wait(w1sem[sl3])
                    if c >= 2:
                        te.wait(l_h, l_h.n - 1)
                    tok = None
                    for k in range(8):
                        for (t0, tw) in fch:
                            tok = te.matmul(out=pA[:, sl, t0:t0 + tw],
                                            lhsT=w1s[:, sl3, k * 128:(k + 1) * 128],
                                            rhs=xh[:, k, t0:t0 + tw],
                                            start=(k == 0), stop=False)
                    if c == 0:
                        te.wait(lg, lg.n)
                    ii = c // 8
                    for ti, (t0, tw) in enumerate(fch):
                        tok = te.matmul(out=pA[:, sl, t0:t0 + tw],
                                        lhsT=ev1[0:1, ii * 1024 + (c % 8) * 128:ii * 1024 + (c % 8) * 128 + 128],
                                        rhs=ufl[0:1, ii * THF + t0:ii * THF + t0 + tw],
                                        start=False, stop=True)
                    tok.inc(s_q3, 1)
                    W1TOK[c] = s_q3.n
                    ac.wait(s_q3)
                    ac.activation(out=hT[:, c, :], in_=pA[:, sl, 0:THF], func=AF.Gelu,
                                  bias=b1_fm[:, c:c + 1]).inc(l_h, 1)
                    if c % 8 == 7:
                        qi = c // 8
                        if qi == 0:
                            te.wait(s_q1, s_q1.n)
                        else:
                            te.wait(lg, lg.n)
                        te.wait(l_h, l_h.n)
                        tok = None
                        for k in range(8):
                            for (t0, tw) in fch:
                                tok = te.matmul(out=pR[:, t0:t0 + tw],
                                                lhsT=ltc[:, k, 9:16:2],
                                                rhs=hT[:, qi * 8 + k, t0:t0 + tw],
                                                start=(k == 0), stop=(k == 7))
                        tok.inc(l_v, 1)
                        ac.wait(l_v)
                        if qi >= 1:
                            ac.wait(lg, lg.n)
                        ac.activation(out=vsb[:, 0:THF], in_=pR[:, 0:THF],
                                      func=AF.Copy).inc(s_q4, 1)
                        gp.wait(s_q4)
                        for ii2 in range(4):
                            gp.dma_start(out=vfl[0:1, (qi * 4 + ii2) * THF:(qi * 4 + ii2 + 1) * THF],
                                         in_=vsb[ii2:ii2 + 1, 0:THF]).inc(lg, 16)
                if h < 3:
                    l_ev = emit_ln(s_x, list(range((h + 1) * NTHF, (h + 2) * NTHF)),
                                   xn_fm[:, (h + 1) % 2, :, :], "plain", pLN[:, :])
                    lntok[h + 1] = l_ev.n
                te.wait(l_h, l_h.n)
                te.wait(lg, lg.n)
                for tt in range(NTHF):
                    sl = tt % 2
                    i = h * NTHF + tt
                    xr_load(sl, s_x[i * 128:(i + 1) * 128, :])
                    te.wait(l_w2)
                    if tt >= 2:
                        te.wait(l_ox, l_ox.n - 1)
                    tok = None
                    for k in range(32):
                        for t0 in range(2):
                            tok = te.matmul(out=pA[:, sl, t0 * 512:(t0 + 1) * 512],
                                            lhsT=hT[:, k, tt * 128:(tt + 1) * 128],
                                            rhs=w2a[:, k, t0 * 512:(t0 + 1) * 512],
                                            start=(k == 0), stop=False)
                    for ii in range(4):
                        for qi in range(4):
                            tok = te.matmul(out=pA[:, sl, ii * 256:(ii + 1) * 256],
                                            lhsT=vfl[0:1, (qi * 4 + ii) * THF + tt * 128:(qi * 4 + ii) * THF + tt * 128 + 128],
                                            rhs=ev4[0:1, ii * 1024 + qi:ii * 1024 + qi + 1021:4],
                                            start=False, stop=(qi == 3 and ii % 2 == 1))
                    tok.inc(l_o, 1)
                    ve.wait(l_o); ve.wait(lxrp[sl])
                    ve.tensor_add(out=xres[:, sl, :], in0=xres[:, sl, :], in1=b2_bc[:, :])
                    ve.dr()
                    ve.tensor_add(out=xres[:, sl, :], in0=xres[:, sl, :],
                                  in1=pA[:, sl, 0:1024]).inc(l_ox, 1)
                    xr_used(sl, l_ox)
                    sy.wait(l_ox)
                    dst = h_out if is_last else s_x
                    sy.dma_start(out=dst[i * 128:(i + 1) * 128, :],
                                 in_=xres[:, sl, :]).inc(st, 16)
                if h == 3:
                    barrier(st)

    def emit_attn():
        THA = T // 4
        NTHA = NT // 4
        fch = [(s0, min(512, THA - s0)) for s0 in range(0, THA, 512)]
        with contextlib.ExitStack() as ats:
            ab = lambda n, s, d: ats.enter_context(nc.sbuf_tensor(n, list(s), d))
            xn_fm = ab("a_xn", [128, 8, THA], BF16)
            wqn = ab("a_wq", [128, 8, 1024], BF16)
            wkn = ab("a_wk", [128, 8, 1024], BF16)
            wvn = ab("a_wv", [128, 8, 1024], BF16)
            qf = ab("a_q", [128, 8, THA], BF16)
            kf = ab("a_k", [128, 8, THA], BF16)
            vt = ab("a_v", [128, NTHA, 1024], BF16)
            qh = ab("a_qh", [64, 8, THA], BF16)
            kh = ab("a_kh", [64, 8, THA], BF16)
            vh = ab("a_vh", [64, NTHA, 1024], BF16)
            pex = ab("a_pex", [128, 2, 1024], BF16)
            sub32 = ab("a_sub", [128, 1024], F32)
            ptb = ab("a_ptb", [64, 2, 2048], BF16)
            nm = ab("a_nm", [128, 2, 16], F32)
            lsum = ab("a_l", [128, 2, 16], F32)
            linv = ab("a_linv", [128, 2, 16], F32)
            pA = ats.enter_context(nc.psum_tensor("a_pA", [128, 2, 1024], F32))
            pT = ats.enter_context(nc.psum_tensor("a_pT", [64, 2048], BF16))
            pO = ats.enter_context(nc.psum_tensor("a_pO", [128, 1024], F32))
            lnps_bf = pA[:, 0, :].bitcast(BF16)[:, 0:1024]

            gp.dma_start(out=wqn[:, :, :], in_=h_Wq.rearrange("(k p) j -> p k j", p=128)).inc(lg, 16)
            gp.dma_start(out=wkn[:, :, :], in_=h_Wk.rearrange("(k p) j -> p k j", p=128)).inc(lg, 16)
            gp.dma_start(out=wvn[:, :, :], in_=h_Wv.rearrange("(k p) j -> p k j", p=128)).inc(lg, 16)

            for h in range(4):
                tiles = list(range(h * NTHA, (h + 1) * NTHA))
                l_ev = emit_ln(s_x, tiles, xn_fm, "plain", lnps_bf)
                barrier(l_ev)
                te.wait(lg)
                it = 0
                for jc in range(8):
                    for wsrc, dst, bias in ((wqn, qf, bq_fm), (wkn, kf, bk_fm)):
                        sl = it % 2
                        if it >= 2:
                            te.wait(l_qe, l_qe.n - 1)
                        tok = None
                        for k in range(8):
                            for (t0, tw) in fch:
                                tok = te.matmul(out=pA[:, sl, t0:t0 + tw],
                                                lhsT=wsrc[:, k, jc * 128:(jc + 1) * 128],
                                                rhs=xn_fm[:, k, t0:t0 + tw],
                                                start=(k == 0), stop=(k == 7))
                        tok.inc(l_qk, 1)
                        ac.wait(l_qk)
                        ac.activation(out=dst[:, jc, :], in_=pA[:, sl, 0:THA], func=AF.Identity,
                                      bias=bias[:, jc:jc + 1]).inc(l_qe, 1)
                        sy.wait(l_qe)
                        sy.dma_start(out=(qh if dst is qf else kh)[0:64, jc, :],
                                     in_=dst[64:128, jc, :]).inc(l_sm, 16)
                        it += 1
                for tt in range(NTHA):
                    sl = tt % 2
                    if tt < 2:
                        te.wait(l_qe)
                    else:
                        te.wait(l_vv, l_vv.n - 1)
                    tok = None
                    for k in range(8):
                        for t0 in range(2):
                            tok = te.matmul(out=pA[:, sl, t0 * 512:(t0 + 1) * 512],
                                            lhsT=xn_fm[:, k, tt * 128:(tt + 1) * 128],
                                            rhs=wvn[:, k, t0 * 512:(t0 + 1) * 512],
                                            start=(k == 0), stop=(k == 7))
                    tok.inc(l_vm, 1)
                    ve.wait(l_vm)
                    ve.tensor_add(out=vt[:, tt, :], in0=pA[:, sl, 0:1024],
                                  in1=bv_bc[:, :]).inc(l_vv, 1)
                    sy.wait(l_vv)
                    sy.dma_start(out=vh[0:64, tt, :], in_=vt[64:128, tt, :]).inc(l_sm, 16)
                if ATTN_STAGE < 2:
                    barrier(l_vv)
                    barrier(l_sm)
                    continue
                vv_q = l_vv.n
                sm_q = l_sm.n
                ex_tok = {}; pn_tok = {}; tt_tok = {}; pt_tok = {}
                av_tok = {}; ox_tok = {}; sc_tok = {}; sub_tok = {}

                def scores(tt):
                    sl = tt % 2
                    if tt < 2:
                        te.wait(l_vv, vv_q)
                        te.wait(l_sm, sm_q)
                        te.wait(l_qe, l_qe.n)
                    else:
                        te.wait(s_q5, sub_tok[tt - 2])
                    tok = None
                    for hd in range(16):
                        ch = hd // 2
                        qsrc = qf if hd % 2 == 0 else qh
                        ksrc = kf if hd % 2 == 0 else kh
                        for ib in range(2):
                            tok = te.matmul(out=pA[ib * 64:(ib + 1) * 64, sl, hd * 64:(hd + 1) * 64],
                                            lhsT=qsrc[0:64, ch, tt * 128 + ib * 64:tt * 128 + ib * 64 + 64],
                                            rhs=ksrc[0:64, ch, tt * 128 + ib * 64:tt * 128 + ib * 64 + 64],
                                            start=True, stop=True)
                    tok.inc(l_sc, 1)
                    sc_tok[tt] = l_sc.n

                def chain(tt):
                    sl = tt % 2
                    i = h * NTHA + tt
                    ve.wait(l_sc, sc_tok[tt])
                    ve.reduce_max(out=nm[:, sl, :],
                                  in_=pA[:, sl, :].rearrange("p (a j) -> p a j", a=16),
                                  axis=AX.X)
                    ve.dr()
                    ve.tensor_sub(out=sub32[:, :].rearrange("p (a j) -> p a j", a=16),
                                  in0=pA[:, sl, :].rearrange("p (a j) -> p a j", a=16),
                                  in1=nm[:, sl, :].rearrange("p (a o) -> p a o", o=1).to_broadcast((128, 16, 64))).inc(s_q5, 1)
                    sub_tok[tt] = s_q5.n
                    ac.wait(s_q5, sub_tok[tt])
                    if tt >= 2:
                        ac.wait(l_tt, tt_tok[tt - 2])
                    ac.activation(out=pex[:, sl, :], in_=sub32[:, :],
                                  func=AF.Exp, scale=0.125).inc(l_ex, 1)
                    ex_tok[tt] = l_ex.n
                    ve.wait(l_ex, ex_tok[tt])
                    ve.reduce_sum(out=lsum[:, sl, :],
                                  in_=pex[:, sl, :].rearrange("p (a j) -> p a j", a=16),
                                  axis=AX.X)
                    ve.dr()
                    ve.reciprocal(out=linv[:, sl, :], in_=lsum[:, sl, :])
                    ve.dr()
                    ve.tensor_mul(out=pex[:, sl, :].rearrange("p (a j) -> p a j", a=16),
                                  in0=linv[:, sl, :].rearrange("p (a o) -> p a o", o=1).to_broadcast((128, 16, 64)),
                                  in1=pex[:, sl, :].rearrange("p (a j) -> p a j", a=16)).inc(l_pn, 1)
                    pn_tok[tt] = l_pn.n
                    if ATTN_STAGE < 3:
                        return
                    te.wait(l_pn, pn_tok[tt])
                    if tt >= 1:
                        te.wait(l_pt, pt_tok[tt - 1])
                    tok = None
                    for ib in range(2):
                        for hd in range(16):
                            tok = te.transpose(out=pT[0:64, (ib * 16 + hd) * 64:(ib * 16 + hd) * 64 + 64],
                                               in_=pex[ib * 64:(ib + 1) * 64, sl, hd * 64:(hd + 1) * 64],
                                               identity=eye[ib * 64:(ib + 1) * 64, ib * 64:(ib + 1) * 64])
                    tok.inc(l_tt, 1)
                    tt_tok[tt] = l_tt.n
                    ac.wait(l_tt, tt_tok[tt])
                    if tt >= 2:
                        ac.wait(l_av, av_tok[tt - 2])
                    ac.activation(out=ptb[0:64, sl, :], in_=pT[0:64, :], func=AF.Copy).inc(l_pt, 1)
                    pt_tok[tt] = l_pt.n
                    if ATTN_STAGE < 4:
                        return
                    te.wait(l_pt, pt_tok[tt])
                    if tt >= 1:
                        te.wait(l_ox, ox_tok[tt - 1])
                    tok = None
                    for ib in range(2):
                        vsrc = vt if ib == 0 else vh
                        for hd in range(16):
                            tok = te.matmul(out=pO[ib * 64:(ib + 1) * 64, hd * 64:(hd + 1) * 64],
                                            lhsT=ptb[0:64, sl, (ib * 16 + hd) * 64:(ib * 16 + hd) * 64 + 64],
                                            rhs=vsrc[0:64, tt, hd * 64:(hd + 1) * 64],
                                            start=True, stop=True)
                    tok.inc(l_av, 1)
                    av_tok[tt] = l_av.n
                    xr_load(sl, s_x[i * 128:(i + 1) * 128, :])
                    ve.wait(l_av, av_tok[tt]); ve.wait(lxrp[sl])
                    ve.tensor_add(out=xres[:, sl, :], in0=xres[:, sl, :],
                                  in1=pO[:, 0:1024]).inc(l_ox, 1)
                    ox_tok[tt] = l_ox.n
                    xr_used(sl, l_ox)
                    sy.wait(l_ox, ox_tok[tt])
                    sy.dma_start(out=s_x[i * 128:(i + 1) * 128, :],
                                 in_=xres[:, sl, :]).inc(st, 16)

                for tt in range(NTHA):
                    scores(tt)
                    if tt >= 1:
                        chain(tt - 1)
                chain(NTHA - 1)
                barrier(st)

    emit_mamba()
    xr_load(0, s_x[0:128, :])
    xr_load(1, s_x[128:256, :])
    p2toks = emit_prep2()
    if nsub >= 2:
        emit_ffn("a", is_last=(nsub == 2), prep_toks=p2toks, preloaded=2)
    if nsub >= 3:
        emit_attn()
    if nsub >= 4:
        emit_ffn("b", is_last=True)
    if nsub < 2 or nsub == 3:
        cpb = b.sb("cpb", [128, 2, D], F32)
        for i in range(NT):
            sl = i % 2
            if i >= 2:
                sy.wait(st, st.n)
            sy.dma_start(out=cpb[:, sl, :], in_=s_x[i * 128:(i + 1) * 128, :]).inc(lxrp[sl], 16)
            sy.wait(lxrp[sl])
            sy.dma_start(out=h_out[i * 128:(i + 1) * 128, :], in_=cpb[:, sl, :]).inc(st, 16)
    sy.wait(st)

    b.finalize()
    es.close()
    return nc


def make_inmaps(inputs, T=2048):
    x = np.ascontiguousarray(np.asarray(inputs["L0_attn_a_x"], dtype=np.float32))
    nB = x.shape[0]
    eyec = np.eye(128).astype(ml_dtypes.bfloat16)
    shared = {
        "gamma": np.asarray(inputs["L0_attn_a_gamma"], np.float32),
        "beta": np.asarray(inputs["L0_attn_a_beta"], np.float32),
        "Win": np.asarray(inputs["L0_attn_a_W_in"], np.float32),
        "bin": np.asarray(inputs["L0_attn_a_b_in"], np.float32),
        "convw": np.asarray(inputs["L0_attn_a_conv_w"], np.float32),
        "convb": np.asarray(inputs["L0_attn_a_conv_b"], np.float32),
        "Wxm": np.asarray(inputs["L0_attn_a_W_x"], np.float32),
        "Wdt": np.asarray(inputs["L0_attn_a_W_dt"], np.float32),
        "bdt": np.asarray(inputs["L0_attn_a_b_dt"], np.float32),
        "Alog": np.asarray(inputs["L0_attn_a_A_log"], np.float32),
        "Dp": np.asarray(inputs["L0_attn_a_D_param"], np.float32),
        "Wout": np.asarray(inputs["L0_attn_a_W_out"], np.float32),
        "bout": np.asarray(inputs["L0_attn_a_b_out"], np.float32),
        "Wq": np.asarray(inputs["L0_attn_b_Wq"], np.float32),
        "bq": np.asarray(inputs["L0_attn_b_bq"], np.float32),
        "Wk": np.asarray(inputs["L0_attn_b_Wk"], np.float32),
        "bk": np.asarray(inputs["L0_attn_b_bk"], np.float32),
        "Wv": np.asarray(inputs["L0_attn_b_Wv"], np.float32),
        "bv": np.asarray(inputs["L0_attn_b_bv"], np.float32),
        "W1": np.asarray(inputs["L0_ffn_W1"], np.float32),
        "b1": np.asarray(inputs["L0_ffn_b1"], np.float32),
        "W2": np.asarray(inputs["L0_ffn_W2"], np.float32),
        "b2": np.asarray(inputs["L0_ffn_b2"], np.float32),
        "fscale": np.asarray(inputs["L0_ffn_scale"], np.float32).reshape(1, 1),
        "eyebf": eyec,
    }
    return [dict(shared, x=np.ascontiguousarray(x[bb, :T, :])) for bb in range(nB)]


def kernel(**inputs):
    T = 2048
    nc = build_program(T=T, nsub=4)
    in_maps = make_inmaps(inputs, T=T)
    res = run_bass_kernel_spmd(nc, in_maps, core_ids=list(range(len(in_maps))))
    return np.stack([np.asarray(r["out"], dtype=np.float32) for r in res.results], axis=0)
